# version 01
# baseline (speedup 1.0000x reference)
import contextlib
import numpy as np
import ml_dtypes

import concourse.bass as bass
import concourse.mybir as mybir
from concourse.bass_utils import run_bass_kernel_spmd

F32 = mybir.dt.float32
BF16 = mybir.dt.bfloat16
AF = mybir.ActivationFunctionType
ALU = mybir.AluOpType
AXX = mybir.AxisListType.X

BF = ml_dtypes.bfloat16

B = 8
T = 2048
D = 1024
NDT = D // 128
NTT = T // 128
NTC = T // 512

MAGIC = float(3 << 22)
N_CORES = 8


def _ins(ap, axis, step, count):
    nap = [list(d) for d in ap.ap]
    nap.insert(axis, [step, count])
    return bass.AP(ap.tensor, ap.offset, nap)


def bcast(ap, axis, count):
    return _ins(ap, axis, 0, count)


def cbc(col_ap, n):
    return bass.AP(col_ap.tensor, col_ap.offset, [list(col_ap.ap[0]), [0, n]])


def col2d(ap):
    return _ins(ap, 2, 1, 1)



class Prog:

    ENGS = ("t", "v", "s", "g", "y")

    def __init__(self, nc, es):
        self.nc = nc
        self.es = es
        self.pend = {e: [] for e in self.ENGS}
        self.sems = {}
        self.semv = {}
        for e in self.ENGS:
            self.new_sem("fin_" + e)
        self.new_sem("dma")
        self.new_sem("dmag")
        self.new_sem("dmaw")
        for i in range(8):
            self.new_sem(f"dms{i}")
        self.ninstr = 0

    def new_sem(self, name):
        self.sems[name] = self.es.enter_context(self.nc.semaphore(name))
        self.semv[name] = 0
        return name

    def mark(self, eng):
        return ("fin_" + eng, self.semv["fin_" + eng])

    def op(self, eng, emit, waits=(), incs=(), dma=False):
        wl = [(self.sems[w[0]], w[1]) for w in waits]
        il = []
        for sname in incs:
            n = 16 if dma else 1
            self.semv[sname] += n
            il.append((self.sems[sname], n))
        self.ninstr += 1

        def run(e, emit=emit, wl=wl, il=il):
            for h, v in wl:
                if v > 0:
                    e.wait_ge(h, v)
            inst = emit(e)
            for h, n in il:
                inst.then_inc(h, n)

        self.pend[eng].append(run)

    def barrier(self, producers=ENGS, consumers=ENGS):
        marks = []
        for e in producers:
            sname = "fin_" + e
            self.op(e, lambda en: en.drain(), incs=(sname,))
            marks.append((sname, self.semv[sname]))
        for e in consumers:
            self.op(e, lambda en: en.nop(), waits=marks)

    def flush(self, block):
        for e, attr in (("t", "tensor"), ("v", "vector"), ("s", "scalar"),
                        ("g", "gpsimd"), ("y", "sync")):
            ops = self.pend[e]
            if not ops:
                continue
            self.pend[e] = []

            def runall(eng, ops=ops):
                for f in ops:
                    f(eng)

            getattr(block, attr)(runall)


class Bank:

    def __init__(self, prog, ap):
        self.p = prog
        self.ap = ap
        self.wmark = None
        self.rmarks = []

    def mm(self, emits, extra_waits=()):
        p = self.p
        n = len(emits)
        for k, em in enumerate(emits):
            waits = tuple(self.rmarks) + tuple(extra_waits) if k == 0 else ()
            incs = ("fin_t",) if k == n - 1 else ()
            p.op("t", em, waits=waits, incs=incs)
        self.rmarks = []
        self.wmark = ("fin_t", p.semv["fin_t"])

    def read(self, eng, emit, extra_waits=(), incs_extra=()):
        p = self.p
        waits = ((self.wmark,) if self.wmark else ()) + tuple(extra_waits)
        sname = "fin_" + eng
        p.op(eng, emit, waits=waits, incs=(sname,) + tuple(incs_extra))
        self.rmarks.append((sname, p.semv[sname]))



def host_constants():
    c = {}
    c["c_ones_row"] = np.ones((1, 512), dtype=np.float32)
    c["c_ones_col"] = np.ones((128, 128), dtype=np.float32)
    c["c_ident"] = np.eye(128, dtype=np.float32)
    dk = 64
    inv_freq = 1.0 / (10000.0 ** (np.arange(0, dk, 2, dtype=np.float64) / dk))
    pos = np.arange(T, dtype=np.float64)
    fr = pos[:, None] * inv_freq[None, :]
    idx = np.arange(128) % 32
    c["c_rope_cos"] = np.ascontiguousarray(np.cos(fr).T[idx]).astype(np.float32)
    c["c_rope_sin"] = np.ascontiguousarray(np.sin(fr).T[idx]).astype(np.float32)
    tril = np.where(np.tril(np.ones((128, 128), dtype=bool)), 0.0, -1e9)
    c["c_trilm"] = tril.astype(np.float32)
    seln = np.zeros((48, 2048), dtype=np.float32)
    for n in range(16):
        seln[n, n * 128:(n + 1) * 128] = 1.0
        seln[32 + n, n * 128:(n + 1) * 128] = 1.0
    c["c_seln"] = seln
    return c



def build_nc(arrs, sublayers):
    nc = bass.Bass("TRN2", target_bir_lowering=False, debug=False)
    es = contextlib.ExitStack()
    p = Prog(nc, es)

    X = {}
    for name, arr in arrs.items():
        dt = F32 if arr.dtype == np.float32 else BF16
        X[name] = nc.dram_tensor(name, list(arr.shape), dt, kind="ExternalInput").ap()
    OUT = nc.dram_tensor("out", [D, T], F32, kind="ExternalOutput").ap()

    with es:
        build_body(nc, p, es, X, OUT, sublayers)
    return nc


def build_body(nc, p, es, X, OUT, sublayers):
    sb = lambda name, shape, dt=BF16: es.enter_context(nc.sbuf_tensor(name, shape, dt))

    ctx = dict(nc=nc, p=p, es=es, X=X, sb=sb)
    ctx["xres"] = sb("xres", [128, NDT, T], F32)
    ctx["bigA"] = sb("bigA", [128, NDT, T], BF16)
    ctx["bigB"] = sb("bigB", [128, NDT, T], BF16)
    ctx["bigC"] = sb("bigC", [128, NDT, T], BF16)
    ctx["xn"] = ctx["bigC"]
    ctx["statA"] = sb("statA", [128, 1024], F32)
    ctx["statB"] = sb("statB", [128, 1024], F32)
    ctx["wbuf"] = sb("wbuf", [128, NDT, 1024], BF16)
    ctx["small"] = sb("small", [128, 4096], BF16)
    ctx["aux"] = sb("aux", [128, 6144], BF16)
    ctx["lcol"] = sb("lcol", [128, 536], F32)
    ctx["smf"] = sb("smf", [128, 16], F32)
    ctx["ones_row"] = sb("ones_row", [1, 512])
    ctx["ones_col"] = sb("ones_col", [128, 128])
    ctx["ident"] = sb("ident", [128, 128])

    ctx["dbank"] = [Bank(p, es.enter_context(nc.psum_tensor(f"db{i}", [128, 1024], F32)))
                    for i in range(2)]
    ctx["bank"] = [Bank(p, es.enter_context(nc.psum_tensor(f"sb{i}", [128, 512], F32)))
                   for i in range(3)]
    ctx["tbank"] = Bank(p, es.enter_context(nc.psum_tensor("tb", [128, 1024], BF16)))

    cpool = sb("cpool", [128, 16], F32)
    ctx["cpool"] = cpool
    for i, vv in enumerate((1e-5, 1e-6, 1.0, 0.0)):
        p.op("v", lambda e, i=i, vv=vv: e.memset(cpool[:, i:i + 1], vv))
        nc.const_aps.register(F32, vv, cpool[:, i:i + 1])
    p.op("v", lambda e: e.drain(), incs=("fin_v",))

    for nm in ("ones_row", "ones_col", "ident"):
        dst = ctx[nm]
        p.op("g", lambda e, dst=dst, nm=nm: e.dma_start(dst[:], X["c_" + nm]),
             incs=("dmag",), dma=True)
    xf = X["x_f"].rearrange("(n q) t -> q n t", q=128)
    p.op("y", lambda e: e.dma_start(ctx["xres"][:], xf), incs=("dma",), dma=True)
    dmam = [("dma", p.semv["dma"]), ("dmag", p.semv["dmag"]), p.mark("v")]
    for e in ("t", "v", "s"):
        p.op(e, lambda en: en.nop(), waits=dmam)
    p.op("v", lambda en: en.nop(), waits=dmam)

    emitters = [emit_l0_attn_a, emit_l0_ffn, emit_l0_attn_b, emit_l0_ffn,
                emit_l1_attn_a, emit_l1_ffn, emit_l1_attn_b, emit_l1_ffn]
    for i in sublayers:
        emitters[i](ctx, i)
        p.barrier(("t", "v", "s", "g"), ("t", "v", "s", "g"))

    p.barrier(("v", "t", "s"), ("y",))
    ov = OUT.rearrange("(n q) t -> q n t", q=128)
    p.op("y", lambda e: e.dma_start(ov, ctx["xres"][:]), incs=("dma",), dma=True)
    p.op("y", lambda e: e.nop(), waits=[("dma", p.semv["dma"])])
    with nc.Block() as blk:
        p.flush(blk)



def emit_layernorm(ctx, eps=1e-5):
    p = ctx["p"]
    xres, xn = ctx["xres"], ctx["xn"]
    bigA = ctx["bigA"]
    statA, statB = ctx["statA"], ctx["statB"]
    ones_col = ctx["ones_col"]
    dbank = ctx["dbank"]

    for c in range(2):
        sl = slice(c * 1024, (c + 1) * 1024)
        for i in range(NDT):
            p.op("v", lambda e, i=i, sl=sl: e.tensor_copy(xn[:, i, sl], xres[:, i, sl]),
                 incs=("fin_v",) if i == NDT - 1 else ())
        cm = p.mark("v")
        for i in range(NDT):
            p.op("v", lambda e, i=i, sl=sl: e.tensor_tensor(
                bigA[:, i, sl], xn[:, i, sl], xn[:, i, sl], ALU.mult),
                waits=(cm,) if i == 0 else (),
                incs=("fin_v",) if i == NDT - 1 else ())
        vm = p.mark("v")
        for h, src in ((0, xn), (1, bigA)):
            bk = dbank[h]
            ems = []
            for s2 in range(2):
                for i in range(NDT):
                    ems.append(lambda e, i=i, s2=s2, src=src, c=c, bk=bk:
                               e.matmul(bk.ap[:, s2 * 512:(s2 + 1) * 512], ones_col[:],
                                        src[:, i, c * 1024 + s2 * 512:c * 1024 + (s2 + 1) * 512],
                                        start=(i == 0), stop=(i == NDT - 1)))
            bk.mm(ems, extra_waits=(vm,))
        dbank[0].read("v", lambda e: e.tensor_scalar_mul(statA[:], dbank[0].ap[:], 1.0 / D))
        p.op("v", lambda e: e.scalar_tensor_tensor(statB[:], statA[:], -1.0, statA[:],
                                                   ALU.mult, ALU.mult),
             waits=(p.mark("v"),), incs=("fin_v",))
        dbank[1].read("v", lambda e: e.scalar_tensor_tensor(
            statB[:], dbank[1].ap[:], 1.0 / D, statB[:], ALU.mult, ALU.add),
            extra_waits=(p.mark("v"),))
        vmk = p.mark("v")
        p.op("s", lambda e: e.activation(statB[:], statB[:], AF.Ln, bias=eps, scale=1.0),
             waits=(vmk,), incs=("fin_s",))
        p.op("s", lambda e: e.activation(statB[:], statB[:], AF.Exp, bias=0.0, scale=-0.5),
             waits=(p.mark("s"),), incs=("fin_s",))
        smk = p.mark("s")
        for i in range(NDT):
            p.op("v", lambda e, i=i, sl=sl: e.tensor_sub(xn[:, i, sl], xres[:, i, sl],
                                                         statA[:]),
                 waits=(smk,) if i == 0 else (),
                 incs=("fin_v",) if i == NDT - 1 else ())
        sm2 = p.mark("v")
        for i in range(NDT):
            p.op("v", lambda e, i=i, sl=sl: e.tensor_mul(xn[:, i, sl], xn[:, i, sl],
                                                         statB[:]),
                 waits=(sm2,) if i == 0 else (),
                 incs=("fin_v",) if i == NDT - 1 else ())
    return p.mark("v")


def emit_residual_add(ctx, src_bf):
    p = ctx["p"]
    xres = ctx["xres"]
    for i in range(NDT):
        p.op("v", lambda e, i=i: e.tensor_add(xres[:, i, :], xres[:, i, :], src_bf[:, i, :]),
             incs=("fin_v",) if i == NDT - 1 else ())
    return p.mark("v")



def emit_l1_attn_a(ctx, idx):
    nc, p, X = ctx["nc"], ctx["p"], ctx["X"]
    xn = ctx["xn"]
    k = ctx["bigA"]
    vtok = ctx["bigB"][:].rearrange("p a t -> p (a t)").rearrange("p (x y) -> p x y", x=NTT)
    qtile = ctx["aux"][:, 0:1024].rearrange("p (a b) -> p a b", a=8)
    brow = ctx["aux"][0:1, 1024:2048]
    wbuf = ctx["wbuf"]
    small, smf = ctx["small"], ctx["smf"]
    xres = ctx["xres"]
    dbank, bank, tbank = ctx["dbank"], ctx["bank"], ctx["tbank"]
    ones_row, ident = ctx["ones_row"], ctx["ident"]

    lnm = emit_layernorm(ctx)

    p.op("g", lambda e: e.dma_start(small[0:1, 0:3 * D], X["L1_attn_a_b_cat"]),
         incs=("dmaw",), dma=True)
    p.op("g", lambda e: e.dma_start(brow[0:1, 0:D],
                                    X["L1_attn_a_b_cat"][0:1, 0:D]),
         incs=("dmaw",), dma=True)
    bvrep = ctx["statA"]

    wv = {0: X["L1_attn_a_WqT"].rearrange("(n q) o -> q n o", q=128),
          1: X["L1_attn_a_WkT"].rearrange("(n q) o -> q n o", q=128),
          2: X["L1_attn_a_WvT"].rearrange("(n q) o -> q n o", q=128)}

    p.op("g", lambda e: e.dma_start(wbuf[:], wv[1]), incs=("dmaw",), dma=True)
    wm = [("dmaw", p.semv["dmaw"]), lnm]
    for ot in range(NDT):
        for c in range(2):
            bk = dbank[c]
            sl = slice(c * 1024, (c + 1) * 1024)
            ems = []
            for s2 in range(2):
                for i in range(NDT):
                    ems.append(lambda e, i=i, ot=ot, c=c, s2=s2, bk=bk:
                               e.matmul(bk.ap[:, s2 * 512:(s2 + 1) * 512],
                                        wbuf[:, i, ot * 128:(ot + 1) * 128],
                                        xn[:, i, c * 1024 + s2 * 512:c * 1024 + (s2 + 1) * 512],
                                        start=(i == 0), stop=False))
                ems.append(lambda e, ot=ot, s2=s2, bk=bk:
                           e.matmul(bk.ap[:, s2 * 512:(s2 + 1) * 512],
                                    small[0:1, D + ot * 128:D + (ot + 1) * 128],
                                    ones_row[0:1, 0:512],
                                    start=False, stop=True))
            bk.mm(ems, extra_waits=tuple(wm) if ot == 0 else ())
            bk.read("v", lambda e, ot=ot, sl=sl, bk=bk:
                    e.tensor_copy(k[:, ot, sl], bk.ap[:]))
    p.op("g", lambda e: e.dma_start(wbuf[:], wv[2]), waits=(p.mark("t"),),
         incs=("dmaw",), dma=True)
    wm = [("dmaw", p.semv["dmaw"])]
    dbank[0].mm([lambda e, s2=s2: e.matmul(dbank[0].ap[:, s2 * 512:(s2 + 1) * 512],
                                           ones_row[0:1, 0:128],
                                           small[0:1, 2 * D + s2 * 512:2 * D + (s2 + 1) * 512],
                                           start=True, stop=True)
                 for s2 in range(2)], extra_waits=tuple(wm))
    dbank[0].read("v", lambda e: e.tensor_copy(bvrep[:], dbank[0].ap[:]))
    bvm = p.mark("v")
    for tt in range(NTT):
        for c in range(2):
            bk = bank[c]
            sl = slice(c * 512, (c + 1) * 512)
            bk.mm([lambda e, i=i, tt=tt, sl=sl, bk=bk:
                   e.matmul(bk.ap[:], xn[:, i, tt * 128:(tt + 1) * 128],
                            wbuf[:, i, sl], start=(i == 0), stop=(i == NDT - 1))
                   for i in range(NDT)],
                  extra_waits=tuple(wm) if tt == 0 else ())
            bk.read("v", lambda e, tt=tt, sl=sl, bk=bk:
                    e.tensor_add(vtok[:, tt, sl], bk.ap[:], bvrep[:, sl]),
                    extra_waits=(bvm,))
    p.op("g", lambda e: e.dma_start(wbuf[:], wv[0]), waits=(p.mark("t"),),
         incs=("dmaw",), dma=True)
    wm = [("dmaw", p.semv["dmaw"])]
    p.op("t", lambda e: e.nop(), waits=wm)

    probs = small[:, 0:T]
    pt = small[:, T:2 * T]

    for tq in range(NTT):
        for ot in range(NDT):
            bk = bank[2]
            ems = [lambda e, i=i, ot=ot, tq=tq, bk=bk:
                   e.matmul(bk.ap[:, 0:128], wbuf[:, i, ot * 128:(ot + 1) * 128],
                            xn[:, i, tq * 128:(tq + 1) * 128], start=(i == 0), stop=False)
                   for i in range(NDT)]
            ems.append(lambda e, ot=ot, tq=tq, bk=bk:
                       e.matmul(bk.ap[:, 0:128], brow[0:1, ot * 128:(ot + 1) * 128],
                                ones_row[0:1, 0:128], start=False, stop=True))
            bk.mm(ems)
            bk.read("v", lambda e, ot=ot, bk=bk:
                    e.tensor_copy(qtile[:, ot, :], bk.ap[:, 0:128]))
        p.op("v", lambda e: e.drain(), incs=("fin_v",))
        qm = p.mark("v")
        for c in range(2):
            bk = dbank[c]
            ems = []
            for s2 in range(2):
                for i in range(NDT):
                    ems.append(lambda e, i=i, c=c, s2=s2, bk=bk:
                               e.matmul(bk.ap[:, s2 * 512:(s2 + 1) * 512], qtile[:, i, :],
                                        k[:, i, c * 1024 + s2 * 512:c * 1024 + (s2 + 1) * 512],
                                        start=(i == 0), stop=(i == NDT - 1)))
            bk.mm(ems, extra_waits=(qm,))
        dbank[0].read("v", lambda e: e.tensor_reduce(col2d(smf[:, 0:1]), dbank[0].ap[:],
                                                     axis=AXX, op=ALU.max))
        dbank[1].read("v", lambda e: e.tensor_reduce(col2d(smf[:, 1:2]), dbank[1].ap[:],
                                                     axis=AXX, op=ALU.max))
        p.op("v", lambda e: e.tensor_reduce(col2d(smf[:, 2:3]), smf[:, 0:2], axis=AXX,
                                            op=ALU.max), waits=(p.mark("v"),),
             incs=("fin_v",))
        p.op("v", lambda e: e.tensor_scalar_mul(smf[:, 3:4], smf[:, 2:3], -1.0),
             waits=(p.mark("v"),), incs=("fin_v",))
        vmk = p.mark("v")
        for c in range(2):
            dbank[c].read("s", lambda e, c=c: e.activation(
                probs[:, c * 1024:(c + 1) * 1024], dbank[c].ap[:], AF.Exp,
                bias=smf[:, 3:4], scale=1.0), extra_waits=(vmk,))
        smk = p.mark("s")
        p.op("v", lambda e: e.tensor_reduce(col2d(smf[:, 4:5]), probs[:], axis=AXX,
                                            op=ALU.add), waits=(smk,), incs=("fin_v",))
        p.op("v", lambda e: e.reciprocal(smf[:, 5:6], smf[:, 4:5]),
             waits=(p.mark("v"),), incs=("fin_v",))
        p.op("v", lambda e: e.tensor_scalar_mul(probs[:], probs[:], smf[:, 5:6]),
             waits=(p.mark("v"),), incs=("fin_v",))
        dmk = p.mark("v")
        for half in range(2):
            tbank.mm([lambda e, tk=tk: e.transpose(
                tbank.ap[:, (tk % 8) * 128:(tk % 8 + 1) * 128],
                probs[:, tk * 128:(tk + 1) * 128], ident[:])
                for tk in range(half * 8, (half + 1) * 8)], extra_waits=(dmk,))
            tbank.read("v", lambda e, half=half: e.tensor_copy(
                pt[:, half * 1024:(half + 1) * 1024], tbank.ap[:]))
        ptm = p.mark("v")
        for dv in range(NDT):
            bk = bank[2]
            bk.mm([lambda e, dv=dv, tk=tk, bk=bk:
                   e.matmul(bk.ap[:, 0:128], vtok[:, tk, dv * 128:(dv + 1) * 128],
                            pt[:, tk * 128:(tk + 1) * 128],
                            start=(tk == 0), stop=(tk == NTT - 1))
                   for tk in range(NTT)], extra_waits=(ptm,))
            bk.read("v", lambda e, dv=dv, tq=tq, bk=bk:
                    e.tensor_add(xres[:, dv, tq * 128:(tq + 1) * 128],
                                 xres[:, dv, tq * 128:(tq + 1) * 128], bk.ap[:, 0:128]))
    p.op("v", lambda e: e.drain(), incs=("fin_v",))
    return p.mark("v")



def emit_l0_attn_a(ctx, idx):
    nc, p, X = ctx["nc"], ctx["p"], ctx["X"]
    xn = ctx["xn"]
    xnn = ctx["bigA"]
    dlt = ctx["bigB"]
    wbuf, small, smf = ctx["wbuf"], ctx["small"], ctx["smf"]
    lcol = ctx["lcol"]
    xres = ctx["xres"]
    statA, statB = ctx["statA"], ctx["statB"]
    ones_col, ident = ctx["ones_col"], ctx["ident"]
    bank, dbank, tbank = ctx["bank"], ctx["dbank"], ctx["tbank"]
    wflat = wbuf[:].rearrange("p a t -> p (a t)")

    lnm = emit_layernorm(ctx)

    p.op("y", lambda e: e.dma_start(lcol[:, 0:512], X["L0aa_A_pp"]),
         waits=(lnm,), incs=("dma",), dma=True)
    p.op("y", lambda e: e.dma_start(lcol[:, 512:536], X["L0aa_cols"]),
         incs=("dma",), dma=True)
    dm = [("dma", p.semv["dma"])]
    p.op("s", lambda e: e.activation(lcol[:, 0:512], lcol[:, 0:512], AF.Exp,
                                     bias=0.0, scale=1.0), waits=dm, incs=("fin_s",))
    p.op("v", lambda e: e.tensor_scalar_mul(lcol[:, 0:512], lcol[:, 0:512], -1.0),
         waits=(p.mark("s"),), incs=("fin_v",))
    acm = p.mark("v")

    for c in range(2):
        sl = slice(c * 1024, (c + 1) * 1024)
        for i in range(NDT):
            p.op("v", lambda e, i=i, sl=sl: e.tensor_tensor(
                xnn[:, i, sl], xn[:, i, sl], xn[:, i, sl], ALU.mult),
                waits=(lnm, acm) if i == 0 else (),
                incs=("fin_v",) if i == NDT - 1 else ())
        vm = p.mark("v")
        bkr = dbank[0]
        ems = []
        for s2 in range(2):
            for i in range(NDT):
                ems.append(lambda e, i=i, s2=s2, c=c, bkr=bkr:
                           e.matmul(bkr.ap[:, s2 * 512:(s2 + 1) * 512], ones_col[:],
                                    xnn[:, i, c * 1024 + s2 * 512:c * 1024 + (s2 + 1) * 512],
                                    start=(i == 0), stop=(i == NDT - 1)))
        bkr.mm(ems, extra_waits=(vm,))
        bkr.read("v", lambda e, bkr=bkr: e.tensor_scalar_mul(statB[:], bkr.ap[:],
                                                             1.0 / D),
                 extra_waits=(p.mark("v"),))
        p.op("v", lambda e: e.drain(), incs=("fin_v",))
        vmk = p.mark("v")
        p.op("s", lambda e: e.activation(statB[:], statB[:], AF.Ln, bias=1e-6,
                                         scale=1.0), waits=(vmk,), incs=("fin_s",))
        p.op("s", lambda e: e.activation(statB[:], statB[:], AF.Exp, bias=0.0,
                                         scale=-0.5), waits=(p.mark("s"),),
             incs=("fin_s",))
        smk = p.mark("s")
        for i in range(NDT):
            p.op("v", lambda e, i=i, sl=sl: e.scalar_tensor_tensor(
                xnn[:, i, sl], xn[:, i, sl], lcol[:, 512 + i:513 + i], statB[:],
                ALU.mult, ALU.mult),
                waits=(smk,) if i == 0 else (), incs=("fin_v",))
            p.op("v", lambda e, i=i, sl=sl: e.tensor_scalar_mul(
                xn[:, i, sl], xn[:, i, sl], lcol[:, 528 + i:529 + i]),
                waits=(p.mark("v"),), incs=("fin_v",))
    p.op("v", lambda e: e.drain(), incs=("fin_v",))
    xdm = p.mark("v")

    for mt in range(NDT):
        slot = mt % 2
        p.op("g", lambda e, mt=mt, slot=slot: e.dma_start(
            wflat[:, slot * 1024:slot * 1024 + 1024].rearrange("p (a b) -> p a b", a=8),
            X["L0aa_wo"][mt].rearrange("a p o -> p a o")),
            waits=(p.mark("t"),), incs=(f"dms{slot}",), dma=True)
        wm = [(f"dms{slot}", p.semv[f"dms{slot}"])]
        for c in range(NTC):
            ts = slice(c * 512, (c + 1) * 512)
            bk = bank[c % 2]
            bk.mm([lambda e, i=i, mt=mt, ts=ts, bk=bk, slot=slot:
                   e.matmul(bk.ap[:], wflat[:, slot * 1024 + i * 128:slot * 1024 + (i + 1) * 128],
                            xn[:, i, ts], start=(i == 0), stop=(i == NDT - 1))
                   for i in range(NDT)],
                  extra_waits=tuple(wm) + ((xdm,) if c == 0 else ()))
            bk.read("v", lambda e, mt=mt, ts=ts, bk=bk: e.tensor_add(
                xres[:, mt, ts], xres[:, mt, ts], bk.ap[:]))
    p.op("v", lambda e: e.drain(), incs=("fin_v",))

    T1 = small[:, 0:T]
    T2 = small[:, 2048:2048 + T]
    p.op("g", lambda e: e.dma_start(
        wflat[:, 2048:4096].rearrange("p (a b) -> p a b", a=8),
        X["L0aa_wxpT"].rearrange("(n q) o -> q n o", q=128)),
        waits=(p.mark("t"),), incs=("dms2",), dma=True)
    wm = [("dms2", p.semv["dms2"])]
    for mt2, dst in ((0, T1), (1, T2)):
        for c in range(NTC):
            ts = slice(c * 512, (c + 1) * 512)
            bk = bank[2]
            bk.mm([lambda e, i=i, mt2=mt2, ts=ts, bk=bk:
                   e.matmul(bk.ap[:], wflat[:, 2048 + i * 256 + mt2 * 128:2048 + i * 256 + (mt2 + 1) * 128],
                            xnn[:, i, ts], start=(i == 0), stop=(i == NDT - 1))
                   for i in range(NDT)],
                  extra_waits=tuple(wm) if mt2 == 0 and c == 0 else ())
            bk.read("v", lambda e, dst=dst, ts=ts, bk=bk: e.tensor_copy(
                dst[:, ts], bk.ap[:]), extra_waits=(p.mark("v"),))
    p.op("v", lambda e: e.drain(), incs=("fin_v",))
    prm = p.mark("v")

    p.op("g", lambda e: e.dma_start(wflat[64:128, 0:1024], X["L0aa_wdtT"]),
         waits=(p.mark("t"),), incs=("dms3",), dma=True)
    wm = [("dms3", p.semv["dms3"])]
    for mt in range(NDT):
        for c2 in range(2):
            ts = slice(c2 * 1024, (c2 + 1) * 1024)
            bkd = dbank[1]
            bkd.mm([lambda e, mt=mt, ts=ts, s2=s2, bkd=bkd: e.matmul(
                bkd.ap[:, s2 * 512:(s2 + 1) * 512],
                wflat[64:128, mt * 128:(mt + 1) * 128],
                T1[64:128, ts.start + s2 * 512:ts.start + (s2 + 1) * 512],
                start=True, stop=True)
                for s2 in range(2)],
                extra_waits=tuple(wm) + ((prm,) if mt == 0 and c2 == 0 else ()))
            bkd.read("v", lambda e, mt=mt, bkd=bkd: e.tensor_scalar_add(
                statA[:], bkd.ap[:], lcol[:, 520 + mt:521 + mt]),
                extra_waits=(p.mark("v"),))
            p.op("v", lambda e: e.tensor_scalar_mul(statB[:], statA[:], -1.0),
                 waits=(p.mark("v"),), incs=("fin_v",))
            p.op("v", lambda e: e.tensor_max(statB[:], statB[:], statA[:]),
                 waits=(p.mark("v"),), incs=("fin_v",))
            vsp = p.mark("v")
            p.op("s", lambda e: e.activation(statB[:], statB[:], AF.Exp, bias=0.0,
                                             scale=-1.0), waits=(vsp,), incs=("fin_s",))
            p.op("s", lambda e: e.activation(statB[:], statB[:], AF.Ln, bias=1.0,
                                             scale=1.0), waits=(p.mark("s"),),
                 incs=("fin_s",))
            ssp = p.mark("s")
            p.op("v", lambda e, mt=mt, ts=ts: e.scalar_tensor_tensor(
                dlt[:, mt, ts], statA[:], 0.0, statB[:], ALU.max, ALU.add),
                 waits=(ssp,), incs=("fin_v",))
    p.op("v", lambda e: e.drain(), incs=("fin_v",))
    dlm = p.mark("v")

    seln0 = ctx["aux"][:, 0:128]
    p.op("g", lambda e: e.dma_start(seln0, X["c_seln0"]), incs=("dmag",), dma=True)
    sbm = [("dmag", p.semv["dmag"])]
    cflat = ctx["xn"][:].rearrange("p a t -> p (a t)")
    brep = cflat[:, 0:2048]
    crep = cflat[:, 2048:4096]
    ucol = cflat[:, 4096:6144]
    dac = cflat[:, 6144:8192]
    dbuc = cflat[:, 8192:10240]
    hbuf = cflat[:, 10240:12288]
    hcbuf = cflat[:, 12288:14336]
    selyb = cflat[:, 14336:14464]
    for half, (rep, src) in enumerate(((brep, T1), (crep, T2))):
        for c in range(NTC):
            ts = slice(c * 512, (c + 1) * 512)
            bk = bank[2]
            bk.mm([lambda e, src=src, ts=ts, bk=bk: e.matmul(
                bk.ap[:], seln0, src[:, ts], start=True, stop=True)],
                extra_waits=tuple(sbm) + (dlm,) if half == 0 and c == 0 else ())
            bk.read("v", lambda e, rep=rep, ts=ts, bk=bk: e.tensor_copy(
                rep[:, ts], bk.ap[:]), extra_waits=(p.mark("v"),))
    p.op("v", lambda e: e.drain(), incs=("fin_v",))
    bcm = p.mark("v")

    p.op("g", lambda e: e.dma_start(wflat[:, 0:8192], X["c_seld"]),
         waits=(p.mark("t"),), incs=("dms4",), dma=True)
    sdm = [("dms4", p.semv["dms4"])]

    for dtile in range(NDT):
        p.op("v", lambda e, dtile=dtile: e.tensor_tensor(
            ucol, dlt[:, dtile, :], xnn[:, dtile, :], ALU.mult),
            waits=(bcm, dlm, p.mark("t"), p.mark("v")), incs=("fin_v",))
        um = p.mark("v")
        ybk0, ybk1 = dbank[0], dbank[1]
        for m in range(64):
            k = dtile * 64 + m
            tbank.mm([lambda e, m=m: e.transpose(
                tbank.ap[:, 0:128], wflat[:, m * 128:(m + 1) * 128], ident[:])],
                extra_waits=tuple(sdm) + (p.mark("v"),))
            tbank.read("v", lambda e: e.tensor_copy(selyb, tbank.ap[:, 0:128]),
                       extra_waits=(p.mark("v"),))
            selym = p.mark("v")
            for c in range(NTC):
                ts = slice(c * 512, (c + 1) * 512)
                bka, bku = bank[0], bank[1]
                bka.mm([lambda e, m=m, dtile=dtile, ts=ts, bka=bka: e.matmul(
                    bka.ap[:], wflat[:, m * 128:(m + 1) * 128],
                    dlt[:, dtile, ts], start=True, stop=True)],
                    extra_waits=(um,) if m == 0 and c == 0 else ())
                bka.read("s", lambda e, k=k, ts=ts, bka=bka: e.activation(
                    dac[:, ts], bka.ap[:], AF.Exp, bias=0.0,
                    scale=lcol[:, k:k + 1]), extra_waits=(p.mark("v"),))
                bku.mm([lambda e, m=m, ts=ts, bku=bku: e.matmul(
                    bku.ap[:], wflat[:, m * 128:(m + 1) * 128],
                    ucol[:, ts], start=True, stop=True)],
                    extra_waits=(um,) if m == 0 and c == 0 else ())
                bku.read("v", lambda e, ts=ts, bku=bku: e.tensor_tensor(
                    dbuc[:, ts], bku.ap[:], brep[:, ts], ALU.mult),
                    extra_waits=(p.mark("v"),))
            expm = p.mark("s")
            p.op("v", lambda e: e.memset(_ins(dac, 2, 64, 1)[:, 0:2048:64][:, :, 0]
                                         if False else
                                         bass.AP(dac.tensor, dac.offset,
                                                 [list(dac.ap[0]), [64, 32]]), 0.0),
                 waits=(expm, p.mark("v")), incs=("fin_v",))
            zm = p.mark("v")
            with nc.allow_low_precision(reason="scan state is fp32 internal"):
                p.op("v", lambda e: e.tensor_tensor_scan(
                    hbuf, dac, dbuc, 0.0, ALU.mult, ALU.add),
                    waits=(zm, p.mark("t")), incs=("fin_v",))
            p.op("v", lambda e: e.tensor_tensor(hcbuf, hbuf, crep, ALU.mult),
                 waits=(p.mark("v"),), incs=("fin_v",))
            hcm = p.mark("v")
            for c in range(NTC):
                ts = slice(c * 512, (c + 1) * 512)
                ybk = ybk0 if c < 2 else ybk1
                hf = (c % 2) * 512
                ybk.mm([lambda e, ts=ts, hf=hf, m=m, ybk=ybk: e.matmul(
                    ybk.ap[:, hf:hf + 512], selyb, hcbuf[:, ts],
                    start=(m == 0), stop=(m == 63))],
                    extra_waits=(hcm, selym))
        for c in range(NTC):
            ts = slice(c * 512, (c + 1) * 512)
            ybk = ybk0 if c < 2 else ybk1
            hf = (c % 2) * 512
            ybk.read("v", lambda e, dtile=dtile, ts=ts, hf=hf, ybk=ybk:
                     e.tensor_copy(dlt[:, dtile, ts], ybk.ap[:, hf:hf + 512]),
                     extra_waits=(p.mark("v"),))
    p.op("v", lambda e: e.drain(), incs=("fin_v",))
    yfm = p.mark("v")

    for mt in range(NDT):
        slot = mt % 2
        p.op("g", lambda e, mt=mt, slot=slot: e.dma_start(
            wflat[:, slot * 1024:slot * 1024 + 1024].rearrange("p (a b) -> p a b", a=8),
            X["L0aa_wo"][mt].rearrange("a p o -> p a o")),
            waits=(p.mark("t"),), incs=(f"dms{slot}",), dma=True)
        wm = [(f"dms{slot}", p.semv[f"dms{slot}"])]
        for c in range(NTC):
            ts = slice(c * 512, (c + 1) * 512)
            bk = bank[c % 2]
            bk.mm([lambda e, i=i, mt=mt, ts=ts, bk=bk, slot=slot:
                   e.matmul(bk.ap[:], wflat[:, slot * 1024 + i * 128:slot * 1024 + (i + 1) * 128],
                            dlt[:, i, ts], start=(i == 0), stop=(i == NDT - 1))
                   for i in range(NDT)],
                  extra_waits=tuple(wm) + ((yfm,) if c == 0 else ())),
            bk.read("v", lambda e, mt=mt, ts=ts, bk=bk: e.tensor_add(
                xres[:, mt, ts], xres[:, mt, ts], bk.ap[:]))
    p.op("v", lambda e: e.drain(), incs=("fin_v",))
    return p.mark("v")


def emit_l0_ffn(ctx, idx):
    nc, p, X = ctx["nc"], ctx["p"], ctx["X"]
    xq = ctx["xn"]
    ga, ub = ctx["bigA"], ctx["bigB"]
    gav = ga[:].rearrange("p a t -> p (a t)").rearrange("p (x y) -> p x y", x=32)
    ubv = ub[:].rearrange("p a t -> p (a t)").rearrange("p (x y) -> p x y", x=32)
    small, smf = ctx["small"], ctx["smf"]
    statA, statB = ctx["statA"], ctx["statB"]
    cpool = ctx["cpool"]
    wbuf = ctx["wbuf"]
    wbv = wbuf[:].rearrange("p a t -> p (a t)").rearrange("p (x y z) -> p x y z", y=8, z=128)
    xres = ctx["xres"]
    ones_col = ctx["ones_col"]
    bank, dbank = ctx["bank"], ctx["dbank"]

    if "wq_g" not in ctx:
        ctx["wq_g"] = nc.dram_tensor("wq_g", [8, 32, 128, 128], BF16).ap()
        ctx["wq_u"] = nc.dram_tensor("wq_u", [8, 32, 128, 128], BF16).ap()
        ctx["wq_d"] = nc.dram_tensor("wq_d", [32, 8, 128, 128], BF16).ap()

    if not ctx.get("l0ffn_quantized"):
        ctx["l0ffn_quantized"] = True
        for wi, (nm, dst, I, O, mwslot) in enumerate((
                ("L0_ffn_w_gate_T", ctx["wq_g"], 1024, 4096, 4),
                ("L0_ffn_w_up_T", ctx["wq_u"], 1024, 4096, 5),
                ("L0_ffn_w_down_T", ctx["wq_d"], 4096, 1024, 6))):
            wv = X[nm].rearrange("(a q) o -> q a o", q=128)
            nit, noc = I // 128, O // 512
            p.op("v", lambda e: e.memset(smf[:, 0:1], 0.0), incs=("fin_v",))
            for it in range(nit):
                for oc in range(noc):
                    half = (it * noc + oc) % 2
                    hs = slice(half * 512, (half + 1) * 512)
                    p.op("y", lambda e, it=it, oc=oc, wv=wv, hs=hs:
                         e.dma_start(statA[:, hs], wv[:, it, oc * 512:(oc + 1) * 512]),
                         waits=(p.mark("v"),), incs=("dma",), dma=True)
                    p.op("v", lambda e, hs=hs: e.tensor_reduce(
                        col2d(smf[:, 1:2]), statA[:, hs], axis=AXX, op=ALU.add,
                        apply_absolute_value=True),
                        waits=[("dma", p.semv["dma"])], incs=("fin_v",))
                    p.op("v", lambda e: e.tensor_add(smf[:, 0:1], smf[:, 0:1],
                                                     smf[:, 1:2]),
                         waits=(p.mark("v"),), incs=("fin_v",))
            p.op("v", lambda e: e.tensor_copy(ctx["aux"][:, 144:145], smf[:, 0:1]),
                 waits=(p.mark("v"),), incs=("fin_v",))
            vm = p.mark("v")
            bk = bank[2]
            bk.mm([lambda e, bk=bk: e.matmul(bk.ap[:, 0:1], ones_col[:],
                                             ctx["aux"][:, 144:145], start=True, stop=True)],
                  extra_waits=(vm,))
            bk.read("v", lambda e, I=I, O=O, mwslot=mwslot: e.tensor_scalar(
                cpool[:, mwslot:mwslot + 1], bk.ap[:, 0:1], 1.0 / (I * O), 1e-5,
                ALU.mult, ALU.max))
            p.op("v", lambda e, mwslot=mwslot: e.reciprocal(smf[:, 2:3],
                                                            cpool[:, mwslot:mwslot + 1]),
                 waits=(p.mark("v"),), incs=("fin_v",))
            for it in range(nit):
                for oc in range(noc):
                    half = (it * noc + oc) % 2
                    hs = slice(half * 512, (half + 1) * 512)
                    p.op("y", lambda e, it=it, oc=oc, wv=wv, hs=hs:
                         e.dma_start(statA[:, hs], wv[:, it, oc * 512:(oc + 1) * 512]),
                         waits=(p.mark("v"),), incs=("dma",), dma=True)
                    p.op("v", lambda e, hs=hs: e.tensor_scalar(
                        statB[:, hs], statA[:, hs], smf[:, 2:3], MAGIC,
                        ALU.mult, ALU.add),
                        waits=[("dma", p.semv["dma"])], incs=("fin_v",))
                    p.op("v", lambda e, hs=hs: e.tensor_scalar(
                        statB[:, hs], statB[:, hs], MAGIC, -1.0,
                        ALU.subtract, ALU.max), waits=(p.mark("v"),), incs=("fin_v",))
                    p.op("v", lambda e, hs=hs, half=half: e.tensor_scalar_min(
                        small[:, half * 512:(half + 1) * 512],
                        statB[:, hs], 1.0),
                        waits=(p.mark("v"),
                               (f"dms{6 + half}", p.semv[f"dms{6 + half}"])),
                        incs=("fin_v",))
                    vmk = p.mark("v")
                    sq = f"dms{6 + half}"
                    p.op("g", lambda e, it=it, oc=oc, dst=dst, half=half:
                         e.dma_start(
                             _wq_out_ap(dst, it, oc),
                             small[:, half * 512:(half + 1) * 512]
                             .rearrange("p (a b) -> p a b", a=4)),
                         waits=(vmk,), incs=(sq,), dma=True)
            p.op("v", lambda e: e.drain(),
                 waits=[("dms6", p.semv["dms6"]), ("dms7", p.semv["dms7"])],
                 incs=("fin_v",))

    lnm = emit_layernorm(ctx)

    mrep = small[:, 0:T]
    mtmp = small[:, 2048:4096]
    p.op("v", lambda e: e.tensor_tensor(mtmp, xq[:, 0, :], xq[:, 0, :], ALU.mult),
         waits=(p.mark("v"),), incs=("fin_v",))
    for i in range(1, NDT):
        p.op("v", lambda e, i=i: e.tensor_tensor(mrep, xq[:, i, :], xq[:, i, :],
                                                 ALU.mult),
             waits=(p.mark("v"),), incs=("fin_v",))
        p.op("v", lambda e: e.tensor_max(mtmp, mtmp, mrep),
             waits=(p.mark("v"),), incs=("fin_v",))
    gm = emit_part_max_rep(ctx, mtmp, T, mrep, sqrt_of_sq=True)
    for c in range(2):
        sl = slice(c * 1024, (c + 1) * 1024)
        p.op("v", lambda e, sl=sl: e.tensor_scalar_max(statA[:], mrep[:, sl], 1e-5),
             waits=(gm,) if c == 0 else (p.mark("v"),), incs=("fin_v",))
        p.op("v", lambda e: e.reciprocal(statB[:], statA[:]),
             waits=(p.mark("v"),), incs=("fin_v",))
        bm = p.mark("v")
        for i in range(NDT):
            p.op("v", lambda e, i=i, sl=sl: e.tensor_tensor(statA[:], xq[:, i, sl],
                                                            statB[:], ALU.mult),
                 waits=(bm,) if i == 0 else (p.mark("v"),), incs=("fin_v",))
            p.op("v", lambda e: e.tensor_scalar(statA[:], statA[:], 127.0, MAGIC,
                                                ALU.mult, ALU.add),
                 waits=(p.mark("v"),), incs=("fin_v",))
            p.op("v", lambda e: e.tensor_scalar(statA[:], statA[:], MAGIC, -128.0,
                                                ALU.subtract, ALU.max),
                 waits=(p.mark("v"),), incs=("fin_v",))
            p.op("v", lambda e, i=i, sl=sl: e.tensor_scalar_min(xq[:, i, sl],
                                                                statA[:], 127.0),
                 waits=(p.mark("v"),), incs=("fin_v",))
    qm = p.mark("v")

    p.op("v", lambda e: e.tensor_tensor(cpool[:, 7:8], cpool[:, 5:6], cpool[:, 6:7],
                                        ALU.mult), waits=(p.mark("v"),), incs=("fin_v",))
    p.op("v", lambda e: e.tensor_scalar_mul(cpool[:, 7:8], cpool[:, 7:8],
                                            1.0 / (127.0 * 127.0)),
         waits=(p.mark("v"),), incs=("fin_v",))
    p.op("v", lambda e: e.tensor_scalar_mul(smf[:, 3:4], cpool[:, 4:5], 1.0 / 127.0),
         waits=(p.mark("v"),), incs=("fin_v",))
    scm = p.mark("v")

    mh = small[:, 2048:2560]
    for c in range(4):
        ts = slice(c * 512, (c + 1) * 512)
        for ot in range(32):
            slot_g = (ot % 2) * 2
            slot_u = slot_g + 1
            sg, su = f"dms{slot_g}", f"dms{slot_u}"
            p.op("g", lambda e, ot=ot, slot_g=slot_g:
                 e.dma_start(wbv[:, slot_g, :, :], ctx["wq_g"][:, ot, :, :]
                             .rearrange("a p o -> p a o")),
                 waits=(p.mark("t"),) if ot >= 2 else (qm,), incs=(sg,), dma=True)
            p.op("g", lambda e, ot=ot, slot_u=slot_u:
                 e.dma_start(wbv[:, slot_u, :, :], ctx["wq_u"][:, ot, :, :]
                             .rearrange("a p o -> p a o")),
                 incs=(su,), dma=True)
            wmk = [(sg, p.semv[sg]), (su, p.semv[su])]
            for h, slot, bidx in ((0, slot_g, 0), (1, slot_u, 1)):
                bk = bank[bidx]
                bk.mm([lambda e, i=i, slot=slot, ts=ts, bk=bk:
                       e.matmul(bk.ap[:], wbv[:, slot, i, :], xq[:, i, ts],
                                start=(i == 0), stop=(i == NDT - 1))
                       for i in range(NDT)],
                      extra_waits=tuple(wmk) + ((qm,) if ot < 2 else ()))
            bank[0].read("v", lambda e, ot=ot, ts=ts: e.tensor_tensor(
                gav[:, ot, :], bank[0].ap[:], mrep[:, ts], ALU.mult),
                extra_waits=(scm,))
            vmk = p.mark("v")
            p.op("s", lambda e, ot=ot, ts=ts: e.activation(
                gav[:, ot, :], gav[:, ot, :], AF.Silu, bias=0.0,
                scale=smf[:, 3:4]), waits=(vmk,), incs=("fin_s",))
            bank[1].read("v", lambda e, ot=ot, ts=ts: e.tensor_tensor(
                ubv[:, ot, :], bank[1].ap[:], mrep[:, ts], ALU.mult))
        smk = p.mark("s")
        hw0 = (smk, p.mark("v"))
        for ot in range(32):
            p.op("v", lambda e, ot=ot, ts=ts: e.tensor_tensor(
                gav[:, ot, :], gav[:, ot, :], ubv[:, ot, :], ALU.mult),
                waits=hw0 if ot == 0 else (),
                incs=("fin_v",) if ot == 31 else ())
        hm = p.mark("v")
        mhtmp = small[:, 2560:3072]
        mhsq = small[:, 3584:4096]
        p.op("v", lambda e: e.tensor_tensor(mhtmp, gav[:, 0, :], gav[:, 0, :],
                                            ALU.mult), waits=(hm,), incs=("fin_v",))
        for ot in range(1, 32):
            p.op("v", lambda e, ot=ot: e.tensor_tensor(mhsq, gav[:, ot, :],
                                                       gav[:, ot, :], ALU.mult),
                 waits=(p.mark("v"),), incs=("fin_v",))
            p.op("v", lambda e: e.tensor_max(mhtmp, mhtmp, mhsq),
                 waits=(p.mark("v"),), incs=("fin_v",))
        gm2 = emit_part_max_rep(ctx, mhtmp, 512, mh, sqrt_of_sq=True)
        p.op("v", lambda e: e.tensor_scalar_max(statA[:, 0:512], mh, 1e-5),
             waits=(gm2,), incs=("fin_v",))
        p.op("v", lambda e: e.reciprocal(statB[:, 0:512], statA[:, 0:512]),
             waits=(p.mark("v"),), incs=("fin_v",))
        bm2 = p.mark("v")
        for ot in range(32):
            p.op("v", lambda e, ot=ot, ts=ts: e.tensor_tensor(
                statA[:, 512:1024], gav[:, ot, :], statB[:, 0:512], ALU.mult),
                waits=(bm2,) if ot == 0 else (p.mark("v"),), incs=("fin_v",))
            p.op("v", lambda e: e.tensor_scalar(statA[:, 512:1024], statA[:, 512:1024],
                                                127.0, MAGIC, ALU.mult, ALU.add),
                 waits=(p.mark("v"),), incs=("fin_v",))
            p.op("v", lambda e: e.tensor_scalar(statA[:, 512:1024], statA[:, 512:1024],
                                                MAGIC, -128.0, ALU.subtract, ALU.max),
                 waits=(p.mark("v"),), incs=("fin_v",))
            p.op("v", lambda e, ot=ot, ts=ts: e.tensor_scalar_min(
                gav[:, ot, :], statA[:, 512:1024], 127.0),
                 waits=(p.mark("v"),), incs=("fin_v",))
        hqm = p.mark("v")
        for ot2 in range(NDT):
            sb4 = (ot2 % 2) * 4
            wdv = wbuf[:].rearrange("p a t -> p (a t)").rearrange(
                "p (s x y) -> p s x y", s=2, x=32)
            sd = f"dms{4 + ot2 % 2}"
            p.op("g", lambda e, ot2=ot2, sb4=sb4, wdv=wdv:
                 e.dma_start(wdv[:, sb4 // 4, :, :],
                             ctx["wq_d"][:, ot2, :, :].rearrange("a p o -> p a o")),
                 waits=(p.mark("t"),) if ot2 >= 2 else (hqm,), incs=(sd,), dma=True)
            wmk = [(sd, p.semv[sd])]
            bk = bank[2]
            bk.mm([lambda e, it=it, sb4=sb4, ts=ts, bk=bk, wdv=wdv:
                   e.matmul(bk.ap[:], wdv[:, sb4 // 4, it, :],
                            gav[:, it, :], start=(it == 0), stop=(it == 31))
                   for it in range(32)],
                  extra_waits=tuple(wmk) + ((hqm,) if ot2 < 2 else ()))
            bk.read("v", lambda e, ot2=ot2, ts=ts, bk=bk: e.tensor_tensor(
                statB[:, 0:512], bk.ap[:], mh, ALU.mult),
                extra_waits=(p.mark("v"),))
            p.op("v", lambda e, ot2=ot2, ts=ts: e.scalar_tensor_tensor(
                xres[:, ot2, ts], statB[:, 0:512], cpool[:, 7:8],
                xres[:, ot2, ts], ALU.mult, ALU.add),
                waits=(p.mark("v"),), incs=("fin_v",))
    return p.mark("v")



def emit_part_max_rep(ctx, m_ap, W, out_rep, clipv=1e-5, sqrt_of_sq=False):
    p = ctx["p"]
    tbank = ctx["tbank"]
    bank = ctx["bank"]
    ident, ones_row = ctx["ident"], ctx["ones_row"]
    small = ctx["small"]
    nt = W // 128
    mtok = ctx["aux"][:, 128:128 + nt]
    for j0 in range(0, nt, 8):
        jn = min(8, nt - j0)
        tbank.mm([lambda e, j=j: e.transpose(
            tbank.ap[:, (j % 8) * 128:(j % 8 + 1) * 128],
            m_ap[:, j * 128:(j + 1) * 128], ident[:])
            for j in range(j0, j0 + jn)], extra_waits=(p.mark("v"),))
        for j in range(j0, j0 + jn):
            tbank.read("v", lambda e, j=j: e.tensor_reduce(
                col2d(mtok[:, j:j + 1]), tbank.ap[:, (j % 8) * 128:(j % 8 + 1) * 128],
                axis=AXX, op=ALU.max))
    p.op("v", lambda e: e.drain(), incs=("fin_v",))
    if sqrt_of_sq:
        p.op("v", lambda e, nt=nt: e.tensor_scalar_max(mtok[:, 0:nt], mtok[:, 0:nt],
                                                       float(clipv) ** 2),
             waits=(p.mark("v"),), incs=("fin_v",))
        vq = p.mark("v")
        p.op("s", lambda e, nt=nt: e.activation(mtok[:, 0:nt], mtok[:, 0:nt], AF.Ln,
                                                bias=0.0, scale=1.0), waits=(vq,),
             incs=("fin_s",))
        p.op("s", lambda e, nt=nt: e.activation(mtok[:, 0:nt], mtok[:, 0:nt], AF.Exp,
                                                bias=0.0, scale=0.5),
             waits=(p.mark("s"),), incs=("fin_s",))
        sq = p.mark("s")
        p.op("v", lambda e: e.drain(), waits=(sq,), incs=("fin_v",))
    mm2 = p.mark("v")
    for j0 in range(0, nt, 8):
        jn = min(8, nt - j0)
        tbank.mm([lambda e, j=j: e.transpose(
            tbank.ap[0:1, (j % 8) * 128:(j % 8 + 1) * 128],
            mtok[:, j:j + 1], ident[:])
            for j in range(j0, j0 + jn)], extra_waits=(mm2,))
        tbank.read("v", lambda e, j0=j0, jn=jn: e.tensor_copy(
            out_rep[0:1, j0 * 128:(j0 + jn) * 128], tbank.ap[0:1, 0:jn * 128]))
    p.op("v", lambda e: e.drain(), incs=("fin_v",))
    rm = p.mark("v")
    for j in range(nt):
        bk = bank[2]
        bk.mm([lambda e, j=j, bk=bk: e.matmul(
            bk.ap[:, 0:128], ones_row[0:1, 0:128],
            out_rep[0:1, j * 128:(j + 1) * 128], start=True, stop=True)],
              extra_waits=(rm,))
        bk.read("v", lambda e, j=j, bk=bk: e.tensor_scalar_max(
            out_rep[:, j * 128:(j + 1) * 128], bk.ap[:, 0:128], clipv))
    p.op("v", lambda e: e.drain(), incs=("fin_v",))
    return p.mark("v")


def _wq_out_ap(dst, it, oc):
    nt = dst.shape[1]
    base = (it * nt + oc * 4) * 128 * 128
    return bass.AP(dst.tensor, base, [[128, 128], [128 * 128, 4], [1, 128]])


def emit_l0_attn_b(ctx, idx):
    nc, p, X = ctx["nc"], ctx["p"], ctx["X"]
    xn = ctx["xn"]
    k = ctx["bigA"]
    vtok = ctx["bigB"][:].rearrange("p a t -> p (a t)").rearrange("p (x y) -> p x y", x=NTT)
    aux = ctx["aux"]
    qtile = aux[:, 4480:5504].rearrange("p (a b) -> p a b", a=8)
    cosb = aux[:, 0:T]
    sinb = aux[:, T:2 * T]
    qstg = aux[:, 2 * T:2 * T + 256].rearrange("p (a b) -> p a b", a=2)
    trilm = aux[:, 2 * T + 256:2 * T + 384]
    wbuf, small, smf = ctx["wbuf"], ctx["small"], ctx["smf"]
    xres = ctx["xres"]
    dbank, bank, tbank = ctx["dbank"], ctx["bank"], ctx["tbank"]
    ones_row, ident = ctx["ones_row"], ctx["ident"]

    p.op("v", lambda e: e.memset(qstg[:], 0.0), incs=("fin_v",))
    lnm = emit_layernorm(ctx)

    wq = X["L0_attn_b_WqT_perm"].rearrange("(n q) o -> q n o", q=128)
    wk = X["L0_attn_b_WkT_perm"].rearrange("(n q) o -> q n o", q=128)
    wv = X["L0_attn_b_WvT"].rearrange("(n q) o -> q n o", q=128)
    wo = X["L0_attn_b_WoT"].rearrange("(n q) o -> q n o", q=128)

    p.op("g", lambda e: e.dma_start(wbuf[:], wk), incs=("dmaw",), dma=True)
    wm = [("dmaw", p.semv["dmaw"]), lnm]
    for ot in range(NDT):
        for c in range(2):
            bk = dbank[c]
            ems = []
            for s2 in range(2):
                for i in range(NDT):
                    ems.append(lambda e, i=i, ot=ot, c=c, s2=s2, bk=bk:
                               e.matmul(bk.ap[:, s2 * 512:(s2 + 1) * 512],
                                        wbuf[:, i, ot * 128:(ot + 1) * 128],
                                        xn[:, i, c * 1024 + s2 * 512:c * 1024 + (s2 + 1) * 512],
                                        start=(i == 0), stop=(i == NDT - 1)))
            bk.mm(ems, extra_waits=tuple(wm) if ot == 0 else ())
            bk.read("v", lambda e, ot=ot, c=c, bk=bk:
                    e.tensor_copy(k[:, ot, c * 1024:(c + 1) * 1024], bk.ap[:]))
    p.op("g", lambda e: e.dma_start(wbuf[:], wv), waits=(p.mark("t"),),
         incs=("dmaw",), dma=True)
    wm = [("dmaw", p.semv["dmaw"])]
    for tt in range(NTT):
        for c in range(2):
            bk = bank[c]
            sl = slice(c * 512, (c + 1) * 512)
            bk.mm([lambda e, i=i, tt=tt, sl=sl, bk=bk:
                   e.matmul(bk.ap[:], xn[:, i, tt * 128:(tt + 1) * 128],
                            wbuf[:, i, sl], start=(i == 0), stop=(i == NDT - 1))
                   for i in range(NDT)],
                  extra_waits=tuple(wm) if tt == 0 else ())
            bk.read("v", lambda e, tt=tt, sl=sl, bk=bk:
                    e.tensor_copy(vtok[:, tt, sl], bk.ap[:]))
    p.op("g", lambda e: e.dma_start(wbuf[:], wq), waits=(p.mark("t"),),
         incs=("dmaw",), dma=True)
    p.op("g", lambda e: e.dma_start(cosb, X["c_rope_cos"]),
         incs=("dmaw",), dma=True)
    p.op("g", lambda e: e.dma_start(sinb, X["c_rope_sin"]),
         incs=("dmaw",), dma=True)
    p.op("g", lambda e: e.dma_start(trilm, X["c_trilm"]),
         incs=("dmaw",), dma=True)
    wm = [("dmaw", p.semv["dmaw"])]
    p.op("t", lambda e: e.nop(), waits=wm)
    p.op("v", lambda e: e.nop(), waits=wm)

    ta = small[:, 0:1024]
    tb = small[:, 1024:2048]
    tc2 = small[:, 2048:3072]
    for g in range(4):
        for c in range(2):
            sl = slice(c * 1024, (c + 1) * 1024)
            vw = (p.mark("v"),)
            p.op("v", lambda e, g=g, sl=sl: e.tensor_tensor(ta, k[:, g, sl],
                                                            cosb[:, sl], ALU.mult),
                 waits=vw, incs=("fin_v",))
            p.op("v", lambda e, g=g, sl=sl: e.tensor_tensor(tb, k[:, 4 + g, sl],
                                                            sinb[:, sl], ALU.mult),
                 incs=("fin_v",))
            p.op("v", lambda e, g=g, sl=sl: e.tensor_tensor(tc2, k[:, g, sl],
                                                            sinb[:, sl], ALU.mult),
                 incs=("fin_v",))
            vm = p.mark("v")
            p.op("v", lambda e, g=g, sl=sl: e.tensor_tensor(k[:, g, sl], ta, tb,
                                                            ALU.subtract),
                 waits=(vm,), incs=("fin_v",))
            p.op("v", lambda e, g=g, sl=sl: e.tensor_tensor(ta, k[:, 4 + g, sl],
                                                            cosb[:, sl], ALU.mult),
                 waits=(p.mark("v"),), incs=("fin_v",))
            vm = p.mark("v")
            p.op("v", lambda e, g=g, sl=sl: e.tensor_tensor(k[:, 4 + g, sl], tc2, ta,
                                                            ALU.add),
                 waits=(vm,), incs=("fin_v",))
    p.op("v", lambda e: e.drain(), incs=("fin_v",))
    krm = p.mark("v")

    probs = small[:, 0:T]
    pt = small[:, T:2 * T]

    for tq in range(NTT):
        tqs = slice(tq * 128, (tq + 1) * 128)
        w = (tq + 1) * 128
        for ot in range(NDT):
            bk = bank[2]
            bk.mm([lambda e, i=i, ot=ot, tq=tq, bk=bk:
                   e.matmul(bk.ap[:, 0:128], wbuf[:, i, ot * 128:(ot + 1) * 128],
                            xn[:, i, tq * 128:(tq + 1) * 128], start=(i == 0),
                            stop=(i == NDT - 1))
                   for i in range(NDT)],
                  extra_waits=(krm,) if (tq == 0 and ot == 0) else ())
            bk.read("v", lambda e, ot=ot, bk=bk:
                    e.tensor_copy(qtile[:, ot, :], bk.ap[:, 0:128]))
        vw = (p.mark("v"),)
        p.op("v", lambda e, tqs=tqs: e.tensor_tensor(
            ta[:, 0:512], qtile[:, 0:4, :].rearrange("p a t -> p (a t)"),
            _rope4(cosb, tqs), ALU.mult), waits=vw, incs=("fin_v",))
        p.op("v", lambda e, tqs=tqs: e.tensor_tensor(
            tb[:, 0:512], qtile[:, 4:8, :].rearrange("p a t -> p (a t)"),
            _rope4(sinb, tqs), ALU.mult), incs=("fin_v",))
        p.op("v", lambda e, tqs=tqs: e.tensor_tensor(
            tc2[:, 0:512], qtile[:, 0:4, :].rearrange("p a t -> p (a t)"),
            _rope4(sinb, tqs), ALU.mult), incs=("fin_v",))
        vm = p.mark("v")
        p.op("v", lambda e: e.tensor_tensor(
            qtile[:, 0:4, :].rearrange("p a t -> p (a t)"), ta[:, 0:512],
            tb[:, 0:512], ALU.subtract), waits=(vm,), incs=("fin_v",))
        p.op("v", lambda e, tqs=tqs: e.tensor_tensor(
            ta[:, 0:512], qtile[:, 4:8, :].rearrange("p a t -> p (a t)"),
            _rope4(cosb, tqs), ALU.mult), waits=(p.mark("v"),), incs=("fin_v",))
        vm = p.mark("v")
        p.op("v", lambda e: e.tensor_tensor(
            qtile[:, 4:8, :].rearrange("p a t -> p (a t)"), tc2[:, 0:512],
            ta[:, 0:512], ALU.add), waits=(vm,), incs=("fin_v",))
        qrm = p.mark("v")

        for hi in range(16):
            m, g = hi // 4, hi % 4
            h = g * 4 + m
            if g == 0:
                pm = (m - 1) % 4
                p.op("v", lambda e, pm=pm: e.memset(
                    qstg[32 * pm:32 * pm + 32, :, :], 0.0),
                    waits=(qrm, p.mark("t")), incs=("fin_v",))
            p.op("v", lambda e, g=g, m=m: e.tensor_copy(
                qstg[32 * m:32 * m + 32, 0, :], qtile[32 * m:32 * m + 32, g, :]),
                waits=(qrm, p.mark("t")), incs=("fin_v",))
            p.op("v", lambda e, g=g, m=m: e.tensor_copy(
                qstg[32 * m:32 * m + 32, 1, :], qtile[32 * m:32 * m + 32, 4 + g, :]),
                incs=("fin_v",))
            stm = p.mark("v")
            nch = tq // 4 + 1
            for ch in range(nch):
                bk = dbank[ch // 2]
                base = ch * 512
                nn = min(512, w - base)
                bk.mm([lambda e, par=par, ch=ch, nn=nn, bk=bk, g=g, base=base:
                       e.matmul(bk.ap[:, (ch % 2) * 512:(ch % 2) * 512 + nn],
                                qstg[:, par, :], k[:, par * 4 + g, base:base + nn],
                                start=(par == 0), stop=(par == 1))
                       for par in range(2)],
                      extra_waits=(stm,) if ch == 0 else ())
            dbk = dbank[(tq // 4) // 2]
            doff = ((tq // 4) % 2) * 512 + (tq % 4) * 128
            dbk.read("v", lambda e, dbk=dbk, doff=doff: e.tensor_add(
                dbk.ap[:, doff:doff + 128], dbk.ap[:, doff:doff + 128], trilm))
            mskm = p.mark("v")
            nred = (w + 1023) // 1024
            for r in range(nred):
                ww = min(1024, w - r * 1024)
                dbank[r].read("v", lambda e, r=r, ww=ww: e.tensor_reduce(
                    col2d(smf[:, r:r + 1]), dbank[r].ap[:, 0:ww], axis=AXX,
                    op=ALU.max), extra_waits=(mskm,))
            p.op("v", lambda e, nred=nred: e.tensor_reduce(
                col2d(smf[:, 2:3]), smf[:, 0:nred], axis=AXX, op=ALU.max),
                waits=(p.mark("v"),), incs=("fin_v",))
            p.op("v", lambda e: e.tensor_scalar_mul(smf[:, 3:4], smf[:, 2:3],
                                                    -0.125),
                 waits=(p.mark("v"),), incs=("fin_v",))
            vmk = p.mark("v")
            for r in range(nred):
                ww = min(1024, w - r * 1024)
                dbank[r].read("s", lambda e, r=r, ww=ww: e.activation(
                    probs[:, r * 1024:r * 1024 + ww], dbank[r].ap[:, 0:ww], AF.Exp,
                    bias=smf[:, 3:4], scale=0.125), extra_waits=(vmk,))
            smk = p.mark("s")
            p.op("v", lambda e, w=w: e.tensor_reduce(col2d(smf[:, 4:5]),
                                                     probs[:, 0:w], axis=AXX,
                                                     op=ALU.add),
                 waits=(smk,), incs=("fin_v",))
            p.op("v", lambda e: e.reciprocal(smf[:, 5:6], smf[:, 4:5]),
                 waits=(p.mark("v"),), incs=("fin_v",))
            p.op("v", lambda e, w=w: e.tensor_scalar_mul(probs[:, 0:w], probs[:, 0:w],
                                                         smf[:, 5:6]),
                 waits=(p.mark("v"),), incs=("fin_v",))
            dmk = p.mark("v")
            for tk0 in range(0, tq + 1, 8):
                tkn = min(8, tq + 1 - tk0)
                tbank.mm([lambda e, tkt=tkt: e.transpose(
                    tbank.ap[:, (tkt % 8) * 128:(tkt % 8 + 1) * 128],
                    probs[:, tkt * 128:(tkt + 1) * 128], ident[:])
                    for tkt in range(tk0, tk0 + tkn)], extra_waits=(dmk,))
                tbank.read("v", lambda e, tk0=tk0, tkn=tkn: e.tensor_copy(
                    pt[:, tk0 * 128:(tk0 + tkn) * 128],
                    tbank.ap[:, 0:tkn * 128]))
            ptm = p.mark("v")
            hb = (h % 2) * 64
            bk = bank[0] if h % 2 == 0 else bank[1]
            bk.mm([lambda e, tkt=tkt, h=h, hb=hb, bk=bk, tq=tq:
                   e.matmul(bk.ap[hb:hb + 64, 0:128],
                            vtok[:, tkt, h * 64:(h + 1) * 64],
                            pt[:, tkt * 128:(tkt + 1) * 128],
                            start=(tkt == 0), stop=(tkt == tq))
                   for tkt in range(tq + 1)], extra_waits=(ptm,))
            bk.read("v", lambda e, h=h, hb=hb, tqs=tqs, bk=bk: e.tensor_copy(
                bass.AP(xn[:].tensor, xn[:].offset + (h // 2) * T + tqs.start,
                        [[NDT * T, 64], [1, 128]]) if False else
                xn[hb:hb + 64, h // 2, tqs], bk.ap[hb:hb + 64, 0:128]))
    p.op("v", lambda e: e.drain(), incs=("fin_v",))
    aom = p.mark("v")

    p.op("g", lambda e: e.dma_start(wbuf[:], wo), waits=(p.mark("t"),),
         incs=("dmaw",), dma=True)
    wm = [("dmaw", p.semv["dmaw"]), aom]
    for ot in range(NDT):
        for c in range(NTC):
            bk = bank[c % 2]
            sl = slice(c * 512, (c + 1) * 512)
            bk.mm([lambda e, i=i, ot=ot, sl=sl, bk=bk:
                   e.matmul(bk.ap[:], wbuf[:, i, ot * 128:(ot + 1) * 128],
                            xn[:, i, sl], start=(i == 0), stop=(i == NDT - 1))
                   for i in range(NDT)],
                  extra_waits=tuple(wm) if ot == 0 and c == 0 else ())
            bk.read("v", lambda e, ot=ot, sl=sl, bk=bk: e.tensor_add(
                xres[:, ot, sl], xres[:, ot, sl], bk.ap[:]))
    p.op("v", lambda e: e.drain(), incs=("fin_v",))
    return p.mark("v")


def _rope4(tab, tqs):
    sl = tab[:, tqs]
    return bcast(sl, 1, 4)


def emit_l1_ffn(ctx, idx):
    nc, p, X = ctx["nc"], ctx["p"], ctx["X"]
    xq = ctx["xn"]
    ga, ub = ctx["bigA"], ctx["bigB"]
    gav = ga[:].rearrange("p a t -> p (a t)")[:, 0:24 * 512].rearrange(
        "p (x y) -> p x y", x=24)
    ubv = ub[:].rearrange("p a t -> p (a t)")[:, 0:24 * 512].rearrange(
        "p (x y) -> p x y", x=24)
    small, smf = ctx["small"], ctx["smf"]
    statA, statB = ctx["statA"], ctx["statB"]
    cpool, lcol = ctx["cpool"], ctx["lcol"]
    wbuf = ctx["wbuf"]
    wflat = wbuf[:].rearrange("p a t -> p (a t)")
    xres = ctx["xres"]
    ident = ctx["ident"]
    bank, dbank, tbank = ctx["bank"], ctx["dbank"], ctx["tbank"]
    II = 3072
    NIT = II // 128

    if "wq_lg" not in ctx:
        ctx["wq_lg"] = nc.dram_tensor("wq_lg", [8, NIT, 128, 128], BF16).ap()
        ctx["wq_lu"] = nc.dram_tensor("wq_lu", [8, NIT, 128, 128], BF16).ap()
        ctx["wq_ld"] = nc.dram_tensor("wq_ld", [NIT, 8, 128, 128], BF16).ap()
        ctx["upr"] = nc.dram_tensor("upr", [NIT, 128, T], BF16).ap()
        ctx["gtr"] = nc.dram_tensor("gtr", [NIT, 128, T], BF16).ap()
        ctx["cc_in"] = nc.dram_tensor("cc_in", [1, 8], BF16).ap()
        ctx["cc_out"] = nc.dram_tensor("cc_out", [1, 8], BF16).ap()

    def dbar(sems):
        p.op("v", lambda e: e.drain(),
             waits=tuple((sq, p.semv[sq]) for sq in sems), incs=("fin_v",))

    if not ctx.get("l1ffn_quantized"):
        ctx["l1ffn_quantized"] = True
        for nm, dst, O, I in (("L1_ffn_w_gate", ctx["wq_lg"], II, 1024),
                              ("L1_ffn_w_up", ctx["wq_lu"], II, 1024),
                              ("L1_ffn_w_down", ctx["wq_ld"], 1024, II)):
            wv = X[nm].rearrange("(a q) o -> q a o", q=128)
            for ot in range(O // 128):
                for ic in range(I // 512):
                    half = (ot * (I // 512) + ic) % 2
                    hs = slice(half * 512, (half + 1) * 512)
                    p.op("y", lambda e, ot=ot, ic=ic, wv=wv, hs=hs:
                         e.dma_start(statA[:, hs], wv[:, ot, ic * 512:(ic + 1) * 512]),
                         waits=(p.mark("v"), ("dma", p.semv["dma"])),
                         incs=("dma",), dma=True)
                    dmw = [("dma", p.semv["dma"])]
                    p.op("v", lambda e, hs=hs: e.tensor_reduce(
                        _ins(smf[:, 0:16], 2, 1, 1),
                        statA[:, hs].rearrange("p (a b) -> p a b", b=32),
                        axis=AXX, op=ALU.max, apply_absolute_value=True),
                        waits=dmw, incs=("fin_v",))
                    p.op("v", lambda e: e.tensor_scalar(
                        smf[:, 0:16], smf[:, 0:16], 1.0 / 7.0, 1e-12,
                        ALU.mult, ALU.max), waits=(p.mark("v"),), incs=("fin_v",))
                    p.op("v", lambda e: e.reciprocal(lcol[:, 0:16], smf[:, 0:16]),
                         waits=(p.mark("v"),), incs=("fin_v",))
                    p.op("v", lambda e, hs=hs: e.tensor_tensor(
                        statB[:, hs].rearrange("p (a b) -> p a b", b=32),
                        statA[:, hs].rearrange("p (a b) -> p a b", b=32),
                        bcast(lcol[:, 0:16], 2, 32), ALU.mult),
                        waits=(p.mark("v"),), incs=("fin_v",))
                    p.op("v", lambda e, hs=hs: e.tensor_scalar(
                        statB[:, hs], statB[:, hs], MAGIC, MAGIC,
                        ALU.add, ALU.subtract), waits=(p.mark("v"),), incs=("fin_v",))
                    p.op("v", lambda e, hs=hs: e.tensor_scalar(
                        statB[:, hs], statB[:, hs], -8.0, 7.0,
                        ALU.max, ALU.min), waits=(p.mark("v"),), incs=("fin_v",))
                    p.op("v", lambda e, hs=hs: e.tensor_tensor(
                        small[:, hs].rearrange("p (a b) -> p a b", b=32),
                        statB[:, hs].rearrange("p (a b) -> p a b", b=32),
                        bcast(smf[:, 0:16], 2, 32), ALU.mult),
                        waits=(p.mark("v"),), incs=("fin_v",))
                    wqm = p.mark("v")
                    tbank.mm([lambda e, j=j, half=half: e.transpose(
                        tbank.ap[:, (j % 8) * 128:(j % 8 + 1) * 128],
                        small[:, half * 512 + j * 128:half * 512 + (j + 1) * 128],
                        ident[:]) for j in range(4)], extra_waits=(wqm,))
                    sq = f"dms{6 + half}"
                    for j in range(4):
                        tbank.read("v", lambda e, j=j, half=half: e.tensor_copy(
                            small[:, 2048 + half * 512 + j * 128:
                                  2048 + half * 512 + (j + 1) * 128],
                            tbank.ap[:, (j % 8) * 128:(j % 8 + 1) * 128]),
                            extra_waits=((sq, p.semv[sq]),) if j == 0 else ())
                    p.op("v", lambda e: e.drain(), incs=("fin_v",))
                    trm = p.mark("v")
                    p.op("g", lambda e, ot=ot, ic=ic, half=half, dst=dst, I=I:
                         e.dma_start(_lpbq_out_ap(dst, ot, ic),
                                     small[:, 2048 + half * 512:2048 + (half + 1) * 512]
                                     .rearrange("p (a b) -> p a b", a=4)),
                         waits=(trm,), incs=(sq,), dma=True)
        dbar(("dms6", "dms7"))

    lnm = emit_layernorm(ctx)

    mcol = smf[:, 4:5]
    p.op("v", lambda e: e.tensor_tensor(statA[:], xq[:, 0, 0:1024], xq[:, 0, 0:1024],
                                        ALU.mult), waits=(lnm,), incs=("fin_v",))
    for i in range(NDT):
        for c in range(2):
            if i == 0 and c == 0:
                continue
            p.op("v", lambda e, i=i, c=c: e.tensor_tensor(
                statB[:], xq[:, i, c * 1024:(c + 1) * 1024],
                xq[:, i, c * 1024:(c + 1) * 1024], ALU.mult),
                waits=(p.mark("v"),), incs=("fin_v",))
            p.op("v", lambda e: e.tensor_max(statA[:], statA[:], statB[:]),
                 waits=(p.mark("v"),), incs=("fin_v",))
    p.op("v", lambda e: e.tensor_reduce(col2d(mcol), statA[:], axis=AXX, op=ALU.max),
         waits=(p.mark("v"),), incs=("fin_v",))
    _sqrt_col(ctx, mcol, 1)
    arm = _allreduce_max(ctx, [mcol])
    p.op("v", lambda e: e.tensor_scalar_mul(cpool[:, 8:9], lcol[:, 64:65], 1.0 / 32767.0),
         waits=(arm,), incs=("fin_v",))
    p.op("v", lambda e: e.reciprocal(cpool[:, 9:10], cpool[:, 8:9]),
         waits=(p.mark("v"),), incs=("fin_v",))
    scm = p.mark("v")
    for i in range(NDT):
        for c in range(2):
            sl = slice(c * 1024, (c + 1) * 1024)
            p.op("v", lambda e, i=i, sl=sl: e.tensor_scalar(
                statA[:], xq[:, i, sl], cpool[:, 9:10], MAGIC, ALU.mult, ALU.add),
                waits=(scm,) if i == 0 and c == 0 else (p.mark("v"),), incs=("fin_v",))
            p.op("v", lambda e: e.tensor_scalar(statA[:], statA[:], MAGIC, -32768.0,
                                                ALU.subtract, ALU.max),
                 waits=(p.mark("v"),), incs=("fin_v",))
            p.op("v", lambda e, i=i, sl=sl: e.tensor_scalar_min(
                xq[:, i, sl], statA[:], 32767.0),
                 waits=(p.mark("v"),), incs=("fin_v",))
    p.op("v", lambda e: e.drain(), incs=("fin_v",))
    qm = p.mark("v")

    p.op("v", lambda e: e.memset(smf[:, 5:7], 0.0), incs=("fin_v",))
    for c in range(4):
        ts = slice(c * 512, (c + 1) * 512)
        if c > 0:
            dbar(("dms4", "dms5"))
        for ot in range(NIT):
            slot_g, slot_u = (ot % 2) * 2, (ot % 2) * 2 + 1
            sg, su = f"dms{slot_g}", f"dms{slot_u}"
            p.op("g", lambda e, ot=ot, slot_g=slot_g: e.dma_start(
                wflat[:, slot_g * 2048:slot_g * 2048 + 1024]
                .rearrange("p (a b) -> p a b", a=8),
                ctx["wq_lg"][:, ot, :, :].rearrange("a p o -> p a o")),
                waits=(p.mark("t"),) + ((qm,) if c == 0 and ot == 0 else ()),
                incs=(sg,), dma=True)
            p.op("g", lambda e, ot=ot, slot_u=slot_u: e.dma_start(
                wflat[:, slot_u * 2048:slot_u * 2048 + 1024]
                .rearrange("p (a b) -> p a b", a=8),
                ctx["wq_lu"][:, ot, :, :].rearrange("a p o -> p a o")),
                incs=(su,), dma=True)
            wmk = [(sg, p.semv[sg]), (su, p.semv[su])]
            for slot, bidx, dstv in ((slot_g, 0, gav), (slot_u, 1, ubv)):
                bk = bank[bidx]
                bk.mm([lambda e, i=i, slot=slot, ts=ts, bk=bk:
                       e.matmul(bk.ap[:], wflat[:, slot * 2048 + i * 128:slot * 2048 + (i + 1) * 128],
                                xq[:, i, ts], start=(i == 0), stop=(i == NDT - 1))
                       for i in range(NDT)],
                      extra_waits=tuple(wmk) + ((qm,) if c == 0 and ot < 2 else ()))
                bk.read("v", lambda e, ot=ot, bk=bk, dstv=dstv: e.tensor_scalar_mul(
                    dstv[:, ot, :], bk.ap[:], cpool[:, 8:9]),
                    extra_waits=(p.mark("v"),))
            for dstv, mc in ((gav, 6), (ubv, 5)):
                p.op("v", lambda e, ot=ot, dstv=dstv: e.tensor_tensor(
                    statA[:, 0:512], dstv[:, ot, :], dstv[:, ot, :], ALU.mult),
                    waits=(p.mark("v"),), incs=("fin_v",))
                p.op("v", lambda e: e.tensor_reduce(col2d(smf[:, 7:8]), statA[:, 0:512],
                                                    axis=AXX, op=ALU.max),
                     waits=(p.mark("v"),), incs=("fin_v",))
                p.op("v", lambda e, mc=mc: e.tensor_max(smf[:, mc:mc + 1],
                                                        smf[:, mc:mc + 1],
                                                        smf[:, 7:8]),
                     waits=(p.mark("v"),), incs=("fin_v",))
        gu = p.mark("v")
        for ot in range(NIT):
            p.op("g", lambda e, ot=ot, ts=ts: e.dma_start(ctx["gtr"][ot, :, ts],
                                                          gav[:, ot, :]),
                 waits=(gu,), incs=("dms4",), dma=True)
            p.op("g", lambda e, ot=ot, ts=ts: e.dma_start(ctx["upr"][ot, :, ts],
                                                          ubv[:, ot, :]),
                 incs=("dms5",), dma=True)
    _sqrt_col(ctx, smf[:, 5:7], 2)
    arm2 = _allreduce_max(ctx, [smf[:, 5:6], smf[:, 6:7]])
    p.op("v", lambda e: e.tensor_scalar_mul(cpool[:, 10:11], lcol[:, 64:65],
                                            1.0 / 32767.0), waits=(arm2,),
         incs=("fin_v",))
    p.op("v", lambda e: e.tensor_scalar_mul(cpool[:, 12:13], lcol[:, 65:66],
                                            1.0 / 32767.0),
         waits=(p.mark("v"),), incs=("fin_v",))
    p.op("v", lambda e: e.reciprocal(cpool[:, 11:12], cpool[:, 10:11]),
         waits=(p.mark("v"),), incs=("fin_v",))
    p.op("v", lambda e: e.reciprocal(cpool[:, 13:14], cpool[:, 12:13]),
         waits=(p.mark("v"),), incs=("fin_v",))
    p.op("v", lambda e: e.memset(smf[:, 5:6], 0.0), waits=(p.mark("v"),),
         incs=("fin_v",))
    dbar(("dms4", "dms5"))

    for c in range(4):
        ts = slice(c * 512, (c + 1) * 512)
        if c > 0:
            dbar(("dms2", "dms3"))
        for ot in range(NIT):
            p.op("g", lambda e, ot=ot, ts=ts: e.dma_start(gav[:, ot, :],
                                                          ctx["gtr"][ot, :, ts]),
                 waits=(p.mark("v"), p.mark("s")), incs=("dms0",), dma=True)
            p.op("g", lambda e, ot=ot, ts=ts: e.dma_start(ubv[:, ot, :],
                                                          ctx["upr"][ot, :, ts]),
                 incs=("dms1",), dma=True)
            ldm = [("dms0", p.semv["dms0"]), ("dms1", p.semv["dms1"])]
            p.op("v", lambda e, ot=ot: e.tensor_scalar(
                statA[:, 0:512], ubv[:, ot, :], cpool[:, 11:12], MAGIC,
                ALU.mult, ALU.add), waits=tuple(ldm), incs=("fin_v",))
            p.op("v", lambda e: e.tensor_scalar(statA[:, 0:512], statA[:, 0:512],
                                                MAGIC, -32768.0, ALU.subtract,
                                                ALU.max),
                 waits=(p.mark("v"),), incs=("fin_v",))
            p.op("v", lambda e: e.tensor_scalar_min(statA[:, 0:512], statA[:, 0:512],
                                                    32767.0),
                 waits=(p.mark("v"),), incs=("fin_v",))
            p.op("v", lambda e, ot=ot: e.tensor_scalar_mul(ubv[:, ot, :],
                                                           statA[:, 0:512],
                                                           cpool[:, 10:11]),
                 waits=(p.mark("v"),), incs=("fin_v",))
            p.op("v", lambda e, ot=ot: e.tensor_scalar(
                statA[:, 0:512], gav[:, ot, :], cpool[:, 13:14], MAGIC,
                ALU.mult, ALU.add), waits=(p.mark("v"),), incs=("fin_v",))
            p.op("v", lambda e: e.tensor_scalar(statA[:, 0:512], statA[:, 0:512],
                                                MAGIC, -32768.0, ALU.subtract,
                                                ALU.max),
                 waits=(p.mark("v"),), incs=("fin_v",))
            p.op("v", lambda e: e.tensor_scalar_min(statA[:, 0:512], statA[:, 0:512],
                                                    32767.0),
                 waits=(p.mark("v"),), incs=("fin_v",))
            p.op("v", lambda e, ot=ot: e.tensor_scalar_mul(gav[:, ot, :],
                                                           statA[:, 0:512],
                                                           cpool[:, 12:13]),
                 waits=(p.mark("v"),), incs=("fin_v",))
            gdm = p.mark("v")
            p.op("s", lambda e, ot=ot: e.activation(statB[:, 0:512], gav[:, ot, :],
                                                    AF.Sigmoid, bias=0.0, scale=1.0),
                 waits=(gdm,), incs=("fin_s",))
            p.op("v", lambda e: e.tensor_scalar(statB[:, 0:512], statB[:, 0:512],
                                                65536.0, MAGIC, ALU.mult, ALU.add),
                 waits=(p.mark("s"),), incs=("fin_v",))
            p.op("v", lambda e: e.tensor_scalar(statB[:, 0:512], statB[:, 0:512],
                                                MAGIC, 0.0, ALU.subtract, ALU.max),
                 waits=(p.mark("v"),), incs=("fin_v",))
            p.op("v", lambda e: e.tensor_scalar(statB[:, 0:512], statB[:, 0:512],
                                                65535.0, 1.0 / 65536.0, ALU.min,
                                                ALU.mult),
                 waits=(p.mark("v"),), incs=("fin_v",))
            p.op("v", lambda e, ot=ot: e.tensor_tensor(gav[:, ot, :], gav[:, ot, :],
                                                       statB[:, 0:512], ALU.mult),
                 waits=(p.mark("v"),), incs=("fin_v",))
            p.op("v", lambda e, ot=ot: e.tensor_tensor(statA[:, 0:512], gav[:, ot, :],
                                                       gav[:, ot, :], ALU.mult),
                 waits=(p.mark("v"),), incs=("fin_v",))
            p.op("v", lambda e: e.tensor_reduce(col2d(smf[:, 7:8]), statA[:, 0:512],
                                                axis=AXX, op=ALU.max),
                 waits=(p.mark("v"),), incs=("fin_v",))
            p.op("v", lambda e: e.tensor_max(smf[:, 5:6], smf[:, 5:6], smf[:, 7:8]),
                 waits=(p.mark("v"),), incs=("fin_v",))
            g2m = p.mark("v")
            p.op("g", lambda e, ot=ot, ts=ts: e.dma_start(ctx["gtr"][ot, :, ts],
                                                          gav[:, ot, :]),
                 waits=(g2m,), incs=("dms2",), dma=True)
            p.op("g", lambda e, ot=ot, ts=ts: e.dma_start(ctx["upr"][ot, :, ts],
                                                          ubv[:, ot, :]),
                 incs=("dms3",), dma=True)
    _sqrt_col(ctx, smf[:, 5:6], 3)
    arm3 = _allreduce_max(ctx, [smf[:, 5:6]])
    p.op("v", lambda e: e.tensor_scalar_mul(cpool[:, 10:11], lcol[:, 64:65],
                                            1.0 / 32767.0), waits=(arm3,),
         incs=("fin_v",))
    p.op("v", lambda e: e.reciprocal(cpool[:, 11:12], cpool[:, 10:11]),
         waits=(p.mark("v"),), incs=("fin_v",))
    p.op("v", lambda e: e.memset(smf[:, 6:7], 0.0), waits=(p.mark("v"),),
         incs=("fin_v",))
    dbar(("dms2", "dms3"))

    for c in range(4):
        ts = slice(c * 512, (c + 1) * 512)
        if c > 0:
            dbar(("dms2",))
        for ot in range(NIT):
            p.op("g", lambda e, ot=ot, ts=ts: e.dma_start(gav[:, ot, :],
                                                          ctx["gtr"][ot, :, ts]),
                 waits=(p.mark("v"),), incs=("dms0",), dma=True)
            p.op("g", lambda e, ot=ot, ts=ts: e.dma_start(ubv[:, ot, :],
                                                          ctx["upr"][ot, :, ts]),
                 incs=("dms1",), dma=True)
            ldm = [("dms0", p.semv["dms0"]), ("dms1", p.semv["dms1"])]
            p.op("v", lambda e, ot=ot: e.tensor_scalar(
                statA[:, 0:512], gav[:, ot, :], cpool[:, 11:12], MAGIC,
                ALU.mult, ALU.add), waits=tuple(ldm), incs=("fin_v",))
            p.op("v", lambda e: e.tensor_scalar(statA[:, 0:512], statA[:, 0:512],
                                                MAGIC, -32768.0, ALU.subtract,
                                                ALU.max),
                 waits=(p.mark("v"),), incs=("fin_v",))
            p.op("v", lambda e: e.tensor_scalar_min(statA[:, 0:512], statA[:, 0:512],
                                                    32767.0),
                 waits=(p.mark("v"),), incs=("fin_v",))
            p.op("v", lambda e: e.tensor_scalar_mul(statA[:, 0:512], statA[:, 0:512],
                                                    cpool[:, 10:11]),
                 waits=(p.mark("v"),), incs=("fin_v",))
            p.op("v", lambda e, ot=ot: e.tensor_tensor(gav[:, ot, :], statA[:, 0:512],
                                                       ubv[:, ot, :], ALU.mult),
                 waits=(p.mark("v"),), incs=("fin_v",))
            p.op("v", lambda e, ot=ot: e.tensor_tensor(statA[:, 0:512], gav[:, ot, :],
                                                       gav[:, ot, :], ALU.mult),
                 waits=(p.mark("v"),), incs=("fin_v",))
            p.op("v", lambda e: e.tensor_reduce(col2d(smf[:, 7:8]), statA[:, 0:512],
                                                axis=AXX, op=ALU.max),
                 waits=(p.mark("v"),), incs=("fin_v",))
            p.op("v", lambda e: e.tensor_max(smf[:, 6:7], smf[:, 6:7], smf[:, 7:8]),
                 waits=(p.mark("v"),), incs=("fin_v",))
            o3 = p.mark("v")
            p.op("g", lambda e, ot=ot, ts=ts: e.dma_start(ctx["gtr"][ot, :, ts],
                                                          gav[:, ot, :]),
                 waits=(o3,), incs=("dms2",), dma=True)
    _sqrt_col(ctx, smf[:, 6:7], 4)
    arm4 = _allreduce_max(ctx, [smf[:, 6:7]])
    p.op("v", lambda e: e.tensor_scalar_mul(cpool[:, 12:13], lcol[:, 64:65],
                                            1.0 / 32767.0), waits=(arm4,),
         incs=("fin_v",))
    p.op("v", lambda e: e.reciprocal(cpool[:, 13:14], cpool[:, 12:13]),
         waits=(p.mark("v"),), incs=("fin_v",))
    dbar(("dms2",))

    for c in range(4):
        ts = slice(c * 512, (c + 1) * 512)
        for ot in range(NIT):
            p.op("g", lambda e, ot=ot, ts=ts: e.dma_start(gav[:, ot, :],
                                                          ctx["gtr"][ot, :, ts]),
                 waits=(p.mark("v"), p.mark("t")), incs=("dms0",), dma=True)
            ldm = [("dms0", p.semv["dms0"])]
            p.op("v", lambda e, ot=ot: e.tensor_scalar(
                statA[:, 0:512], gav[:, ot, :], cpool[:, 13:14], MAGIC,
                ALU.mult, ALU.add), waits=tuple(ldm), incs=("fin_v",))
            p.op("v", lambda e: e.tensor_scalar(statA[:, 0:512], statA[:, 0:512],
                                                MAGIC, -32768.0, ALU.subtract,
                                                ALU.max),
                 waits=(p.mark("v"),), incs=("fin_v",))
            p.op("v", lambda e, ot=ot: e.tensor_scalar_min(gav[:, ot, :],
                                                           statA[:, 0:512], 32767.0),
                 waits=(p.mark("v"),), incs=("fin_v",))
        oqm = p.mark("v")
        wdv = wflat.rearrange("p (s x y) -> p s x y", s=2, x=32)
        for mt in range(NDT):
            sd = f"dms{4 + mt % 2}"
            p.op("g", lambda e, mt=mt, wdv=wdv: e.dma_start(
                wdv[:, mt % 2, 0:24, :],
                ctx["wq_ld"][:, mt, :, :].rearrange("a p o -> p a o")),
                waits=(p.mark("t"),), incs=(sd,), dma=True)
            wmk = [(sd, p.semv[sd])]
            bk = bank[2]
            bk.mm([lambda e, it=it, mt=mt, bk=bk, wdv=wdv:
                   e.matmul(bk.ap[:], wdv[:, mt % 2, it, :],
                            gav[:, it, :], start=(it == 0), stop=(it == NIT - 1))
                   for it in range(NIT)],
                  extra_waits=tuple(wmk) + ((oqm,) if mt < 2 else ()))
            bk.read("v", lambda e, mt=mt, ts=ts, bk=bk: e.scalar_tensor_tensor(
                xres[:, mt, ts], bk.ap[:], cpool[:, 12:13], xres[:, mt, ts],
                ALU.mult, ALU.add), extra_waits=(p.mark("v"),))
    p.op("v", lambda e: e.drain(), incs=("fin_v",))
    return p.mark("v")


def _sqrt_col(ctx, ap, _tag):
    p = ctx["p"]
    p.op("v", lambda e, ap=ap: e.tensor_scalar_max(ap, ap, 1e-24),
         waits=(p.mark("v"),), incs=("fin_v",))
    vq = p.mark("v")
    p.op("s", lambda e, ap=ap: e.activation(ap, ap, AF.Ln, bias=0.0, scale=1.0),
         waits=(vq,), incs=("fin_s",))
    p.op("s", lambda e, ap=ap: e.activation(ap, ap, AF.Exp, bias=0.0, scale=0.5),
         waits=(p.mark("s"),), incs=("fin_s",))
    sm = p.mark("s")
    p.op("v", lambda e: e.drain(), waits=(sm,), incs=("fin_v",))


def _lpbq_out_ap(dst, ot, ic):
    nt = dst.shape[1]
    base = ((ic * 4) * nt + ot) * 128 * 128
    return bass.AP(dst.tensor, base, [[128, 128], [nt * 128 * 128, 4], [1, 128]])


def _allreduce_max(ctx, cols):
    nc, p = ctx["nc"], ctx["p"]
    lcol, aux = ctx["lcol"], ctx["aux"]
    tbank, bank = ctx["tbank"], ctx["bank"]
    ident, ones_row = ctx["ident"], ctx["ones_row"]
    n = len(cols)
    for j, colap in enumerate(cols):
        p.op("v", lambda e, j=j, colap=colap: e.tensor_copy(aux[:, j:j + 1], colap),
             waits=(p.mark("v"), p.mark("s")), incs=("fin_v",))
    p.op("v", lambda e: e.memset(aux[0:1, 4:8], 0.0),
         waits=(p.mark("v"),), incs=("fin_v",))
    cm = p.mark("v")
    tbank.mm([lambda e, j=j: e.transpose(
        tbank.ap[0:1, j * 128:(j + 1) * 128], aux[:, j:j + 1], ident[:])
        for j in range(n)], extra_waits=(cm,))
    for j in range(n):
        tbank.read("v", lambda e, j=j: e.tensor_reduce(
            col2d(aux[0:1, 4 + j:5 + j]), tbank.ap[0:1, j * 128:(j + 1) * 128],
            axis=AXX, op=ALU.max))
    p.op("v", lambda e: e.drain(), incs=("fin_v",))
    rm = p.mark("v")
    p.op("y", lambda e: e.dma_start(ctx["cc_in"][0:1, 0:4], aux[0:1, 4:8]),
         waits=(rm, ("dma", p.semv["dma"])), incs=("dma",), dma=True)
    dm = [("dma", p.semv["dma"])]
    p.op("g", lambda e: e.collective_compute(
        "AllReduce", ALU.max, replica_groups=[list(range(N_CORES))],
        ins=[ctx["cc_in"][0:1, 0:4]], outs=[ctx["cc_out"][0:1, 0:4]]),
        waits=dm, incs=("fin_g",))
    gm = p.mark("g")
    p.op("y", lambda e: e.dma_start(aux[0:1, 8:12], ctx["cc_out"][0:1, 0:4]),
         waits=(gm, ("dma", p.semv["dma"])), incs=("dma",), dma=True)
    dm2 = [("dma", p.semv["dma"])]
    bk = bank[2]
    bk.mm([lambda e, n=n, bk=bk: e.matmul(bk.ap[:, 0:n], ones_row[0:1, 0:128],
                                          aux[0:1, 8:8 + n], start=True, stop=True)],
          extra_waits=tuple(dm2))
    bk.read("v", lambda e, n=n, bk=bk: e.tensor_copy(lcol[:, 64:64 + n],
                                                     bk.ap[:, 0:n]))
    p.op("v", lambda e: e.drain(), incs=("fin_v",))
    return p.mark("v")


def emit_l1_attn_b(ctx, idx):
    nc, p, X = ctx["nc"], ctx["p"], ctx["X"]
    xloc = ctx["xn"]
    bigA, bigB = ctx["bigA"], ctx["bigB"]
    aux = ctx["aux"]
    wbuf, small, smf = ctx["wbuf"], ctx["small"], ctx["smf"]
    lcol = ctx["lcol"]
    xres = ctx["xres"]
    ident, ones_row = ctx["ident"], ctx["ones_row"]
    bank, dbank, tbank = ctx["bank"], ctx["dbank"], ctx["tbank"]
    DI = 2048
    NDI = DI // 128

    if "xa_scr" not in ctx:
        ctx["xa_scr"] = nc.dram_tensor("xa_scr", [NDI, 128, T], BF16).ap()
        ctx["y_scr"] = nc.dram_tensor("y_scr", [NDI, 128, T], BF16).ap()
    xa_scr, y_scr = ctx["xa_scr"], ctx["y_scr"]

    lnm = emit_layernorm(ctx)
    for i in range(NDT):
        p.op("v", lambda e, i=i: e.tensor_add(xres[:, i, :], xres[:, i, :],
                                              xloc[:, i, :]),
             waits=(lnm,) if i == 0 else (), incs=("fin_v",) if i == NDT - 1 else ())
    p.op("v", lambda e: e.drain(), incs=("fin_v",))

    wflat = wbuf[:].rearrange("p a t -> p (a t)")
    selnB = wflat[0:16, 4096:6144]
    selnC = wflat[32:48, 4096:6144]
    wdtT = wflat[64:128, 6144:8192]
    p.op("g", lambda e: e.dma_start(wflat[0:48, 4096:6144], X["c_seln"]),
         waits=(p.mark("t"),), incs=("dmag",), dma=True)
    p.op("v", lambda e: e.drain(), waits=[("dmag", p.semv["dmag"])],
         incs=("fin_v",))

    xdbl = small[:, 0:T]
    xiw = small[:, 2048:2568]
    xaw = small[:, 2568:3080]
    zsw = small[:, 3080:3592]
    carry = smf[:, 0:16]

    for L in range(5):
        lm = p.mark("v")
        p.op("y", lambda e, L=L: e.dma_start(lcol[:, 0:256],
                                             X["L1ab_A_pp"][L]),
             waits=(lm,), incs=("dma",), dma=True)
        p.op("y", lambda e, L=L: e.dma_start(lcol[:, 256:368],
                                             X["L1ab_cols"][L]),
             incs=("dma",), dma=True)
        dm = [("dma", p.semv["dma"])]
        p.op("s", lambda e: e.activation(lcol[:, 0:256], lcol[:, 0:256], AF.Exp,
                                         bias=0.0, scale=1.0),
             waits=dm, incs=("fin_s",))
        sm = p.mark("s")
        p.op("v", lambda e: e.tensor_scalar_mul(lcol[:, 0:256], lcol[:, 0:256], -1.0),
             waits=(sm,), incs=("fin_v",))
        acm = p.mark("v")
        p.op("g", lambda e, L=L: e.dma_start(wdtT, X["L1ab_wdtT"][L]),
             waits=(p.mark("t"),), incs=("dms7",), dma=True)
        wdtm = [("dms7", p.semv["dms7"])]

        xdbk = [dbank[0], dbank[0], dbank[1], dbank[1]]
        for dt in range(NDI):
            slot = dt % 2
            p.op("g", lambda e, L=L, dt=dt, slot=slot: e.dma_start(
                wflat[:, slot * 2048:slot * 2048 + 1024]
                .rearrange("p (a b) -> p a b", a=8),
                X["L1ab_win"][L, dt].rearrange("a p o -> p a o")),
                waits=(p.mark("t"),), incs=(f"dms{slot}",), dma=True)
            wm = [(f"dms{slot}", p.semv[f"dms{slot}"])]
            dgm0 = p.mark("v")
            for j in range(4):
                p.op("v", lambda e, j=j, dt=dt: e.tensor_tensor(
                    aux[:, j * 128:(j + 1) * 128], ident[:],
                    cbc(lcol[:, 304 + dt * 4 + j:304 + dt * 4 + j + 1], 128),
                    ALU.mult),
                    waits=(dgm0, acm) if j == 0 else (), incs=("fin_v",) if j == 3 else ())
            dgm = p.mark("v")
            if dt == 0:
                p.op("v", lambda e: e.memset(xiw[:, 0:4], 0.0), incs=("fin_v",))
            for c4 in range(4):
                ts = slice(c4 * 512, (c4 + 1) * 512)
                bk = bank[c4 % 2]
                bk.mm([lambda e, i=i, L=L, dt=dt, ts=ts, bk=bk, slot=slot:
                       e.matmul(bk.ap[:], wflat[:, slot * 2048 + i * 128:slot * 2048 + (i + 1) * 128],
                                xloc[:, i, ts], start=(i == 0), stop=(i == NDT - 1))
                       for i in range(NDT)],
                      extra_waits=tuple(wm) if c4 == 0 else ())
                bk.read("v", lambda e, bk=bk: e.tensor_copy(xiw[:, 4:516], bk.ap[:]),
                        extra_waits=(p.mark("v"), p.mark("t")))
                xim = p.mark("v")
                cb2 = bank[2]
                cb2.mm([lambda e, j=j, cb2=cb2: e.matmul(
                    cb2.ap[:], aux[:, j * 128:(j + 1) * 128],
                    xiw[:, 1 + j:513 + j], start=(j == 0), stop=(j == 3))
                    for j in range(4)], extra_waits=(xim, dgm))
                cb2.read("s", lambda e, dt=dt, cb2=cb2: e.activation(
                    xaw[:], cb2.ap[:], AF.Silu,
                    bias=lcol[:, 272 + dt:273 + dt], scale=1.0),
                    extra_waits=(p.mark("v"), ("dms2", p.semv["dms2"]),
                                 ("dms3", p.semv["dms3"])))
                ssm = p.mark("s")
                p.op("v", lambda e: e.tensor_copy(xiw[:, 0:4], xiw[:, 512:516]),
                     waits=(p.mark("s"),), incs=("fin_v",))
                p.op("g", lambda e, L=L, dt=dt, ts=ts: e.dma_start(
                    xa_scr[dt, :, ts], xaw[:]),
                    waits=(ssm,), incs=("dms2",), dma=True)
                xbk = xdbk[c4]
                half = (c4 % 2) * 512
                p.op("g", lambda e, L=L, dt=dt: e.dma_start(
                    aux[:, 512:640], X["L1ab_wx"][L, dt]),
                    waits=(p.mark("t"),), incs=("dms3",), dma=True)
                wxm = [("dms3", p.semv["dms3"]), ssm]
                xbk.mm([lambda e, xbk=xbk, half=half, dt=dt: e.matmul(
                    xbk.ap[:, half:half + 512], aux[:, 512:640], xaw[:],
                    start=(dt == 0), stop=(dt == NDI - 1))],
                    extra_waits=tuple(wxm))
        for c4 in range(4):
            xbk = xdbk[c4]
            half = (c4 % 2) * 512
            xbk.read("v", lambda e, xbk=xbk, c4=c4, half=half: e.tensor_copy(
                xdbl[:, c4 * 512:(c4 + 1) * 512], xbk.ap[:, half:half + 512]))
        p.op("v", lambda e: e.drain(), incs=("fin_v",))
        xdm = p.mark("v")

        p.op("v", lambda e: e.drain(), waits=[("dms2", p.semv["dms2"])],
             incs=("fin_v",))
        for ch in range(2):
            tsl = slice(ch * 1024, (ch + 1) * 1024)
            brep = bigB[:].rearrange("p a t -> p (a t)") \
                .rearrange("p (n t) -> p n t", n=16)
            crep = bigA[:].rearrange("p a t -> p (a t)") \
                .rearrange("p (n t) -> p n t", n=16)
            for which, rep, rbase in (("b", brep, 0), ("c", crep, 32)):
                seln = selnB if which == "b" else selnC
                for n in range(16):
                    bk = bank[2]
                    bk.mm([lambda e, n=n, rbase=rbase, tsl=tsl, bk=bk, seln=seln:
                           e.matmul(bk.ap[:], seln[:, n * 128:(n + 1) * 128],
                                    xdbl[rbase:rbase + 16, tsl.start:tsl.start + 512],
                                    start=True, stop=True)],
                        extra_waits=(xdm,) if n == 0 else ())
                    bk.read("v", lambda e, n=n, rep=rep, bk=bk: e.tensor_copy(
                        rep[:, n, 0:512], bk.ap[:]))
                    bk2 = bank[1]
                    bk2.mm([lambda e, n=n, rbase=rbase, tsl=tsl, bk2=bk2, seln=seln:
                            e.matmul(bk2.ap[:], seln[:, n * 128:(n + 1) * 128],
                                     xdbl[rbase:rbase + 16, tsl.start + 512:tsl.start + 1024],
                                     start=True, stop=True)],
                        extra_waits=(xdm,) if n == 0 else ())
                    bk2.read("v", lambda e, n=n, rep=rep, bk2=bk2: e.tensor_copy(
                        rep[:, n, 512:1024], bk2.ap[:]))
            p.op("v", lambda e: e.drain(), incs=("fin_v",))
            bcm = p.mark("v")

            hbuf = aux[:, 0:1024]
            hcbuf = aux[:, 1024:2048]
            dcol = aux[:, 2048:3072]
            ucol = aux[:, 3072:4096]
            xat = aux[:, 4096:5120]
            dan = aux[:, 5120:6144]
            for dt in range(NDI):
                bkd = dbank[0]
                bkd.mm([lambda e, dt=dt, tsl=tsl, s2=s2, bkd=bkd: e.matmul(
                    bkd.ap[:, s2 * 512:(s2 + 1) * 512],
                    wdtT[:, dt * 128:(dt + 1) * 128],
                    xdbl[64:128, tsl.start + s2 * 512:tsl.start + (s2 + 1) * 512],
                    start=True, stop=True)
                    for s2 in range(2)],
                    extra_waits=tuple(wdtm) + ((xdm,) if dt == 0 and ch == 0 else ()))
                bkd.read("v", lambda e, dt=dt, bkd=bkd: e.tensor_scalar_add(
                    ctx["statA"][:], bkd.ap[0:128, 0:1024], lcol[:, 256 + dt:257 + dt]),
                    extra_waits=(p.mark("v"),))
                p.op("v", lambda e: e.tensor_scalar_mul(ctx["statB"][:], ctx["statA"][:],
                                                        -1.0),
                     waits=(p.mark("v"),), incs=("fin_v",))
                p.op("v", lambda e: e.tensor_max(ctx["statB"][:], ctx["statB"][:],
                                                 ctx["statA"][:]),
                     waits=(p.mark("v"),), incs=("fin_v",))
                vsp = p.mark("v")
                p.op("s", lambda e: e.activation(ctx["statB"][:], ctx["statB"][:],
                                                 AF.Exp, bias=0.0, scale=-1.0),
                     waits=(vsp,), incs=("fin_s",))
                p.op("s", lambda e: e.activation(ctx["statB"][:], ctx["statB"][:],
                                                 AF.Ln, bias=1.0, scale=1.0),
                     waits=(p.mark("s"),), incs=("fin_s",))
                ssp = p.mark("s")
                p.op("v", lambda e: e.scalar_tensor_tensor(
                    dcol, ctx["statA"][:], 0.0, ctx["statB"][:], ALU.max, ALU.add),
                     waits=(ssp,), incs=("fin_v",))
                ddm = p.mark("v")
                p.op("g", lambda e, dt=dt, tsl=tsl: e.dma_start(xat, xa_scr[dt, :, tsl]),
                     waits=(p.mark("v"),), incs=("dms4",), dma=True)
                xam = [("dms4", p.semv["dms4"])]
                p.op("v", lambda e: e.tensor_tensor(ucol, dcol, xat, ALU.mult),
                     waits=tuple(xam) + (ddm,), incs=("fin_v",))
                um = p.mark("v")
                ybk = dbank[1]
                for n in range(16):
                    p.op("s", lambda e, n=n, dt=dt: e.activation(
                        dan, dcol, AF.Exp, bias=0.0,
                        scale=lcol[:, dt * 16 + n:dt * 16 + n + 1]),
                        waits=(ddm, acm, p.mark("v"),
                               ("dms5", p.semv["dms5"])), incs=("fin_s",))
                    dam = p.mark("s")
                    p.op("v", lambda e, n=n: e.tensor_tensor(
                        hcbuf, ucol, brep[:, n, :], ALU.mult),
                        waits=(um, bcm, p.mark("t"), p.mark("v")), incs=("fin_v",))
                    with nc.allow_low_precision(reason="scan state is fp32 internal; bf16 io"):
                        p.op("v", lambda e, n=n, ch=ch: e.tensor_tensor_scan(
                            hbuf, dan, hcbuf,
                            0.0 if ch == 0 else carry[:, n:n + 1],
                            ALU.mult, ALU.add),
                            waits=(dam, p.mark("v")),
                            incs=("fin_v",))
                    p.op("v", lambda e, n=n: e.tensor_copy(carry[:, n:n + 1],
                                                           hbuf[:, 1023:1024]),
                         waits=(p.mark("v"),), incs=("fin_v",))
                    p.op("v", lambda e, n=n: e.tensor_tensor(hcbuf, hbuf,
                                                             crep[:, n, :], ALU.mult),
                         waits=(p.mark("v"),), incs=("fin_v",))
                    hcm = p.mark("v")
                    ybk.mm([lambda e, n=n, s2=s2, ybk=ybk: e.matmul(
                        ybk.ap[:, s2 * 512:(s2 + 1) * 512], ident[:],
                        hcbuf[:, s2 * 512:(s2 + 1) * 512],
                        start=(n == 0), stop=(n == 15))
                        for s2 in range(2)], extra_waits=(hcm,))
                ybk.read("v", lambda e, dt=dt, ybk=ybk: e.scalar_tensor_tensor(
                    dan, xat, lcol[:, 288 + dt:289 + dt], ybk.ap[:],
                    ALU.mult, ALU.add),
                    extra_waits=(p.mark("v"), p.mark("s")))
                yem = p.mark("v")
                p.op("g", lambda e, dt=dt, tsl=tsl: e.dma_start(
                    y_scr[dt, :, tsl], dan),
                    waits=(yem,), incs=("dms5",), dma=True)
        p.op("v", lambda e: e.drain(), waits=[("dms5", p.semv["dms5"])],
             incs=("fin_v",))

        ygv = bigB[:].rearrange("p a t -> p (a t)")[:, 0:8192] \
            .rearrange("p (a t) -> p a t", a=16)
        ytile = small[:, 2048:2560]
        for c4 in range(4):
            ts = slice(c4 * 512, (c4 + 1) * 512)
            for dt in range(NDI):
                slot = dt % 2
                p.op("g", lambda e, L=L, dt=dt, slot=slot: e.dma_start(
                    wflat[:, slot * 2048:slot * 2048 + 1024]
                    .rearrange("p (a b) -> p a b", a=8),
                    X["L1ab_win"][L, 16 + dt].rearrange("a p o -> p a o")),
                    waits=(p.mark("t"),), incs=(f"dms{slot}",), dma=True)
                wm = [(f"dms{slot}", p.semv[f"dms{slot}"])]
                bk = bank[c4 % 2]
                bk.mm([lambda e, i=i, ts=ts, bk=bk, slot=slot:
                       e.matmul(bk.ap[:], wflat[:, slot * 2048 + i * 128:slot * 2048 + (i + 1) * 128],
                                xloc[:, i, ts], start=(i == 0), stop=(i == NDT - 1))
                       for i in range(NDT)], extra_waits=tuple(wm))
                bk.read("s", lambda e, bk=bk: e.activation(zsw, bk.ap[:], AF.Silu,
                                                           bias=0.0, scale=1.0),
                        extra_waits=(p.mark("v"),))
                zm = p.mark("s")
                p.op("g", lambda e, dt=dt, ts=ts: e.dma_start(
                    ytile, y_scr[dt, :, ts]),
                    waits=(p.mark("v"),), incs=("dms6",), dma=True)
                ym = [("dms6", p.semv["dms6"])]
                p.op("v", lambda e, dt=dt: e.tensor_tensor(ygv[:, dt, :],
                                                           ytile, zsw,
                                                           ALU.mult),
                     waits=tuple(ym) + (zm,), incs=("fin_v",))
            ygm = p.mark("v")
            for mt in range(NDT):
                slot = mt % 2
                p.op("g", lambda e, L=L, mt=mt, slot=slot: e.dma_start(
                    wflat[:, slot * 2048:slot * 2048 + 2048]
                    .rearrange("p (a b) -> p a b", a=16),
                    X["L1ab_wout"][L, mt].rearrange("a p o -> p a o")),
                    waits=(p.mark("t"),), incs=(f"dms{slot + 2}",), dma=True)
                wm = [(f"dms{slot + 2}", p.semv[f"dms{slot + 2}"])]
                bk = bank[2]
                bk.mm([lambda e, dt=dt2, mt=mt, slot=slot, bk=bk:
                       e.matmul(bk.ap[:], wflat[:, slot * 2048 + dt * 128:slot * 2048 + (dt + 1) * 128],
                                ygv[:, dt, :], start=(dt == 0), stop=(dt == NDI - 1))
                       for dt2 in range(NDI)],
                      extra_waits=tuple(wm) + (ygm,))
                bk.read("v", lambda e, mt=mt, ts=ts, bk=bk: e.tensor_add(
                    xres[:, mt, ts], xres[:, mt, ts], bk.ap[:]),
                    extra_waits=(p.mark("v"),))
                p.op("v", lambda e, mt=mt, ts=ts: e.tensor_add(
                    xloc[:, mt, ts], xloc[:, mt, ts], bank[2].ap[:]),
                     waits=(p.mark("v"),), incs=("fin_v",))
                bank[2].rmarks.append(("fin_v", p.semv["fin_v"]))
        p.op("v", lambda e: e.drain(), incs=("fin_v",))
    return p.mark("v")



def prep_inputs(inputs, core):
    d = dict(host_constants())
    x = np.asarray(inputs["L0_attn_a_x"])[core]
    d["x_f"] = np.ascontiguousarray(x.T)
    d["L1_attn_a_WqT"] = np.ascontiguousarray(np.asarray(inputs["L1_attn_a_Wq"]).T)
    d["L1_attn_a_WkT"] = np.ascontiguousarray(np.asarray(inputs["L1_attn_a_Wk"]).T)
    d["L1_attn_a_WvT"] = np.ascontiguousarray(np.asarray(inputs["L1_attn_a_Wv"]).T)
    d["L0_ffn_w_gate_T"] = np.ascontiguousarray(np.asarray(inputs["L0_ffn_w_gate"]).T)
    d["L0_ffn_w_up_T"] = np.ascontiguousarray(np.asarray(inputs["L0_ffn_w_up"]).T)
    d["L0_ffn_w_down_T"] = np.ascontiguousarray(np.asarray(inputs["L0_ffn_w_down"]).T)
    perm = np.zeros(1024, dtype=np.int64)
    for par in range(2):
        for h in range(16):
            for i in range(32):
                op = (par * 4 + h // 4) * 128 + (h % 4) * 32 + i
                perm[op] = h * 64 + 2 * i + par
    d["L0_attn_b_WqT_perm"] = np.ascontiguousarray(
        np.asarray(inputs["L0_attn_b_Wq"]).T[:, perm])
    d["L0_attn_b_WkT_perm"] = np.ascontiguousarray(
        np.asarray(inputs["L0_attn_b_Wk"]).T[:, perm])
    d["L0_attn_b_WvT"] = np.ascontiguousarray(np.asarray(inputs["L0_attn_b_Wv"]).T)
    d["L0_attn_b_WoT"] = np.ascontiguousarray(np.asarray(inputs["L0_attn_b_Wo"]).T)
    WXP = np.asarray(inputs["L0_attn_a_x_proj_w"])
    pad = np.zeros((256, 1024), dtype=np.float32)
    pad[0:64] = WXP[64:128]
    pad[64:128] = WXP[0:64]
    pad[128:192] = WXP[128:192]
    d["L0aa_wxpT"] = np.ascontiguousarray(pad.T)
    d["L0aa_wdtT"] = np.ascontiguousarray(np.asarray(inputs["L0_attn_a_dt_proj_w"]).T)
    WOo = np.asarray(inputs["L0_attn_a_out_proj_w"])
    d["L0aa_wo"] = np.ascontiguousarray(WOo.T.reshape(8, 128, 8, 128)
                                        .transpose(2, 0, 1, 3))
    AL0 = np.asarray(inputs["L0_attn_a_A_log"])
    app = np.zeros((128, 512), dtype=np.float32)
    for kk in range(512):
        dtile, m = kk // 64, kk % 64
        for pp in range(128):
            dd = dtile * 128 + 2 * m + pp // 64
            nn2 = pp % 64
            app[pp, kk] = AL0[dd, nn2]
    d["L0aa_A_pp"] = app
    cols0 = np.zeros((128, 24), dtype=np.float32)
    cols0[:, 0:8] = np.asarray(inputs["L0_attn_a_norm_w"]).reshape(8, 128).T
    cols0[:, 8:16] = np.asarray(inputs["L0_attn_a_dt_proj_b"]).reshape(8, 128).T
    cols0[:, 16:24] = np.asarray(inputs["L0_attn_a_D"]).reshape(8, 128).T
    d["L0aa_cols"] = cols0
    seld = np.zeros((128, 64 * 128), dtype=np.float32)
    for m in range(64):
        for pp in range(128):
            seld[2 * m + pp // 64, m * 128 + pp] = 1.0
    d["c_seld"] = seld
    seln0 = np.zeros((128, 128), dtype=np.float32)
    for pp in range(128):
        seln0[pp % 64, pp] = 1.0
    d["c_seln0"] = seln0
    WI = np.asarray(inputs["L1_attn_b_W_in"])
    d["L1ab_win"] = np.ascontiguousarray(
        WI.transpose(0, 2, 1).reshape(5, 8, 128, 32, 128).transpose(0, 3, 1, 2, 4))
    WO = np.asarray(inputs["L1_attn_b_W_out"])
    d["L1ab_wout"] = np.ascontiguousarray(
        WO.transpose(0, 2, 1).reshape(5, 16, 128, 8, 128).transpose(0, 3, 1, 2, 4))
    WX = np.asarray(inputs["L1_attn_b_W_x"])
    wxp = np.zeros((5, 128, 2048), dtype=np.float32)
    wxp[:, 0:16] = WX[:, 64:80]
    wxp[:, 32:48] = WX[:, 80:96]
    wxp[:, 64:128] = WX[:, 0:64]
    d["L1ab_wx"] = np.ascontiguousarray(
        wxp.transpose(0, 2, 1).reshape(5, 16, 128, 128))
    d["L1ab_wdtT"] = np.ascontiguousarray(
        np.asarray(inputs["L1_attn_b_W_dt"]).transpose(0, 2, 1))
    AL = np.asarray(inputs["L1_attn_b_A_log"])
    d["L1ab_A_pp"] = np.ascontiguousarray(AL.reshape(5, 16, 128, 16)
                                          .transpose(0, 2, 1, 3).reshape(5, 128, 256))
    cols = np.zeros((5, 128, 112), dtype=np.float32)
    cols[:, :, 0:16] = np.asarray(inputs["L1_attn_b_b_dt"]).reshape(5, 16, 128).transpose(0, 2, 1)
    cols[:, :, 16:32] = np.asarray(inputs["L1_attn_b_conv_b"]).reshape(5, 16, 128).transpose(0, 2, 1)
    cols[:, :, 32:48] = np.asarray(inputs["L1_attn_b_D"]).reshape(5, 16, 128).transpose(0, 2, 1)
    CW = np.asarray(inputs["L1_attn_b_conv_w"]).reshape(5, 16, 128, 4)
    cols[:, :, 48:112] = CW.transpose(0, 2, 1, 3).reshape(5, 128, 64)
    d["L1ab_cols"] = np.ascontiguousarray(cols)
    d["L1_ffn_w_gate"] = np.asarray(inputs["L1_ffn_w_gate"])
    d["L1_ffn_w_up"] = np.asarray(inputs["L1_ffn_w_up"])
    d["L1_ffn_w_down"] = np.asarray(inputs["L1_ffn_w_down"])
    d["L1_attn_a_b_cat"] = np.concatenate([
        np.asarray(inputs["L1_attn_a_bq"]), np.asarray(inputs["L1_attn_a_bk"]),
        np.asarray(inputs["L1_attn_a_bv"])]).reshape(1, 3 * D).astype(np.float32)
    return d


SUBLAYERS = tuple(range(8))


def kernel(**inputs):
    per_core = [prep_inputs(inputs, c) for c in range(8)]
    nc = build_nc(per_core[0], SUBLAYERS)
    res = run_bass_kernel_spmd(nc, per_core, core_ids=list(range(8)))
    out = np.stack([r["out"].T for r in res.results], axis=0)
    return out.astype(np.float32)


# version 02
# speedup vs baseline: 1.2583x; 1.2583x over previous
import contextlib
import numpy as np
import ml_dtypes

import concourse.bass as bass
import concourse.mybir as mybir
from concourse.bass_utils import run_bass_kernel_spmd

F32 = mybir.dt.float32
BF16 = mybir.dt.bfloat16
AF = mybir.ActivationFunctionType
ALU = mybir.AluOpType
AXX = mybir.AxisListType.X

BF = ml_dtypes.bfloat16

B = 8
T = 2048
D = 1024
NDT = D // 128
NTT = T // 128
NTC = T // 512

MAGIC = float(3 << 22)
N_CORES = 8


def _ins(ap, axis, step, count):
    nap = [list(d) for d in ap.ap]
    nap.insert(axis, [step, count])
    return bass.AP(ap.tensor, ap.offset, nap)


def bcast(ap, axis, count):
    return _ins(ap, axis, 0, count)


def cbc(col_ap, n):
    return bass.AP(col_ap.tensor, col_ap.offset, [list(col_ap.ap[0]), [0, n]])


def col2d(ap):
    return _ins(ap, 2, 1, 1)



class Prog:

    ENGS = ("t", "v", "s", "g", "y")

    def __init__(self, nc, es):
        self.nc = nc
        self.es = es
        self.pend = {e: [] for e in self.ENGS}
        self.sems = {}
        self.semv = {}
        for e in self.ENGS:
            self.new_sem("fin_" + e)
        self.new_sem("dma")
        self.new_sem("dmag")
        self.new_sem("dmaw")
        for i in range(8):
            self.new_sem(f"dms{i}")
        for i in range(4):
            self.new_sem(f"dmy{i}")
        self.ninstr = 0
        self.tag = "init"

    def new_sem(self, name):
        self.sems[name] = self.es.enter_context(self.nc.semaphore(name))
        self.semv[name] = 0
        return name

    def mark(self, eng):
        return ("fin_" + eng, self.semv["fin_" + eng])

    def op(self, eng, emit, waits=(), incs=(), dma=False):
        wl = [(self.sems[w[0]], w[1]) for w in waits]
        il = []
        for sname in incs:
            n = 16 if dma else 1
            self.semv[sname] += n
            il.append((self.sems[sname], n))
        self.ninstr += 1

        tag = f"{self.tag}.{eng}" + ("d" if dma else "")

        def run(e, emit=emit, wl=wl, il=il, tag=tag):
            with self.nc.named_scope(tag):
                for h, v in wl:
                    if v > 0:
                        e.wait_ge(h, v)
                inst = emit(e)
                for h, n in il:
                    inst.then_inc(h, n)

        self.pend[eng].append(run)

    def barrier(self, producers=ENGS, consumers=ENGS):
        marks = []
        for e in producers:
            sname = "fin_" + e
            self.op(e, lambda en: en.drain(), incs=(sname,))
            marks.append((sname, self.semv[sname]))
        for e in consumers:
            self.op(e, lambda en: en.nop(), waits=marks)

    def flush(self, block):
        for e, attr in (("t", "tensor"), ("v", "vector"), ("s", "scalar"),
                        ("g", "gpsimd"), ("y", "sync")):
            ops = self.pend[e]
            if not ops:
                continue
            self.pend[e] = []

            def runall(eng, ops=ops):
                for f in ops:
                    f(eng)

            getattr(block, attr)(runall)


class Bank:

    def __init__(self, prog, ap):
        self.p = prog
        self.ap = ap
        self.wmark = None
        self.rmarks = []

    def mm(self, emits, extra_waits=()):
        p = self.p
        n = len(emits)
        for k, em in enumerate(emits):
            waits = tuple(self.rmarks) + tuple(extra_waits) if k == 0 else ()
            incs = ("fin_t",) if k == n - 1 else ()
            p.op("t", em, waits=waits, incs=incs)
        self.rmarks = []
        self.wmark = ("fin_t", p.semv["fin_t"])

    def read(self, eng, emit, extra_waits=(), incs_extra=()):
        p = self.p
        waits = ((self.wmark,) if self.wmark else ()) + tuple(extra_waits)
        sname = "fin_" + eng
        p.op(eng, emit, waits=waits, incs=(sname,) + tuple(incs_extra))
        self.rmarks.append((sname, p.semv[sname]))



def host_constants():
    c = {}
    c["c_ones_row"] = np.ones((1, 512), dtype=np.float32)
    c["c_ones_col"] = np.ones((128, 128), dtype=np.float32)
    c["c_ident"] = np.eye(128, dtype=np.float32)
    dk = 64
    inv_freq = 1.0 / (10000.0 ** (np.arange(0, dk, 2, dtype=np.float64) / dk))
    pos = np.arange(T, dtype=np.float64)
    fr = pos[:, None] * inv_freq[None, :]
    idx = np.arange(128) % 32
    c["c_rope_cos"] = np.ascontiguousarray(np.cos(fr).T[idx]).astype(np.float32)
    c["c_rope_sin"] = np.ascontiguousarray(np.sin(fr).T[idx]).astype(np.float32)
    tril = np.where(np.tril(np.ones((128, 128), dtype=bool)), 0.0, -1e9)
    c["c_trilm"] = tril.astype(np.float32)
    seln = np.zeros((48, 2048), dtype=np.float32)
    for n in range(16):
        seln[n, n * 128:(n + 1) * 128] = 1.0
        seln[32 + n, n * 128:(n + 1) * 128] = 1.0
    c["c_seln"] = seln
    return c



def build_nc(arrs, sublayers):
    nc = bass.Bass("TRN2", target_bir_lowering=False, debug=False)
    es = contextlib.ExitStack()
    p = Prog(nc, es)

    X = {}
    for name, arr in arrs.items():
        dt = F32 if arr.dtype == np.float32 else BF16
        X[name] = nc.dram_tensor(name, list(arr.shape), dt, kind="ExternalInput").ap()
    OUT = nc.dram_tensor("out", [D, T], F32, kind="ExternalOutput").ap()

    with es:
        build_body(nc, p, es, X, OUT, sublayers)
    return nc


def build_body(nc, p, es, X, OUT, sublayers):
    sb = lambda name, shape, dt=BF16: es.enter_context(nc.sbuf_tensor(name, shape, dt))

    ctx = dict(nc=nc, p=p, es=es, X=X, sb=sb)
    ctx["xres"] = sb("xres", [128, NDT, T], F32)
    ctx["bigA"] = sb("bigA", [128, NDT, T], BF16)
    ctx["bigB"] = sb("bigB", [128, NDT, T], BF16)
    ctx["bigC"] = sb("bigC", [128, NDT, T], BF16)
    ctx["xn"] = ctx["bigC"]
    ctx["statA"] = sb("statA", [128, 1024], F32)
    ctx["statB"] = sb("statB", [128, 1024], F32)
    ctx["wbuf"] = sb("wbuf", [128, NDT, 1024], BF16)
    ctx["small"] = sb("small", [128, 4096], BF16)
    ctx["aux"] = sb("aux", [128, 6144], BF16)
    ctx["lcol"] = sb("lcol", [128, 536], F32)
    ctx["smf"] = sb("smf", [128, 16], F32)
    ctx["ones_row"] = sb("ones_row", [1, 512])
    ctx["ones_col"] = sb("ones_col", [128, 128])
    ctx["ident"] = sb("ident", [128, 128])

    ctx["dbank"] = [Bank(p, es.enter_context(nc.psum_tensor(f"db{i}", [128, 1024], F32)))
                    for i in range(2)]
    ctx["bank"] = [Bank(p, es.enter_context(nc.psum_tensor(f"sb{i}", [128, 512], F32)))
                   for i in range(3)]
    ctx["tbank"] = Bank(p, es.enter_context(nc.psum_tensor("tb", [128, 1024], BF16)))

    cpool = sb("cpool", [128, 16], F32)
    ctx["cpool"] = cpool
    for i, vv in enumerate((1e-5, 1e-6, 1.0, 0.0)):
        p.op("v", lambda e, i=i, vv=vv: e.memset(cpool[:, i:i + 1], vv))
        nc.const_aps.register(F32, vv, cpool[:, i:i + 1])
    p.op("v", lambda e: e.drain(), incs=("fin_v",))

    for nm in ("ones_row", "ones_col", "ident"):
        dst = ctx[nm]
        p.op("g", lambda e, dst=dst, nm=nm: e.dma_start(dst[:], X["c_" + nm]),
             incs=("dmag",), dma=True)
    xf = X["x_f"].rearrange("(n q) t -> q n t", q=128)
    p.op("y", lambda e: e.dma_start(ctx["xres"][:], xf), incs=("dma",), dma=True)
    dmam = [("dma", p.semv["dma"]), ("dmag", p.semv["dmag"]), p.mark("v")]
    for e in ("t", "v", "s"):
        p.op(e, lambda en: en.nop(), waits=dmam)
    p.op("v", lambda en: en.nop(), waits=dmam)

    emitters = [emit_l0_attn_a, emit_l0_ffn, emit_l0_attn_b, emit_l0_ffn,
                emit_l1_attn_a, emit_l1_ffn, emit_l1_attn_b, emit_l1_ffn]
    for i in sublayers:
        p.tag = f"sub{i}"
        emitters[i](ctx, i)
        p.barrier(("t", "v", "s", "g"), ("t", "v", "s", "g"))
    p.tag = "fini"

    p.barrier(("v", "t", "s"), ("y",))
    ov = OUT.rearrange("(n q) t -> q n t", q=128)
    p.op("y", lambda e: e.dma_start(ov, ctx["xres"][:]), incs=("dma",), dma=True)
    p.op("y", lambda e: e.nop(), waits=[("dma", p.semv["dma"])])
    with nc.Block() as blk:
        p.flush(blk)



def emit_layernorm(ctx, eps=1e-5):
    p = ctx["p"]
    xres, xn = ctx["xres"], ctx["xn"]
    bigA = ctx["bigA"]
    statA, statB = ctx["statA"], ctx["statB"]
    ones_col = ctx["ones_col"]
    dbank = ctx["dbank"]

    for c in range(2):
        sl = slice(c * 1024, (c + 1) * 1024)
        for i in range(NDT):
            p.op("v", lambda e, i=i, sl=sl: e.tensor_copy(xn[:, i, sl], xres[:, i, sl]),
                 incs=("fin_v",) if i == NDT - 1 else ())
        cm = p.mark("v")
        for i in range(NDT):
            p.op("v", lambda e, i=i, sl=sl: e.tensor_tensor(
                bigA[:, i, sl], xn[:, i, sl], xn[:, i, sl], ALU.mult),
                waits=(cm,) if i == 0 else (),
                incs=("fin_v",) if i == NDT - 1 else ())
        vm = p.mark("v")
        for h, src in ((0, xn), (1, bigA)):
            bk = dbank[h]
            ems = []
            for s2 in range(2):
                for i in range(NDT):
                    ems.append(lambda e, i=i, s2=s2, src=src, c=c, bk=bk:
                               e.matmul(bk.ap[:, s2 * 512:(s2 + 1) * 512], ones_col[:],
                                        src[:, i, c * 1024 + s2 * 512:c * 1024 + (s2 + 1) * 512],
                                        start=(i == 0), stop=(i == NDT - 1)))
            bk.mm(ems, extra_waits=(vm,))
        dbank[0].read("v", lambda e: e.tensor_scalar_mul(statA[:], dbank[0].ap[:], 1.0 / D))
        p.op("v", lambda e: e.scalar_tensor_tensor(statB[:], statA[:], -1.0, statA[:],
                                                   ALU.mult, ALU.mult),
             waits=(p.mark("v"),), incs=("fin_v",))
        dbank[1].read("v", lambda e: e.scalar_tensor_tensor(
            statB[:], dbank[1].ap[:], 1.0 / D, statB[:], ALU.mult, ALU.add),
            extra_waits=(p.mark("v"),))
        vmk = p.mark("v")
        p.op("s", lambda e: e.activation(statB[:], statB[:], AF.Ln, bias=eps, scale=1.0),
             waits=(vmk,), incs=("fin_s",))
        p.op("s", lambda e: e.activation(statB[:], statB[:], AF.Exp, bias=0.0, scale=-0.5),
             waits=(p.mark("s"),), incs=("fin_s",))
        smk = p.mark("s")
        for i in range(NDT):
            p.op("v", lambda e, i=i, sl=sl: e.tensor_sub(xn[:, i, sl], xres[:, i, sl],
                                                         statA[:]),
                 waits=(smk,) if i == 0 else (),
                 incs=("fin_v",) if i == NDT - 1 else ())
        sm2 = p.mark("v")
        for i in range(NDT):
            p.op("v", lambda e, i=i, sl=sl: e.tensor_mul(xn[:, i, sl], xn[:, i, sl],
                                                         statB[:]),
                 waits=(sm2,) if i == 0 else (),
                 incs=("fin_v",) if i == NDT - 1 else ())
    return p.mark("v")


def emit_residual_add(ctx, src_bf):
    p = ctx["p"]
    xres = ctx["xres"]
    for i in range(NDT):
        p.op("v", lambda e, i=i: e.tensor_add(xres[:, i, :], xres[:, i, :], src_bf[:, i, :]),
             incs=("fin_v",) if i == NDT - 1 else ())
    return p.mark("v")



def emit_l1_attn_a(ctx, idx):
    nc, p, X = ctx["nc"], ctx["p"], ctx["X"]
    xn = ctx["xn"]
    k = ctx["bigA"]
    vtok = ctx["bigB"][:].rearrange("p a t -> p (a t)").rearrange("p (x y) -> p x y", x=NTT)
    qtile = ctx["aux"][:, 0:1024].rearrange("p (a b) -> p a b", a=8)
    brow = ctx["aux"][0:1, 1024:2048]
    wbuf = ctx["wbuf"]
    small, smf = ctx["small"], ctx["smf"]
    xres = ctx["xres"]
    dbank, bank, tbank = ctx["dbank"], ctx["bank"], ctx["tbank"]
    ones_row, ident = ctx["ones_row"], ctx["ident"]

    lnm = emit_layernorm(ctx)

    p.op("g", lambda e: e.dma_start(small[0:1, 0:3 * D], X["L1_attn_a_b_cat"]),
         incs=("dmaw",), dma=True)
    p.op("g", lambda e: e.dma_start(brow[0:1, 0:D],
                                    X["L1_attn_a_b_cat"][0:1, 0:D]),
         incs=("dmaw",), dma=True)
    bvrep = ctx["statA"]

    wv = {0: X["L1_attn_a_WqT"].rearrange("(n q) o -> q n o", q=128),
          1: X["L1_attn_a_WkT"].rearrange("(n q) o -> q n o", q=128),
          2: X["L1_attn_a_WvT"].rearrange("(n q) o -> q n o", q=128)}

    p.op("g", lambda e: e.dma_start(wbuf[:], wv[1]), incs=("dmaw",), dma=True)
    wm = [("dmaw", p.semv["dmaw"]), lnm]
    for ot in range(NDT):
        for c in range(2):
            bk = dbank[c]
            sl = slice(c * 1024, (c + 1) * 1024)
            ems = []
            for s2 in range(2):
                for i in range(NDT):
                    ems.append(lambda e, i=i, ot=ot, c=c, s2=s2, bk=bk:
                               e.matmul(bk.ap[:, s2 * 512:(s2 + 1) * 512],
                                        wbuf[:, i, ot * 128:(ot + 1) * 128],
                                        xn[:, i, c * 1024 + s2 * 512:c * 1024 + (s2 + 1) * 512],
                                        start=(i == 0), stop=False))
                ems.append(lambda e, ot=ot, s2=s2, bk=bk:
                           e.matmul(bk.ap[:, s2 * 512:(s2 + 1) * 512],
                                    small[0:1, D + ot * 128:D + (ot + 1) * 128],
                                    ones_row[0:1, 0:512],
                                    start=False, stop=True))
            bk.mm(ems, extra_waits=tuple(wm) if ot == 0 else ())
            bk.read("v", lambda e, ot=ot, sl=sl, bk=bk:
                    e.tensor_copy(k[:, ot, sl], bk.ap[:]))
    p.op("g", lambda e: e.dma_start(wbuf[:], wv[2]), waits=(p.mark("t"),),
         incs=("dmaw",), dma=True)
    wm = [("dmaw", p.semv["dmaw"])]
    dbank[0].mm([lambda e, s2=s2: e.matmul(dbank[0].ap[:, s2 * 512:(s2 + 1) * 512],
                                           ones_row[0:1, 0:128],
                                           small[0:1, 2 * D + s2 * 512:2 * D + (s2 + 1) * 512],
                                           start=True, stop=True)
                 for s2 in range(2)], extra_waits=tuple(wm))
    dbank[0].read("v", lambda e: e.tensor_copy(bvrep[:], dbank[0].ap[:]))
    bvm = p.mark("v")
    for tt in range(NTT):
        for c in range(2):
            bk = bank[c]
            sl = slice(c * 512, (c + 1) * 512)
            bk.mm([lambda e, i=i, tt=tt, sl=sl, bk=bk:
                   e.matmul(bk.ap[:], xn[:, i, tt * 128:(tt + 1) * 128],
                            wbuf[:, i, sl], start=(i == 0), stop=(i == NDT - 1))
                   for i in range(NDT)],
                  extra_waits=tuple(wm) if tt == 0 else ())
            bk.read("v", lambda e, tt=tt, sl=sl, bk=bk:
                    e.tensor_add(vtok[:, tt, sl], bk.ap[:], bvrep[:, sl]),
                    extra_waits=(bvm,))
    p.op("g", lambda e: e.dma_start(wbuf[:], wv[0]), waits=(p.mark("t"),),
         incs=("dmaw",), dma=True)
    wm = [("dmaw", p.semv["dmaw"])]
    p.op("t", lambda e: e.nop(), waits=wm)

    probs = small[:, 0:T]
    pt = small[:, T:2 * T]

    for tq in range(NTT):
        for ot in range(NDT):
            bk = bank[2]
            ems = [lambda e, i=i, ot=ot, tq=tq, bk=bk:
                   e.matmul(bk.ap[:, 0:128], wbuf[:, i, ot * 128:(ot + 1) * 128],
                            xn[:, i, tq * 128:(tq + 1) * 128], start=(i == 0), stop=False)
                   for i in range(NDT)]
            ems.append(lambda e, ot=ot, tq=tq, bk=bk:
                       e.matmul(bk.ap[:, 0:128], brow[0:1, ot * 128:(ot + 1) * 128],
                                ones_row[0:1, 0:128], start=False, stop=True))
            bk.mm(ems)
            bk.read("v", lambda e, ot=ot, bk=bk:
                    e.tensor_copy(qtile[:, ot, :], bk.ap[:, 0:128]))
        p.op("v", lambda e: e.drain(), incs=("fin_v",))
        qm = p.mark("v")
        for c in range(2):
            bk = dbank[c]
            ems = []
            for s2 in range(2):
                for i in range(NDT):
                    ems.append(lambda e, i=i, c=c, s2=s2, bk=bk:
                               e.matmul(bk.ap[:, s2 * 512:(s2 + 1) * 512], qtile[:, i, :],
                                        k[:, i, c * 1024 + s2 * 512:c * 1024 + (s2 + 1) * 512],
                                        start=(i == 0), stop=(i == NDT - 1)))
            bk.mm(ems, extra_waits=(qm,))
        dbank[0].read("v", lambda e: e.tensor_reduce(col2d(smf[:, 0:1]), dbank[0].ap[:],
                                                     axis=AXX, op=ALU.max))
        dbank[1].read("v", lambda e: e.tensor_reduce(col2d(smf[:, 1:2]), dbank[1].ap[:],
                                                     axis=AXX, op=ALU.max))
        p.op("v", lambda e: e.tensor_reduce(col2d(smf[:, 2:3]), smf[:, 0:2], axis=AXX,
                                            op=ALU.max), waits=(p.mark("v"),),
             incs=("fin_v",))
        p.op("v", lambda e: e.tensor_scalar_mul(smf[:, 3:4], smf[:, 2:3], -1.0),
             waits=(p.mark("v"),), incs=("fin_v",))
        vmk = p.mark("v")
        for c in range(2):
            dbank[c].read("s", lambda e, c=c: e.activation(
                probs[:, c * 1024:(c + 1) * 1024], dbank[c].ap[:], AF.Exp,
                bias=smf[:, 3:4], scale=1.0), extra_waits=(vmk,))
        smk = p.mark("s")
        p.op("v", lambda e: e.tensor_reduce(col2d(smf[:, 4:5]), probs[:], axis=AXX,
                                            op=ALU.add), waits=(smk,), incs=("fin_v",))
        p.op("v", lambda e: e.reciprocal(smf[:, 5:6], smf[:, 4:5]),
             waits=(p.mark("v"),), incs=("fin_v",))
        p.op("v", lambda e: e.tensor_scalar_mul(probs[:], probs[:], smf[:, 5:6]),
             waits=(p.mark("v"),), incs=("fin_v",))
        dmk = p.mark("v")
        for half in range(2):
            tbank.mm([lambda e, tk=tk: e.transpose(
                tbank.ap[:, (tk % 8) * 128:(tk % 8 + 1) * 128],
                probs[:, tk * 128:(tk + 1) * 128], ident[:])
                for tk in range(half * 8, (half + 1) * 8)], extra_waits=(dmk,))
            tbank.read("v", lambda e, half=half: e.tensor_copy(
                pt[:, half * 1024:(half + 1) * 1024], tbank.ap[:]))
        ptm = p.mark("v")
        for dv in range(NDT):
            bk = bank[2]
            bk.mm([lambda e, dv=dv, tk=tk, bk=bk:
                   e.matmul(bk.ap[:, 0:128], vtok[:, tk, dv * 128:(dv + 1) * 128],
                            pt[:, tk * 128:(tk + 1) * 128],
                            start=(tk == 0), stop=(tk == NTT - 1))
                   for tk in range(NTT)], extra_waits=(ptm,))
            bk.read("v", lambda e, dv=dv, tq=tq, bk=bk:
                    e.tensor_add(xres[:, dv, tq * 128:(tq + 1) * 128],
                                 xres[:, dv, tq * 128:(tq + 1) * 128], bk.ap[:, 0:128]))
    p.op("v", lambda e: e.drain(), incs=("fin_v",))
    return p.mark("v")



def emit_l0_attn_a(ctx, idx):
    nc, p, X = ctx["nc"], ctx["p"], ctx["X"]
    xn = ctx["xn"]
    xnn = ctx["bigA"]
    dlt = ctx["bigB"]
    wbuf, small, smf = ctx["wbuf"], ctx["small"], ctx["smf"]
    lcol = ctx["lcol"]
    xres = ctx["xres"]
    statA, statB = ctx["statA"], ctx["statB"]
    ones_col, ident = ctx["ones_col"], ctx["ident"]
    bank, dbank, tbank = ctx["bank"], ctx["dbank"], ctx["tbank"]
    wflat = wbuf[:].rearrange("p a t -> p (a t)")

    lnm = emit_layernorm(ctx)

    p.op("y", lambda e: e.dma_start(lcol[:, 0:512], X["L0aa_A_pp"]),
         waits=(lnm,), incs=("dma",), dma=True)
    p.op("y", lambda e: e.dma_start(lcol[:, 512:536], X["L0aa_cols"]),
         incs=("dma",), dma=True)
    dm = [("dma", p.semv["dma"])]
    p.op("s", lambda e: e.activation(lcol[:, 0:512], lcol[:, 0:512], AF.Exp,
                                     bias=0.0, scale=1.0), waits=dm, incs=("fin_s",))
    p.op("v", lambda e: e.tensor_scalar_mul(lcol[:, 0:512], lcol[:, 0:512], -1.0),
         waits=(p.mark("s"),), incs=("fin_v",))
    acm = p.mark("v")

    for c in range(2):
        sl = slice(c * 1024, (c + 1) * 1024)
        for i in range(NDT):
            p.op("v", lambda e, i=i, sl=sl: e.tensor_tensor(
                xnn[:, i, sl], xn[:, i, sl], xn[:, i, sl], ALU.mult),
                waits=(lnm, acm) if i == 0 else (),
                incs=("fin_v",) if i == NDT - 1 else ())
        vm = p.mark("v")
        bkr = dbank[0]
        ems = []
        for s2 in range(2):
            for i in range(NDT):
                ems.append(lambda e, i=i, s2=s2, c=c, bkr=bkr:
                           e.matmul(bkr.ap[:, s2 * 512:(s2 + 1) * 512], ones_col[:],
                                    xnn[:, i, c * 1024 + s2 * 512:c * 1024 + (s2 + 1) * 512],
                                    start=(i == 0), stop=(i == NDT - 1)))
        bkr.mm(ems, extra_waits=(vm,))
        bkr.read("v", lambda e, bkr=bkr: e.tensor_scalar_mul(statB[:], bkr.ap[:],
                                                             1.0 / D),
                 extra_waits=(p.mark("v"),))
        p.op("v", lambda e: e.drain(), incs=("fin_v",))
        vmk = p.mark("v")
        p.op("s", lambda e: e.activation(statB[:], statB[:], AF.Ln, bias=1e-6,
                                         scale=1.0), waits=(vmk,), incs=("fin_s",))
        p.op("s", lambda e: e.activation(statB[:], statB[:], AF.Exp, bias=0.0,
                                         scale=-0.5), waits=(p.mark("s"),),
             incs=("fin_s",))
        smk = p.mark("s")
        for i in range(NDT):
            p.op("v", lambda e, i=i, sl=sl: e.scalar_tensor_tensor(
                xnn[:, i, sl], xn[:, i, sl], lcol[:, 512 + i:513 + i], statB[:],
                ALU.mult, ALU.mult),
                waits=(smk,) if i == 0 else (), incs=("fin_v",))
            p.op("v", lambda e, i=i, sl=sl: e.tensor_scalar_mul(
                xn[:, i, sl], xn[:, i, sl], lcol[:, 528 + i:529 + i]),
                waits=(p.mark("v"),), incs=("fin_v",))
    p.op("v", lambda e: e.drain(), incs=("fin_v",))
    xdm = p.mark("v")

    for mt in range(NDT):
        slot = mt % 2
        p.op("g", lambda e, mt=mt, slot=slot: e.dma_start(
            wflat[:, slot * 1024:slot * 1024 + 1024].rearrange("p (a b) -> p a b", a=8),
            X["L0aa_wo"][mt].rearrange("a p o -> p a o")),
            waits=(p.mark("t"),), incs=(f"dms{slot}",), dma=True)
        wm = [(f"dms{slot}", p.semv[f"dms{slot}"])]
        for c in range(NTC):
            ts = slice(c * 512, (c + 1) * 512)
            bk = bank[c % 2]
            bk.mm([lambda e, i=i, mt=mt, ts=ts, bk=bk, slot=slot:
                   e.matmul(bk.ap[:], wflat[:, slot * 1024 + i * 128:slot * 1024 + (i + 1) * 128],
                            xn[:, i, ts], start=(i == 0), stop=(i == NDT - 1))
                   for i in range(NDT)],
                  extra_waits=tuple(wm) + ((xdm,) if c == 0 else ()))
            bk.read("v", lambda e, mt=mt, ts=ts, bk=bk: e.tensor_add(
                xres[:, mt, ts], xres[:, mt, ts], bk.ap[:]))
    p.op("v", lambda e: e.drain(), incs=("fin_v",))

    T1 = small[:, 0:T]
    T2 = small[:, 2048:2048 + T]
    p.op("g", lambda e: e.dma_start(
        wflat[:, 2048:4096].rearrange("p (a b) -> p a b", a=8),
        X["L0aa_wxpT"].rearrange("(n q) o -> q n o", q=128)),
        waits=(p.mark("t"),), incs=("dms2",), dma=True)
    wm = [("dms2", p.semv["dms2"])]
    for mt2, dst in ((0, T1), (1, T2)):
        for c in range(NTC):
            ts = slice(c * 512, (c + 1) * 512)
            bk = bank[2]
            bk.mm([lambda e, i=i, mt2=mt2, ts=ts, bk=bk:
                   e.matmul(bk.ap[:], wflat[:, 2048 + i * 256 + mt2 * 128:2048 + i * 256 + (mt2 + 1) * 128],
                            xnn[:, i, ts], start=(i == 0), stop=(i == NDT - 1))
                   for i in range(NDT)],
                  extra_waits=tuple(wm) if mt2 == 0 and c == 0 else ())
            bk.read("v", lambda e, dst=dst, ts=ts, bk=bk: e.tensor_copy(
                dst[:, ts], bk.ap[:]), extra_waits=(p.mark("v"),))
    p.op("v", lambda e: e.drain(), incs=("fin_v",))
    prm = p.mark("v")

    p.op("g", lambda e: e.dma_start(wflat[64:128, 0:1024], X["L0aa_wdtT"]),
         waits=(p.mark("t"),), incs=("dms3",), dma=True)
    wm = [("dms3", p.semv["dms3"])]
    for mt in range(NDT):
        for c2 in range(2):
            ts = slice(c2 * 1024, (c2 + 1) * 1024)
            bkd = dbank[1]
            bkd.mm([lambda e, mt=mt, ts=ts, s2=s2, bkd=bkd: e.matmul(
                bkd.ap[:, s2 * 512:(s2 + 1) * 512],
                wflat[64:128, mt * 128:(mt + 1) * 128],
                T1[64:128, ts.start + s2 * 512:ts.start + (s2 + 1) * 512],
                start=True, stop=True)
                for s2 in range(2)],
                extra_waits=tuple(wm) + ((prm,) if mt == 0 and c2 == 0 else ()))
            bkd.read("v", lambda e, mt=mt, bkd=bkd: e.tensor_scalar_add(
                statA[:], bkd.ap[:], lcol[:, 520 + mt:521 + mt]),
                extra_waits=(p.mark("v"),))
            p.op("v", lambda e: e.tensor_scalar_mul(statB[:], statA[:], -1.0),
                 waits=(p.mark("v"),), incs=("fin_v",))
            p.op("v", lambda e: e.tensor_max(statB[:], statB[:], statA[:]),
                 waits=(p.mark("v"),), incs=("fin_v",))
            vsp = p.mark("v")
            p.op("s", lambda e: e.activation(statB[:], statB[:], AF.Exp, bias=0.0,
                                             scale=-1.0), waits=(vsp,), incs=("fin_s",))
            p.op("s", lambda e: e.activation(statB[:], statB[:], AF.Ln, bias=1.0,
                                             scale=1.0), waits=(p.mark("s"),),
                 incs=("fin_s",))
            ssp = p.mark("s")
            p.op("v", lambda e, mt=mt, ts=ts: e.scalar_tensor_tensor(
                dlt[:, mt, ts], statA[:], 0.0, statB[:], ALU.max, ALU.add),
                 waits=(ssp,), incs=("fin_v",))
    p.op("v", lambda e: e.drain(), incs=("fin_v",))
    dlm = p.mark("v")

    seln0 = ctx["aux"][:, 0:128]
    p.op("g", lambda e: e.dma_start(seln0, X["c_seln0"]), incs=("dmag",), dma=True)
    sbm = [("dmag", p.semv["dmag"])]
    cflat = ctx["xn"][:].rearrange("p a t -> p (a t)")
    brep = cflat[:, 0:2048]
    crep = cflat[:, 2048:4096]
    ucol = cflat[:, 4096:6144]
    dac = cflat[:, 6144:8192]
    dbuc = cflat[:, 8192:10240]
    hbuf = cflat[:, 10240:12288]
    hcbuf = cflat[:, 12288:14336]
    selyb = cflat[:, 14336:14464]
    for half, (rep, src) in enumerate(((brep, T1), (crep, T2))):
        for c in range(NTC):
            ts = slice(c * 512, (c + 1) * 512)
            bk = bank[2]
            bk.mm([lambda e, src=src, ts=ts, bk=bk: e.matmul(
                bk.ap[:], seln0, src[:, ts], start=True, stop=True)],
                extra_waits=tuple(sbm) + (dlm,) if half == 0 and c == 0 else ())
            bk.read("v", lambda e, rep=rep, ts=ts, bk=bk: e.tensor_copy(
                rep[:, ts], bk.ap[:]), extra_waits=(p.mark("v"),))
    p.op("v", lambda e: e.drain(), incs=("fin_v",))
    bcm = p.mark("v")

    p.op("g", lambda e: e.dma_start(wflat[:, 0:8192], X["c_seld"]),
         waits=(p.mark("t"),), incs=("dms4",), dma=True)
    sdm = [("dms4", p.semv["dms4"])]

    for dtile in range(NDT):
        p.op("v", lambda e, dtile=dtile: e.tensor_tensor(
            ucol, dlt[:, dtile, :], xnn[:, dtile, :], ALU.mult),
            waits=(bcm, dlm, p.mark("t"), p.mark("v")), incs=("fin_v",))
        um = p.mark("v")
        ybk0, ybk1 = dbank[0], dbank[1]
        for m in range(64):
            k = dtile * 64 + m
            tbank.mm([lambda e, m=m: e.transpose(
                tbank.ap[:, 0:128], wflat[:, m * 128:(m + 1) * 128], ident[:])],
                extra_waits=tuple(sdm) + (p.mark("v"),))
            tbank.read("v", lambda e: e.tensor_copy(selyb, tbank.ap[:, 0:128]),
                       extra_waits=(p.mark("v"),))
            selym = p.mark("v")
            for c in range(NTC):
                ts = slice(c * 512, (c + 1) * 512)
                bka, bku = bank[0], bank[1]
                bka.mm([lambda e, m=m, dtile=dtile, ts=ts, bka=bka: e.matmul(
                    bka.ap[:], wflat[:, m * 128:(m + 1) * 128],
                    dlt[:, dtile, ts], start=True, stop=True)],
                    extra_waits=(um,) if m == 0 and c == 0 else ())
                bka.read("s", lambda e, k=k, ts=ts, bka=bka: e.activation(
                    dac[:, ts], bka.ap[:], AF.Exp, bias=0.0,
                    scale=lcol[:, k:k + 1]), extra_waits=(p.mark("v"),))
                bku.mm([lambda e, m=m, ts=ts, bku=bku: e.matmul(
                    bku.ap[:], wflat[:, m * 128:(m + 1) * 128],
                    ucol[:, ts], start=True, stop=True)],
                    extra_waits=(um,) if m == 0 and c == 0 else ())
                bku.read("v", lambda e, ts=ts, bku=bku: e.tensor_tensor(
                    dbuc[:, ts], bku.ap[:], brep[:, ts], ALU.mult),
                    extra_waits=(p.mark("v"),))
            expm = p.mark("s")
            p.op("v", lambda e: e.memset(_ins(dac, 2, 64, 1)[:, 0:2048:64][:, :, 0]
                                         if False else
                                         bass.AP(dac.tensor, dac.offset,
                                                 [list(dac.ap[0]), [64, 32]]), 0.0),
                 waits=(expm, p.mark("v")), incs=("fin_v",))
            zm = p.mark("v")
            with nc.allow_low_precision(reason="scan state is fp32 internal"):
                p.op("v", lambda e: e.tensor_tensor_scan(
                    hbuf, dac, dbuc, 0.0, ALU.mult, ALU.add),
                    waits=(zm, p.mark("t")), incs=("fin_v",))
            p.op("v", lambda e: e.tensor_tensor(hcbuf, hbuf, crep, ALU.mult),
                 waits=(p.mark("v"),), incs=("fin_v",))
            hcm = p.mark("v")
            for c in range(NTC):
                ts = slice(c * 512, (c + 1) * 512)
                ybk = ybk0 if c < 2 else ybk1
                hf = (c % 2) * 512
                ybk.mm([lambda e, ts=ts, hf=hf, m=m, ybk=ybk: e.matmul(
                    ybk.ap[:, hf:hf + 512], selyb, hcbuf[:, ts],
                    start=(m == 0), stop=(m == 63))],
                    extra_waits=(hcm, selym))
        for c in range(NTC):
            ts = slice(c * 512, (c + 1) * 512)
            ybk = ybk0 if c < 2 else ybk1
            hf = (c % 2) * 512
            ybk.read("v", lambda e, dtile=dtile, ts=ts, hf=hf, ybk=ybk:
                     e.tensor_copy(dlt[:, dtile, ts], ybk.ap[:, hf:hf + 512]),
                     extra_waits=(p.mark("v"),))
    p.op("v", lambda e: e.drain(), incs=("fin_v",))
    yfm = p.mark("v")

    for mt in range(NDT):
        slot = mt % 2
        p.op("g", lambda e, mt=mt, slot=slot: e.dma_start(
            wflat[:, slot * 1024:slot * 1024 + 1024].rearrange("p (a b) -> p a b", a=8),
            X["L0aa_wo"][mt].rearrange("a p o -> p a o")),
            waits=(p.mark("t"),), incs=(f"dms{slot}",), dma=True)
        wm = [(f"dms{slot}", p.semv[f"dms{slot}"])]
        for c in range(NTC):
            ts = slice(c * 512, (c + 1) * 512)
            bk = bank[c % 2]
            bk.mm([lambda e, i=i, mt=mt, ts=ts, bk=bk, slot=slot:
                   e.matmul(bk.ap[:], wflat[:, slot * 1024 + i * 128:slot * 1024 + (i + 1) * 128],
                            dlt[:, i, ts], start=(i == 0), stop=(i == NDT - 1))
                   for i in range(NDT)],
                  extra_waits=tuple(wm) + ((yfm,) if c == 0 else ())),
            bk.read("v", lambda e, mt=mt, ts=ts, bk=bk: e.tensor_add(
                xres[:, mt, ts], xres[:, mt, ts], bk.ap[:]))
    p.op("v", lambda e: e.drain(), incs=("fin_v",))
    return p.mark("v")


def emit_l0_ffn(ctx, idx):
    nc, p, X = ctx["nc"], ctx["p"], ctx["X"]
    xq = ctx["xn"]
    ga, ub = ctx["bigA"], ctx["bigB"]
    gav = ga[:].rearrange("p a t -> p (a t)").rearrange("p (x y) -> p x y", x=32)
    ubv = ub[:].rearrange("p a t -> p (a t)").rearrange("p (x y) -> p x y", x=32)
    small, smf = ctx["small"], ctx["smf"]
    statA, statB = ctx["statA"], ctx["statB"]
    cpool = ctx["cpool"]
    wbuf = ctx["wbuf"]
    wbv = wbuf[:].rearrange("p a t -> p (a t)").rearrange("p (x y z) -> p x y z", y=8, z=128)
    xres = ctx["xres"]
    ones_col = ctx["ones_col"]
    bank, dbank = ctx["bank"], ctx["dbank"]

    if "wq_g" not in ctx:
        ctx["wq_g"] = nc.dram_tensor("wq_g", [8, 32, 128, 128], BF16).ap()
        ctx["wq_u"] = nc.dram_tensor("wq_u", [8, 32, 128, 128], BF16).ap()
        ctx["wq_d"] = nc.dram_tensor("wq_d", [32, 8, 128, 128], BF16).ap()

    if not ctx.get("l0ffn_quantized"):
        ctx["l0ffn_quantized"] = True
        for wi, (nm, dst, I, O, mwslot) in enumerate((
                ("L0_ffn_w_gate_T", ctx["wq_g"], 1024, 4096, 4),
                ("L0_ffn_w_up_T", ctx["wq_u"], 1024, 4096, 5),
                ("L0_ffn_w_down_T", ctx["wq_d"], 4096, 1024, 6))):
            wv = X[nm].rearrange("(a q) o -> q a o", q=128)
            nit, noc = I // 128, O // 512
            p.op("v", lambda e: e.memset(smf[:, 0:1], 0.0), incs=("fin_v",))
            for it in range(nit):
                for oc in range(noc):
                    half = (it * noc + oc) % 2
                    hs = slice(half * 512, (half + 1) * 512)
                    p.op("y", lambda e, it=it, oc=oc, wv=wv, hs=hs:
                         e.dma_start(statA[:, hs], wv[:, it, oc * 512:(oc + 1) * 512]),
                         waits=(p.mark("v"),), incs=("dma",), dma=True)
                    p.op("v", lambda e, hs=hs: e.tensor_reduce(
                        col2d(smf[:, 1:2]), statA[:, hs], axis=AXX, op=ALU.add,
                        apply_absolute_value=True),
                        waits=[("dma", p.semv["dma"])], incs=("fin_v",))
                    p.op("v", lambda e: e.tensor_add(smf[:, 0:1], smf[:, 0:1],
                                                     smf[:, 1:2]),
                         waits=(p.mark("v"),), incs=("fin_v",))
            p.op("v", lambda e: e.tensor_copy(ctx["aux"][:, 144:145], smf[:, 0:1]),
                 waits=(p.mark("v"),), incs=("fin_v",))
            vm = p.mark("v")
            bk = bank[2]
            bk.mm([lambda e, bk=bk: e.matmul(bk.ap[:, 0:1], ones_col[:],
                                             ctx["aux"][:, 144:145], start=True, stop=True)],
                  extra_waits=(vm,))
            bk.read("v", lambda e, I=I, O=O, mwslot=mwslot: e.tensor_scalar(
                cpool[:, mwslot:mwslot + 1], bk.ap[:, 0:1], 1.0 / (I * O), 1e-5,
                ALU.mult, ALU.max))
            p.op("v", lambda e, mwslot=mwslot: e.reciprocal(smf[:, 2:3],
                                                            cpool[:, mwslot:mwslot + 1]),
                 waits=(p.mark("v"),), incs=("fin_v",))
            for it in range(nit):
                for oc in range(noc):
                    half = (it * noc + oc) % 2
                    hs = slice(half * 512, (half + 1) * 512)
                    p.op("y", lambda e, it=it, oc=oc, wv=wv, hs=hs:
                         e.dma_start(statA[:, hs], wv[:, it, oc * 512:(oc + 1) * 512]),
                         waits=(p.mark("v"),), incs=("dma",), dma=True)
                    p.op("v", lambda e, hs=hs: e.tensor_scalar(
                        statB[:, hs], statA[:, hs], smf[:, 2:3], MAGIC,
                        ALU.mult, ALU.add),
                        waits=[("dma", p.semv["dma"])], incs=("fin_v",))
                    p.op("v", lambda e, hs=hs: e.tensor_scalar(
                        statB[:, hs], statB[:, hs], MAGIC, -1.0,
                        ALU.subtract, ALU.max), waits=(p.mark("v"),), incs=("fin_v",))
                    p.op("v", lambda e, hs=hs, half=half: e.tensor_scalar_min(
                        small[:, half * 512:(half + 1) * 512],
                        statB[:, hs], 1.0),
                        waits=(p.mark("v"),
                               (f"dms{6 + half}", p.semv[f"dms{6 + half}"])),
                        incs=("fin_v",))
                    vmk = p.mark("v")
                    sq = f"dms{6 + half}"
                    p.op("g", lambda e, it=it, oc=oc, dst=dst, half=half:
                         e.dma_start(
                             _wq_out_ap(dst, it, oc),
                             small[:, half * 512:(half + 1) * 512]
                             .rearrange("p (a b) -> p a b", a=4)),
                         waits=(vmk,), incs=(sq,), dma=True)
            p.op("v", lambda e: e.drain(),
                 waits=[("dms6", p.semv["dms6"]), ("dms7", p.semv["dms7"])],
                 incs=("fin_v",))

    lnm = emit_layernorm(ctx)

    mrep = small[:, 0:T]
    mtmp = small[:, 2048:4096]
    p.op("v", lambda e: e.tensor_tensor(mtmp, xq[:, 0, :], xq[:, 0, :], ALU.mult),
         waits=(p.mark("v"),), incs=("fin_v",))
    for i in range(1, NDT):
        p.op("v", lambda e, i=i: e.tensor_tensor(mrep, xq[:, i, :], xq[:, i, :],
                                                 ALU.mult),
             waits=(p.mark("v"),), incs=("fin_v",))
        p.op("v", lambda e: e.tensor_max(mtmp, mtmp, mrep),
             waits=(p.mark("v"),), incs=("fin_v",))
    gm = emit_part_max_rep(ctx, mtmp, T, mrep, sqrt_of_sq=True)
    for c in range(2):
        sl = slice(c * 1024, (c + 1) * 1024)
        p.op("v", lambda e, sl=sl: e.tensor_scalar_max(statA[:], mrep[:, sl], 1e-5),
             waits=(gm,) if c == 0 else (p.mark("v"),), incs=("fin_v",))
        p.op("v", lambda e: e.reciprocal(statB[:], statA[:]),
             waits=(p.mark("v"),), incs=("fin_v",))
        bm = p.mark("v")
        for i in range(NDT):
            p.op("v", lambda e, i=i, sl=sl: e.tensor_tensor(statA[:], xq[:, i, sl],
                                                            statB[:], ALU.mult),
                 waits=(bm,) if i == 0 else (p.mark("v"),), incs=("fin_v",))
            p.op("v", lambda e: e.tensor_scalar(statA[:], statA[:], 127.0, MAGIC,
                                                ALU.mult, ALU.add),
                 waits=(p.mark("v"),), incs=("fin_v",))
            p.op("v", lambda e: e.tensor_scalar(statA[:], statA[:], MAGIC, -128.0,
                                                ALU.subtract, ALU.max),
                 waits=(p.mark("v"),), incs=("fin_v",))
            p.op("v", lambda e, i=i, sl=sl: e.tensor_scalar_min(xq[:, i, sl],
                                                                statA[:], 127.0),
                 waits=(p.mark("v"),), incs=("fin_v",))
    qm = p.mark("v")

    p.op("v", lambda e: e.tensor_tensor(cpool[:, 7:8], cpool[:, 5:6], cpool[:, 6:7],
                                        ALU.mult), waits=(p.mark("v"),), incs=("fin_v",))
    p.op("v", lambda e: e.tensor_scalar_mul(cpool[:, 7:8], cpool[:, 7:8],
                                            1.0 / (127.0 * 127.0)),
         waits=(p.mark("v"),), incs=("fin_v",))
    p.op("v", lambda e: e.tensor_scalar_mul(smf[:, 3:4], cpool[:, 4:5], 1.0 / 127.0),
         waits=(p.mark("v"),), incs=("fin_v",))
    scm = p.mark("v")

    mh = small[:, 2048:2560]
    for c in range(4):
        ts = slice(c * 512, (c + 1) * 512)
        for ot in range(32):
            slot_g = (ot % 2) * 2
            slot_u = slot_g + 1
            sg, su = f"dms{slot_g}", f"dms{slot_u}"
            p.op("g", lambda e, ot=ot, slot_g=slot_g:
                 e.dma_start(wbv[:, slot_g, :, :], ctx["wq_g"][:, ot, :, :]
                             .rearrange("a p o -> p a o")),
                 waits=(p.mark("t"),) if ot >= 2 else (qm,), incs=(sg,), dma=True)
            p.op("g", lambda e, ot=ot, slot_u=slot_u:
                 e.dma_start(wbv[:, slot_u, :, :], ctx["wq_u"][:, ot, :, :]
                             .rearrange("a p o -> p a o")),
                 incs=(su,), dma=True)
            wmk = [(sg, p.semv[sg]), (su, p.semv[su])]
            for h, slot, bidx in ((0, slot_g, 0), (1, slot_u, 1)):
                bk = bank[bidx]
                bk.mm([lambda e, i=i, slot=slot, ts=ts, bk=bk:
                       e.matmul(bk.ap[:], wbv[:, slot, i, :], xq[:, i, ts],
                                start=(i == 0), stop=(i == NDT - 1))
                       for i in range(NDT)],
                      extra_waits=tuple(wmk) + ((qm,) if ot < 2 else ()))
            bank[0].read("v", lambda e, ot=ot, ts=ts: e.tensor_tensor(
                gav[:, ot, :], bank[0].ap[:], mrep[:, ts], ALU.mult),
                extra_waits=(scm,))
            vmk = p.mark("v")
            p.op("s", lambda e, ot=ot, ts=ts: e.activation(
                gav[:, ot, :], gav[:, ot, :], AF.Silu, bias=0.0,
                scale=smf[:, 3:4]), waits=(vmk,), incs=("fin_s",))
            bank[1].read("v", lambda e, ot=ot, ts=ts: e.tensor_tensor(
                ubv[:, ot, :], bank[1].ap[:], mrep[:, ts], ALU.mult))
        smk = p.mark("s")
        hw0 = (smk, p.mark("v"))
        for ot in range(32):
            p.op("v", lambda e, ot=ot, ts=ts: e.tensor_tensor(
                gav[:, ot, :], gav[:, ot, :], ubv[:, ot, :], ALU.mult),
                waits=hw0 if ot == 0 else (),
                incs=("fin_v",) if ot == 31 else ())
        hm = p.mark("v")
        mhtmp = small[:, 2560:3072]
        mhsq = small[:, 3584:4096]
        p.op("v", lambda e: e.tensor_tensor(mhtmp, gav[:, 0, :], gav[:, 0, :],
                                            ALU.mult), waits=(hm,), incs=("fin_v",))
        for ot in range(1, 32):
            p.op("v", lambda e, ot=ot: e.tensor_tensor(mhsq, gav[:, ot, :],
                                                       gav[:, ot, :], ALU.mult),
                 waits=(p.mark("v"),), incs=("fin_v",))
            p.op("v", lambda e: e.tensor_max(mhtmp, mhtmp, mhsq),
                 waits=(p.mark("v"),), incs=("fin_v",))
        gm2 = emit_part_max_rep(ctx, mhtmp, 512, mh, sqrt_of_sq=True)
        p.op("v", lambda e: e.tensor_scalar_max(statA[:, 0:512], mh, 1e-5),
             waits=(gm2,), incs=("fin_v",))
        p.op("v", lambda e: e.reciprocal(statB[:, 0:512], statA[:, 0:512]),
             waits=(p.mark("v"),), incs=("fin_v",))
        bm2 = p.mark("v")
        for ot in range(32):
            p.op("v", lambda e, ot=ot, ts=ts: e.tensor_tensor(
                statA[:, 512:1024], gav[:, ot, :], statB[:, 0:512], ALU.mult),
                waits=(bm2,) if ot == 0 else (p.mark("v"),), incs=("fin_v",))
            p.op("v", lambda e: e.tensor_scalar(statA[:, 512:1024], statA[:, 512:1024],
                                                127.0, MAGIC, ALU.mult, ALU.add),
                 waits=(p.mark("v"),), incs=("fin_v",))
            p.op("v", lambda e: e.tensor_scalar(statA[:, 512:1024], statA[:, 512:1024],
                                                MAGIC, -128.0, ALU.subtract, ALU.max),
                 waits=(p.mark("v"),), incs=("fin_v",))
            p.op("v", lambda e, ot=ot, ts=ts: e.tensor_scalar_min(
                gav[:, ot, :], statA[:, 512:1024], 127.0),
                 waits=(p.mark("v"),), incs=("fin_v",))
        hqm = p.mark("v")
        for ot2 in range(NDT):
            sb4 = (ot2 % 2) * 4
            wdv = wbuf[:].rearrange("p a t -> p (a t)").rearrange(
                "p (s x y) -> p s x y", s=2, x=32)
            sd = f"dms{4 + ot2 % 2}"
            p.op("g", lambda e, ot2=ot2, sb4=sb4, wdv=wdv:
                 e.dma_start(wdv[:, sb4 // 4, :, :],
                             ctx["wq_d"][:, ot2, :, :].rearrange("a p o -> p a o")),
                 waits=(p.mark("t"),) if ot2 >= 2 else (hqm,), incs=(sd,), dma=True)
            wmk = [(sd, p.semv[sd])]
            bk = bank[2]
            bk.mm([lambda e, it=it, sb4=sb4, ts=ts, bk=bk, wdv=wdv:
                   e.matmul(bk.ap[:], wdv[:, sb4 // 4, it, :],
                            gav[:, it, :], start=(it == 0), stop=(it == 31))
                   for it in range(32)],
                  extra_waits=tuple(wmk) + ((hqm,) if ot2 < 2 else ()))
            bk.read("v", lambda e, ot2=ot2, ts=ts, bk=bk: e.tensor_tensor(
                statB[:, 0:512], bk.ap[:], mh, ALU.mult),
                extra_waits=(p.mark("v"),))
            p.op("v", lambda e, ot2=ot2, ts=ts: e.scalar_tensor_tensor(
                xres[:, ot2, ts], statB[:, 0:512], cpool[:, 7:8],
                xres[:, ot2, ts], ALU.mult, ALU.add),
                waits=(p.mark("v"),), incs=("fin_v",))
    return p.mark("v")



def emit_part_max_rep(ctx, m_ap, W, out_rep, clipv=1e-5, sqrt_of_sq=False):
    p = ctx["p"]
    tbank = ctx["tbank"]
    bank = ctx["bank"]
    ident, ones_row = ctx["ident"], ctx["ones_row"]
    small = ctx["small"]
    nt = W // 128
    mtok = ctx["aux"][:, 128:128 + nt]
    for j0 in range(0, nt, 8):
        jn = min(8, nt - j0)
        tbank.mm([lambda e, j=j: e.transpose(
            tbank.ap[:, (j % 8) * 128:(j % 8 + 1) * 128],
            m_ap[:, j * 128:(j + 1) * 128], ident[:])
            for j in range(j0, j0 + jn)], extra_waits=(p.mark("v"),))
        for j in range(j0, j0 + jn):
            tbank.read("v", lambda e, j=j: e.tensor_reduce(
                col2d(mtok[:, j:j + 1]), tbank.ap[:, (j % 8) * 128:(j % 8 + 1) * 128],
                axis=AXX, op=ALU.max))
    p.op("v", lambda e: e.drain(), incs=("fin_v",))
    if sqrt_of_sq:
        p.op("v", lambda e, nt=nt: e.tensor_scalar_max(mtok[:, 0:nt], mtok[:, 0:nt],
                                                       float(clipv) ** 2),
             waits=(p.mark("v"),), incs=("fin_v",))
        vq = p.mark("v")
        p.op("s", lambda e, nt=nt: e.activation(mtok[:, 0:nt], mtok[:, 0:nt], AF.Ln,
                                                bias=0.0, scale=1.0), waits=(vq,),
             incs=("fin_s",))
        p.op("s", lambda e, nt=nt: e.activation(mtok[:, 0:nt], mtok[:, 0:nt], AF.Exp,
                                                bias=0.0, scale=0.5),
             waits=(p.mark("s"),), incs=("fin_s",))
        sq = p.mark("s")
        p.op("v", lambda e: e.drain(), waits=(sq,), incs=("fin_v",))
    mm2 = p.mark("v")
    for j0 in range(0, nt, 8):
        jn = min(8, nt - j0)
        tbank.mm([lambda e, j=j: e.transpose(
            tbank.ap[0:1, (j % 8) * 128:(j % 8 + 1) * 128],
            mtok[:, j:j + 1], ident[:])
            for j in range(j0, j0 + jn)], extra_waits=(mm2,))
        tbank.read("v", lambda e, j0=j0, jn=jn: e.tensor_copy(
            out_rep[0:1, j0 * 128:(j0 + jn) * 128], tbank.ap[0:1, 0:jn * 128]))
    p.op("v", lambda e: e.drain(), incs=("fin_v",))
    rm = p.mark("v")
    for j in range(nt):
        bk = bank[2]
        bk.mm([lambda e, j=j, bk=bk: e.matmul(
            bk.ap[:, 0:128], ones_row[0:1, 0:128],
            out_rep[0:1, j * 128:(j + 1) * 128], start=True, stop=True)],
              extra_waits=(rm,))
        bk.read("v", lambda e, j=j, bk=bk: e.tensor_scalar_max(
            out_rep[:, j * 128:(j + 1) * 128], bk.ap[:, 0:128], clipv))
    p.op("v", lambda e: e.drain(), incs=("fin_v",))
    return p.mark("v")


def _wq_out_ap(dst, it, oc):
    nt = dst.shape[1]
    base = (it * nt + oc * 4) * 128 * 128
    return bass.AP(dst.tensor, base, [[128, 128], [128 * 128, 4], [1, 128]])


def emit_l0_attn_b(ctx, idx):
    nc, p, X = ctx["nc"], ctx["p"], ctx["X"]
    xn = ctx["xn"]
    k = ctx["bigA"]
    vtok = ctx["bigB"][:].rearrange("p a t -> p (a t)").rearrange("p (x y) -> p x y", x=NTT)
    aux = ctx["aux"]
    qtile = aux[:, 4480:5504].rearrange("p (a b) -> p a b", a=8)
    cosb = aux[:, 0:T]
    sinb = aux[:, T:2 * T]
    qstg = aux[:, 2 * T:2 * T + 256].rearrange("p (a b) -> p a b", a=2)
    trilm = aux[:, 2 * T + 256:2 * T + 384]
    wbuf, small, smf = ctx["wbuf"], ctx["small"], ctx["smf"]
    xres = ctx["xres"]
    dbank, bank, tbank = ctx["dbank"], ctx["bank"], ctx["tbank"]
    ones_row, ident = ctx["ones_row"], ctx["ident"]

    p.op("v", lambda e: e.memset(qstg[:], 0.0), incs=("fin_v",))
    lnm = emit_layernorm(ctx)

    wq = X["L0_attn_b_WqT_perm"].rearrange("(n q) o -> q n o", q=128)
    wk = X["L0_attn_b_WkT_perm"].rearrange("(n q) o -> q n o", q=128)
    wv = X["L0_attn_b_WvT"].rearrange("(n q) o -> q n o", q=128)
    wo = X["L0_attn_b_WoT"].rearrange("(n q) o -> q n o", q=128)

    p.op("g", lambda e: e.dma_start(wbuf[:], wk), incs=("dmaw",), dma=True)
    wm = [("dmaw", p.semv["dmaw"]), lnm]
    for ot in range(NDT):
        for c in range(2):
            bk = dbank[c]
            ems = []
            for s2 in range(2):
                for i in range(NDT):
                    ems.append(lambda e, i=i, ot=ot, c=c, s2=s2, bk=bk:
                               e.matmul(bk.ap[:, s2 * 512:(s2 + 1) * 512],
                                        wbuf[:, i, ot * 128:(ot + 1) * 128],
                                        xn[:, i, c * 1024 + s2 * 512:c * 1024 + (s2 + 1) * 512],
                                        start=(i == 0), stop=(i == NDT - 1)))
            bk.mm(ems, extra_waits=tuple(wm) if ot == 0 else ())
            bk.read("v", lambda e, ot=ot, c=c, bk=bk:
                    e.tensor_copy(k[:, ot, c * 1024:(c + 1) * 1024], bk.ap[:]))
    p.op("g", lambda e: e.dma_start(wbuf[:], wv), waits=(p.mark("t"),),
         incs=("dmaw",), dma=True)
    wm = [("dmaw", p.semv["dmaw"])]
    for tt in range(NTT):
        for c in range(2):
            bk = bank[c]
            sl = slice(c * 512, (c + 1) * 512)
            bk.mm([lambda e, i=i, tt=tt, sl=sl, bk=bk:
                   e.matmul(bk.ap[:], xn[:, i, tt * 128:(tt + 1) * 128],
                            wbuf[:, i, sl], start=(i == 0), stop=(i == NDT - 1))
                   for i in range(NDT)],
                  extra_waits=tuple(wm) if tt == 0 else ())
            bk.read("v", lambda e, tt=tt, sl=sl, bk=bk:
                    e.tensor_copy(vtok[:, tt, sl], bk.ap[:]))
    p.op("g", lambda e: e.dma_start(wbuf[:], wq), waits=(p.mark("t"),),
         incs=("dmaw",), dma=True)
    p.op("g", lambda e: e.dma_start(cosb, X["c_rope_cos"]),
         incs=("dmaw",), dma=True)
    p.op("g", lambda e: e.dma_start(sinb, X["c_rope_sin"]),
         incs=("dmaw",), dma=True)
    p.op("g", lambda e: e.dma_start(trilm, X["c_trilm"]),
         incs=("dmaw",), dma=True)
    wm = [("dmaw", p.semv["dmaw"])]
    p.op("t", lambda e: e.nop(), waits=wm)
    p.op("v", lambda e: e.nop(), waits=wm)

    ta = small[:, 0:1024]
    tb = small[:, 1024:2048]
    tc2 = small[:, 2048:3072]
    for g in range(4):
        for c in range(2):
            sl = slice(c * 1024, (c + 1) * 1024)
            vw = (p.mark("v"),)
            p.op("v", lambda e, g=g, sl=sl: e.tensor_tensor(ta, k[:, g, sl],
                                                            cosb[:, sl], ALU.mult),
                 waits=vw, incs=("fin_v",))
            p.op("v", lambda e, g=g, sl=sl: e.tensor_tensor(tb, k[:, 4 + g, sl],
                                                            sinb[:, sl], ALU.mult),
                 incs=("fin_v",))
            p.op("v", lambda e, g=g, sl=sl: e.tensor_tensor(tc2, k[:, g, sl],
                                                            sinb[:, sl], ALU.mult),
                 incs=("fin_v",))
            vm = p.mark("v")
            p.op("v", lambda e, g=g, sl=sl: e.tensor_tensor(k[:, g, sl], ta, tb,
                                                            ALU.subtract),
                 waits=(vm,), incs=("fin_v",))
            p.op("v", lambda e, g=g, sl=sl: e.tensor_tensor(ta, k[:, 4 + g, sl],
                                                            cosb[:, sl], ALU.mult),
                 waits=(p.mark("v"),), incs=("fin_v",))
            vm = p.mark("v")
            p.op("v", lambda e, g=g, sl=sl: e.tensor_tensor(k[:, 4 + g, sl], tc2, ta,
                                                            ALU.add),
                 waits=(vm,), incs=("fin_v",))
    p.op("v", lambda e: e.drain(), incs=("fin_v",))
    krm = p.mark("v")

    probs = small[:, 0:T]
    pt = small[:, T:2 * T]

    for tq in range(NTT):
        tqs = slice(tq * 128, (tq + 1) * 128)
        w = (tq + 1) * 128
        for ot in range(NDT):
            bk = bank[2]
            bk.mm([lambda e, i=i, ot=ot, tq=tq, bk=bk:
                   e.matmul(bk.ap[:, 0:128], wbuf[:, i, ot * 128:(ot + 1) * 128],
                            xn[:, i, tq * 128:(tq + 1) * 128], start=(i == 0),
                            stop=(i == NDT - 1))
                   for i in range(NDT)],
                  extra_waits=(krm,) if (tq == 0 and ot == 0) else ())
            bk.read("v", lambda e, ot=ot, bk=bk:
                    e.tensor_copy(qtile[:, ot, :], bk.ap[:, 0:128]))
        vw = (p.mark("v"),)
        p.op("v", lambda e, tqs=tqs: e.tensor_tensor(
            ta[:, 0:512], qtile[:, 0:4, :].rearrange("p a t -> p (a t)"),
            _rope4(cosb, tqs), ALU.mult), waits=vw, incs=("fin_v",))
        p.op("v", lambda e, tqs=tqs: e.tensor_tensor(
            tb[:, 0:512], qtile[:, 4:8, :].rearrange("p a t -> p (a t)"),
            _rope4(sinb, tqs), ALU.mult), incs=("fin_v",))
        p.op("v", lambda e, tqs=tqs: e.tensor_tensor(
            tc2[:, 0:512], qtile[:, 0:4, :].rearrange("p a t -> p (a t)"),
            _rope4(sinb, tqs), ALU.mult), incs=("fin_v",))
        vm = p.mark("v")
        p.op("v", lambda e: e.tensor_tensor(
            qtile[:, 0:4, :].rearrange("p a t -> p (a t)"), ta[:, 0:512],
            tb[:, 0:512], ALU.subtract), waits=(vm,), incs=("fin_v",))
        p.op("v", lambda e, tqs=tqs: e.tensor_tensor(
            ta[:, 0:512], qtile[:, 4:8, :].rearrange("p a t -> p (a t)"),
            _rope4(cosb, tqs), ALU.mult), waits=(p.mark("v"),), incs=("fin_v",))
        vm = p.mark("v")
        p.op("v", lambda e: e.tensor_tensor(
            qtile[:, 4:8, :].rearrange("p a t -> p (a t)"), tc2[:, 0:512],
            ta[:, 0:512], ALU.add), waits=(vm,), incs=("fin_v",))
        qrm = p.mark("v")

        for hi in range(16):
            m, g = hi // 4, hi % 4
            h = g * 4 + m
            if g == 0:
                pm = (m - 1) % 4
                p.op("v", lambda e, pm=pm: e.memset(
                    qstg[32 * pm:32 * pm + 32, :, :], 0.0),
                    waits=(qrm, p.mark("t")), incs=("fin_v",))
            p.op("v", lambda e, g=g, m=m: e.tensor_copy(
                qstg[32 * m:32 * m + 32, 0, :], qtile[32 * m:32 * m + 32, g, :]),
                waits=(qrm, p.mark("t")), incs=("fin_v",))
            p.op("v", lambda e, g=g, m=m: e.tensor_copy(
                qstg[32 * m:32 * m + 32, 1, :], qtile[32 * m:32 * m + 32, 4 + g, :]),
                incs=("fin_v",))
            stm = p.mark("v")
            nch = tq // 4 + 1
            for ch in range(nch):
                bk = dbank[ch // 2]
                base = ch * 512
                nn = min(512, w - base)
                bk.mm([lambda e, par=par, ch=ch, nn=nn, bk=bk, g=g, base=base:
                       e.matmul(bk.ap[:, (ch % 2) * 512:(ch % 2) * 512 + nn],
                                qstg[:, par, :], k[:, par * 4 + g, base:base + nn],
                                start=(par == 0), stop=(par == 1))
                       for par in range(2)],
                      extra_waits=(stm,) if ch == 0 else ())
            dbk = dbank[(tq // 4) // 2]
            doff = ((tq // 4) % 2) * 512 + (tq % 4) * 128
            dbk.read("v", lambda e, dbk=dbk, doff=doff: e.tensor_add(
                dbk.ap[:, doff:doff + 128], dbk.ap[:, doff:doff + 128], trilm))
            mskm = p.mark("v")
            nred = (w + 1023) // 1024
            for r in range(nred):
                ww = min(1024, w - r * 1024)
                dbank[r].read("v", lambda e, r=r, ww=ww: e.tensor_reduce(
                    col2d(smf[:, r:r + 1]), dbank[r].ap[:, 0:ww], axis=AXX,
                    op=ALU.max), extra_waits=(mskm,))
            p.op("v", lambda e, nred=nred: e.tensor_reduce(
                col2d(smf[:, 2:3]), smf[:, 0:nred], axis=AXX, op=ALU.max),
                waits=(p.mark("v"),), incs=("fin_v",))
            p.op("v", lambda e: e.tensor_scalar_mul(smf[:, 3:4], smf[:, 2:3],
                                                    -0.125),
                 waits=(p.mark("v"),), incs=("fin_v",))
            vmk = p.mark("v")
            for r in range(nred):
                ww = min(1024, w - r * 1024)
                dbank[r].read("s", lambda e, r=r, ww=ww: e.activation(
                    probs[:, r * 1024:r * 1024 + ww], dbank[r].ap[:, 0:ww], AF.Exp,
                    bias=smf[:, 3:4], scale=0.125), extra_waits=(vmk,))
            smk = p.mark("s")
            p.op("v", lambda e, w=w: e.tensor_reduce(col2d(smf[:, 4:5]),
                                                     probs[:, 0:w], axis=AXX,
                                                     op=ALU.add),
                 waits=(smk,), incs=("fin_v",))
            p.op("v", lambda e: e.reciprocal(smf[:, 5:6], smf[:, 4:5]),
                 waits=(p.mark("v"),), incs=("fin_v",))
            p.op("v", lambda e, w=w: e.tensor_scalar_mul(probs[:, 0:w], probs[:, 0:w],
                                                         smf[:, 5:6]),
                 waits=(p.mark("v"),), incs=("fin_v",))
            dmk = p.mark("v")
            for tk0 in range(0, tq + 1, 8):
                tkn = min(8, tq + 1 - tk0)
                tbank.mm([lambda e, tkt=tkt: e.transpose(
                    tbank.ap[:, (tkt % 8) * 128:(tkt % 8 + 1) * 128],
                    probs[:, tkt * 128:(tkt + 1) * 128], ident[:])
                    for tkt in range(tk0, tk0 + tkn)], extra_waits=(dmk,))
                tbank.read("v", lambda e, tk0=tk0, tkn=tkn: e.tensor_copy(
                    pt[:, tk0 * 128:(tk0 + tkn) * 128],
                    tbank.ap[:, 0:tkn * 128]))
            ptm = p.mark("v")
            hb = (h % 2) * 64
            bk = bank[0] if h % 2 == 0 else bank[1]
            bk.mm([lambda e, tkt=tkt, h=h, hb=hb, bk=bk, tq=tq:
                   e.matmul(bk.ap[hb:hb + 64, 0:128],
                            vtok[:, tkt, h * 64:(h + 1) * 64],
                            pt[:, tkt * 128:(tkt + 1) * 128],
                            start=(tkt == 0), stop=(tkt == tq))
                   for tkt in range(tq + 1)], extra_waits=(ptm,))
            bk.read("v", lambda e, h=h, hb=hb, tqs=tqs, bk=bk: e.tensor_copy(
                bass.AP(xn[:].tensor, xn[:].offset + (h // 2) * T + tqs.start,
                        [[NDT * T, 64], [1, 128]]) if False else
                xn[hb:hb + 64, h // 2, tqs], bk.ap[hb:hb + 64, 0:128]))
    p.op("v", lambda e: e.drain(), incs=("fin_v",))
    aom = p.mark("v")

    p.op("g", lambda e: e.dma_start(wbuf[:], wo), waits=(p.mark("t"),),
         incs=("dmaw",), dma=True)
    wm = [("dmaw", p.semv["dmaw"]), aom]
    for ot in range(NDT):
        for c in range(NTC):
            bk = bank[c % 2]
            sl = slice(c * 512, (c + 1) * 512)
            bk.mm([lambda e, i=i, ot=ot, sl=sl, bk=bk:
                   e.matmul(bk.ap[:], wbuf[:, i, ot * 128:(ot + 1) * 128],
                            xn[:, i, sl], start=(i == 0), stop=(i == NDT - 1))
                   for i in range(NDT)],
                  extra_waits=tuple(wm) if ot == 0 and c == 0 else ())
            bk.read("v", lambda e, ot=ot, sl=sl, bk=bk: e.tensor_add(
                xres[:, ot, sl], xres[:, ot, sl], bk.ap[:]))
    p.op("v", lambda e: e.drain(), incs=("fin_v",))
    return p.mark("v")


def _rope4(tab, tqs):
    sl = tab[:, tqs]
    return bcast(sl, 1, 4)


def emit_l1_ffn(ctx, idx):
    nc, p, X = ctx["nc"], ctx["p"], ctx["X"]
    xq = ctx["xn"]
    ga, ub = ctx["bigA"], ctx["bigB"]
    gav = ga[:].rearrange("p a t -> p (a t)")[:, 0:24 * 512].rearrange(
        "p (x y) -> p x y", x=24)
    ubv = ub[:].rearrange("p a t -> p (a t)")[:, 0:24 * 512].rearrange(
        "p (x y) -> p x y", x=24)
    small, smf = ctx["small"], ctx["smf"]
    statA, statB = ctx["statA"], ctx["statB"]
    cpool, lcol = ctx["cpool"], ctx["lcol"]
    wbuf = ctx["wbuf"]
    wflat = wbuf[:].rearrange("p a t -> p (a t)")
    xres = ctx["xres"]
    ident = ctx["ident"]
    bank, dbank, tbank = ctx["bank"], ctx["dbank"], ctx["tbank"]
    II = 3072
    NIT = II // 128

    if "wq_lg" not in ctx:
        ctx["wq_lg"] = nc.dram_tensor("wq_lg", [8, NIT, 128, 128], BF16).ap()
        ctx["wq_lu"] = nc.dram_tensor("wq_lu", [8, NIT, 128, 128], BF16).ap()
        ctx["wq_ld"] = nc.dram_tensor("wq_ld", [NIT, 8, 128, 128], BF16).ap()
        ctx["upr"] = nc.dram_tensor("upr", [NIT, 128, T], BF16).ap()
        ctx["gtr"] = nc.dram_tensor("gtr", [NIT, 128, T], BF16).ap()
        ctx["cc_in"] = nc.dram_tensor("cc_in", [1, 8], BF16).ap()
        ctx["cc_out"] = nc.dram_tensor("cc_out", [1, 8], BF16).ap()

    def dbar(sems):
        p.op("v", lambda e: e.drain(),
             waits=tuple((sq, p.semv[sq]) for sq in sems), incs=("fin_v",))

    p.tag = f"f{idx}wq"
    if not ctx.get("l1ffn_quantized"):
        ctx["l1ffn_quantized"] = True
        sb_par = [smf[:, 0:16], lcol[:, 16:32]]
        iv_par = [lcol[:, 0:16], lcol[:, 32:48]]
        v0w = p.mark("v")
        consumed = {}
        tile = 0
        for nm, dst, O, I in (("L1_ffn_w_gate", ctx["wq_lg"], II, 1024),
                              ("L1_ffn_w_up", ctx["wq_lu"], II, 1024),
                              ("L1_ffn_w_down", ctx["wq_ld"], 1024, II)):
            wv = X[nm].rearrange("(a q) o -> q a o", q=128)
            for ot in range(O // 128):
                for ic in range(I // 512):
                    half = tile % 2
                    hs = slice(half * 512, (half + 1) * 512)
                    sbuf, ivb = sb_par[half], iv_par[half]
                    p.op("y", lambda e, ot=ot, ic=ic, wv=wv, hs=hs:
                         e.dma_start(statA[:, hs], wv[:, ot, ic * 512:(ic + 1) * 512]),
                         waits=(consumed.get(tile - 2, v0w),
                                ("dma", p.semv["dma"])),
                         incs=("dma",), dma=True)
                    dmw = [("dma", p.semv["dma"])]
                    p.op("v", lambda e, hs=hs, sbuf=sbuf: e.tensor_reduce(
                        _ins(sbuf, 2, 1, 1),
                        statA[:, hs].rearrange("p (a b) -> p a b", b=32),
                        axis=AXX, op=ALU.max, apply_absolute_value=True),
                        waits=tuple(dmw) + (p.mark("v"),), incs=("fin_v",))
                    p.op("v", lambda e, sbuf=sbuf: e.tensor_scalar(
                        sbuf, sbuf, 1.0 / 7.0, 1e-12,
                        ALU.mult, ALU.max), waits=(p.mark("v"),), incs=("fin_v",))
                    p.op("v", lambda e, sbuf=sbuf, ivb=ivb: e.reciprocal(ivb, sbuf),
                         waits=(p.mark("v"),), incs=("fin_v",))
                    p.op("v", lambda e, hs=hs, ivb=ivb: e.tensor_tensor(
                        statB[:, hs].rearrange("p (a b) -> p a b", b=32),
                        statA[:, hs].rearrange("p (a b) -> p a b", b=32),
                        bcast(ivb, 2, 32), ALU.mult),
                        waits=(p.mark("v"),), incs=("fin_v",))
                    p.op("v", lambda e, hs=hs: e.tensor_scalar(
                        statB[:, hs], statB[:, hs], MAGIC, MAGIC,
                        ALU.add, ALU.subtract), waits=(p.mark("v"),), incs=("fin_v",))
                    p.op("v", lambda e, hs=hs: e.tensor_scalar(
                        statB[:, hs], statB[:, hs], -8.0, 7.0,
                        ALU.max, ALU.min), waits=(p.mark("v"),), incs=("fin_v",))
                    p.op("v", lambda e, hs=hs, sbuf=sbuf: e.tensor_tensor(
                        small[:, hs].rearrange("p (a b) -> p a b", b=32),
                        statB[:, hs].rearrange("p (a b) -> p a b", b=32),
                        bcast(sbuf, 2, 32), ALU.mult),
                        waits=(p.mark("v"),), incs=("fin_v",))
                    wqm = p.mark("v")
                    consumed[tile] = wqm
                    tbank.mm([lambda e, j=j, half=half: e.transpose(
                        tbank.ap[:, (half * 4 + j) * 128:(half * 4 + j + 1) * 128],
                        small[:, half * 512 + j * 128:half * 512 + (j + 1) * 128],
                        ident[:]) for j in range(4)], extra_waits=(wqm,))
                    sq = f"dms{6 + half}"
                    for j in range(4):
                        tbank.read("v", lambda e, j=j, half=half: e.tensor_copy(
                            small[:, 2048 + half * 512 + j * 128:
                                  2048 + half * 512 + (j + 1) * 128],
                            tbank.ap[:, (half * 4 + j) * 128:(half * 4 + j + 1) * 128]),
                            extra_waits=((sq, p.semv[sq]),) if j == 0 else ())
                    trm = p.mark("v")
                    p.op("g", lambda e, ot=ot, ic=ic, half=half, dst=dst, I=I:
                         e.dma_start(_lpbq_out_ap(dst, ot, ic),
                                     small[:, 2048 + half * 512:2048 + (half + 1) * 512]
                                     .rearrange("p (a b) -> p a b", a=4)),
                         waits=(trm,), incs=(sq,), dma=True)
                    tile += 1
        dbar(("dms6", "dms7"))

    p.tag = f"f{idx}ln"
    lnm = emit_layernorm(ctx)

    p.tag = f"f{idx}xq"
    mcol = smf[:, 4:5]
    p.op("v", lambda e: e.tensor_tensor(statA[:], xq[:, 0, 0:1024], xq[:, 0, 0:1024],
                                        ALU.mult), waits=(lnm,), incs=("fin_v",))
    for i in range(NDT):
        for c in range(2):
            if i == 0 and c == 0:
                continue
            p.op("v", lambda e, i=i, c=c: e.tensor_tensor(
                statB[:], xq[:, i, c * 1024:(c + 1) * 1024],
                xq[:, i, c * 1024:(c + 1) * 1024], ALU.mult),
                waits=(p.mark("v"),), incs=("fin_v",))
            p.op("v", lambda e: e.tensor_max(statA[:], statA[:], statB[:]),
                 waits=(p.mark("v"),), incs=("fin_v",))
    p.op("v", lambda e: e.tensor_reduce(col2d(mcol), statA[:], axis=AXX, op=ALU.max),
         waits=(p.mark("v"),), incs=("fin_v",))
    _sqrt_col(ctx, mcol, 1)
    arm = _allreduce_max(ctx, [mcol])
    p.op("v", lambda e: e.tensor_scalar_mul(cpool[:, 8:9], lcol[:, 64:65], 1.0 / 32767.0),
         waits=(arm,), incs=("fin_v",))
    p.op("v", lambda e: e.reciprocal(cpool[:, 9:10], cpool[:, 8:9]),
         waits=(p.mark("v"),), incs=("fin_v",))
    scm = p.mark("v")
    for i in range(NDT):
        for c in range(2):
            sl = slice(c * 1024, (c + 1) * 1024)
            p.op("v", lambda e, i=i, sl=sl: e.tensor_scalar(
                statA[:], xq[:, i, sl], cpool[:, 9:10], MAGIC, ALU.mult, ALU.add),
                waits=(scm,) if i == 0 and c == 0 else (p.mark("v"),), incs=("fin_v",))
            p.op("v", lambda e: e.tensor_scalar(statA[:], statA[:], MAGIC, -32768.0,
                                                ALU.subtract, ALU.max),
                 waits=(p.mark("v"),), incs=("fin_v",))
            p.op("v", lambda e, i=i, sl=sl: e.tensor_scalar_min(
                xq[:, i, sl], statA[:], 32767.0),
                 waits=(p.mark("v"),), incs=("fin_v",))
    p.op("v", lambda e: e.drain(), incs=("fin_v",))
    qm = p.mark("v")

    p.tag = f"f{idx}p1"
    p.op("v", lambda e: e.memset(smf[:, 5:7], 0.0), incs=("fin_v",))
    t0p = p.mark("t")
    mmarks = {}
    for ot in range(NIT):
        slot_g, slot_u = (ot % 2) * 2, (ot % 2) * 2 + 1
        sg, su = f"dms{slot_g}", f"dms{slot_u}"
        p.op("g", lambda e, ot=ot, slot_g=slot_g: e.dma_start(
            wflat[:, slot_g * 2048:slot_g * 2048 + 1024]
            .rearrange("p (a b) -> p a b", a=8),
            ctx["wq_lg"][:, ot, :, :].rearrange("a p o -> p a o")),
            waits=(mmarks.get(ot - 2, t0p),), incs=(sg,), dma=True)
        p.op("g", lambda e, ot=ot, slot_u=slot_u: e.dma_start(
            wflat[:, slot_u * 2048:slot_u * 2048 + 1024]
            .rearrange("p (a b) -> p a b", a=8),
            ctx["wq_lu"][:, ot, :, :].rearrange("a p o -> p a o")),
            incs=(su,), dma=True)
        wmk = [(sg, p.semv[sg]), (su, p.semv[su])]
        for c in range(4):
            ts = slice(c * 512, (c + 1) * 512)
            for slot, bidx, stv, scr, sy, mc in (
                    (slot_g, 0, gav, ctx["gtr"], f"dmy{c % 2}", 6),
                    (slot_u, 1, ubv, ctx["upr"], f"dmy{2 + c % 2}", 5)):
                bk = bank[bidx]
                bk.mm([lambda e, i=i, slot=slot, ts=ts, bk=bk:
                       e.matmul(bk.ap[:], wflat[:, slot * 2048 + i * 128:slot * 2048 + (i + 1) * 128],
                                xq[:, i, ts], start=(i == 0), stop=(i == NDT - 1))
                       for i in range(NDT)],
                      extra_waits=tuple(wmk) + ((qm,) if ot == 0 and c == 0 else ()))
                st = stv[:, c % 2, :]
                bk.read("v", lambda e, bk=bk, st=st: e.tensor_scalar_mul(
                    st, bk.ap[:], cpool[:, 8:9]),
                    extra_waits=(p.mark("v"), (sy, p.semv[sy])))
                em = p.mark("v")
                p.op("y", lambda e, ot=ot, ts=ts, st=st, scr=scr: e.dma_start(
                    scr[ot, :, ts], st), waits=(em,), incs=(sy,), dma=True)
                p.op("v", lambda e, st=st: e.tensor_tensor(
                    statA[:, 0:512], st, st, ALU.mult),
                    waits=(p.mark("v"),), incs=("fin_v",))
                p.op("v", lambda e: e.tensor_reduce(col2d(smf[:, 7:8]),
                                                    statA[:, 0:512],
                                                    axis=AXX, op=ALU.max),
                     waits=(p.mark("v"),), incs=("fin_v",))
                p.op("v", lambda e, mc=mc: e.tensor_max(smf[:, mc:mc + 1],
                                                        smf[:, mc:mc + 1],
                                                        smf[:, 7:8]),
                     waits=(p.mark("v"),), incs=("fin_v",))
        mmarks[ot] = p.mark("t")
    _sqrt_col(ctx, smf[:, 5:7], 2)
    arm2 = _allreduce_max(ctx, [smf[:, 5:6], smf[:, 6:7]])
    p.op("v", lambda e: e.tensor_scalar_mul(cpool[:, 10:11], lcol[:, 64:65],
                                            1.0 / 32767.0), waits=(arm2,),
         incs=("fin_v",))
    p.op("v", lambda e: e.tensor_scalar_mul(cpool[:, 12:13], lcol[:, 65:66],
                                            1.0 / 32767.0),
         waits=(p.mark("v"),), incs=("fin_v",))
    p.op("v", lambda e: e.reciprocal(cpool[:, 11:12], cpool[:, 10:11]),
         waits=(p.mark("v"),), incs=("fin_v",))
    p.op("v", lambda e: e.reciprocal(cpool[:, 13:14], cpool[:, 12:13]),
         waits=(p.mark("v"),), incs=("fin_v",))
    p.op("v", lambda e: e.memset(smf[:, 5:6], 0.0), waits=(p.mark("v"),),
         incs=("fin_v",))
    sc2 = p.mark("v")

    p.tag = f"f{idx}p2"
    spw = [(f"dmy{i}", p.semv[f"dmy{i}"]) for i in range(4)]
    for k in range(96):
        c, ot = divmod(k, NIT)
        ts = slice(c * 512, (c + 1) * 512)
        gld = gav[:, 2 + k % 2, :]
        uld = ubv[:, 2 + k % 2, :]
        sgl, sul = f"dms{k % 2}", f"dms{2 + k % 2}"
        syg, syu = f"dmy{k % 2}", f"dmy{2 + k % 2}"
        ldw = (spw if k < 2 else []) + [(syg, p.semv[syg]), (syu, p.semv[syu])]
        p.op("g", lambda e, ot=ot, ts=ts, gld=gld: e.dma_start(
            gld, ctx["gtr"][ot, :, ts]),
            waits=tuple(ldw), incs=(sgl,), dma=True)
        p.op("g", lambda e, ot=ot, ts=ts, uld=uld: e.dma_start(
            uld, ctx["upr"][ot, :, ts]), incs=(sul,), dma=True)
        ldm = [(sgl, p.semv[sgl]), (sul, p.semv[sul])]
        p.op("v", lambda e, gld=gld: e.tensor_scalar(
            statA[:, 0:512], gld, cpool[:, 13:14], MAGIC, ALU.mult, ALU.add),
            waits=tuple(ldm) + ((sc2,) if k < 2 else (p.mark("v"),)),
            incs=("fin_v",))
        p.op("v", lambda e: e.tensor_scalar(statA[:, 0:512], statA[:, 0:512],
                                            MAGIC, -32768.0, ALU.subtract, ALU.max),
             waits=(p.mark("v"),), incs=("fin_v",))
        p.op("v", lambda e: e.tensor_scalar(statA[:, 0:512], statA[:, 0:512],
                                            32767.0, 1.0, ALU.min, ALU.mult),
             waits=(p.mark("v"),), incs=("fin_v",))
        p.op("v", lambda e, gld=gld: e.tensor_scalar_mul(gld, statA[:, 0:512],
                                                         cpool[:, 12:13]),
             waits=(p.mark("v"),), incs=("fin_v",))
        gdm = p.mark("v")
        p.op("s", lambda e, gld=gld: e.activation(statB[:, 0:512], gld,
                                                  AF.Sigmoid, bias=0.0, scale=1.0),
             waits=(gdm, p.mark("v")), incs=("fin_s",))
        sgm = p.mark("s")
        p.op("v", lambda e, uld=uld: e.tensor_scalar(
            statA[:, 512:1024], uld, cpool[:, 11:12], MAGIC, ALU.mult, ALU.add),
            waits=(p.mark("v"),), incs=("fin_v",))
        p.op("v", lambda e: e.tensor_scalar(statA[:, 512:1024], statA[:, 512:1024],
                                            MAGIC, -32768.0, ALU.subtract, ALU.max),
             waits=(p.mark("v"),), incs=("fin_v",))
        p.op("v", lambda e: e.tensor_scalar_min(statA[:, 512:1024],
                                                statA[:, 512:1024], 32767.0),
             waits=(p.mark("v"),), incs=("fin_v",))
        p.op("v", lambda e, uld=uld: e.tensor_scalar_mul(uld, statA[:, 512:1024],
                                                         cpool[:, 10:11]),
             waits=(p.mark("v"),), incs=("fin_v",))
        p.op("v", lambda e: e.tensor_scalar(statB[:, 0:512], statB[:, 0:512],
                                            65536.0, MAGIC, ALU.mult, ALU.add),
             waits=(sgm, p.mark("v")), incs=("fin_v",))
        p.op("v", lambda e: e.tensor_scalar(statB[:, 0:512], statB[:, 0:512],
                                            MAGIC, 0.0, ALU.subtract, ALU.max),
             waits=(p.mark("v"),), incs=("fin_v",))
        p.op("v", lambda e: e.tensor_scalar(statB[:, 0:512], statB[:, 0:512],
                                            65535.0, 1.0 / 65536.0, ALU.min,
                                            ALU.mult),
             waits=(p.mark("v"),), incs=("fin_v",))
        p.op("v", lambda e, gld=gld: e.tensor_tensor(gld, gld, statB[:, 0:512],
                                                     ALU.mult),
             waits=(p.mark("v"),), incs=("fin_v",))
        p.op("v", lambda e, gld=gld: e.tensor_tensor(statA[:, 0:512], gld, gld,
                                                     ALU.mult),
             waits=(p.mark("v"),), incs=("fin_v",))
        p.op("v", lambda e: e.tensor_reduce(col2d(smf[:, 7:8]), statA[:, 0:512],
                                            axis=AXX, op=ALU.max),
             waits=(p.mark("v"),), incs=("fin_v",))
        p.op("v", lambda e: e.tensor_max(smf[:, 5:6], smf[:, 5:6], smf[:, 7:8]),
             waits=(p.mark("v"),), incs=("fin_v",))
        g2m = p.mark("v")
        p.op("y", lambda e, ot=ot, ts=ts, gld=gld: e.dma_start(
            ctx["gtr"][ot, :, ts], gld), waits=(g2m,), incs=(syg,), dma=True)
        p.op("y", lambda e, ot=ot, ts=ts, uld=uld: e.dma_start(
            ctx["upr"][ot, :, ts], uld), incs=(syu,), dma=True)
    _sqrt_col(ctx, smf[:, 5:6], 3)
    arm3 = _allreduce_max(ctx, [smf[:, 5:6]])
    p.op("v", lambda e: e.tensor_scalar_mul(cpool[:, 10:11], lcol[:, 64:65],
                                            1.0 / 32767.0), waits=(arm3,),
         incs=("fin_v",))
    p.op("v", lambda e: e.reciprocal(cpool[:, 11:12], cpool[:, 10:11]),
         waits=(p.mark("v"),), incs=("fin_v",))
    p.op("v", lambda e: e.memset(smf[:, 6:7], 0.0), waits=(p.mark("v"),),
         incs=("fin_v",))
    sc3 = p.mark("v")

    p.tag = f"f{idx}p3"
    spw = [(f"dmy{i}", p.semv[f"dmy{i}"]) for i in range(4)]
    for k in range(96):
        c, ot = divmod(k, NIT)
        ts = slice(c * 512, (c + 1) * 512)
        gld = gav[:, 2 + k % 2, :]
        uld = ubv[:, 2 + k % 2, :]
        sgl, sul = f"dms{k % 2}", f"dms{2 + k % 2}"
        syg = f"dmy{k % 2}"
        ldw = (spw if k < 2 else []) + [(syg, p.semv[syg])]
        p.op("g", lambda e, ot=ot, ts=ts, gld=gld: e.dma_start(
            gld, ctx["gtr"][ot, :, ts]),
            waits=tuple(ldw), incs=(sgl,), dma=True)
        p.op("g", lambda e, ot=ot, ts=ts, uld=uld: e.dma_start(
            uld, ctx["upr"][ot, :, ts]), incs=(sul,), dma=True)
        ldm = [(sgl, p.semv[sgl]), (sul, p.semv[sul])]
        p.op("v", lambda e, gld=gld: e.tensor_scalar(
            statA[:, 0:512], gld, cpool[:, 11:12], MAGIC, ALU.mult, ALU.add),
            waits=tuple(ldm) + ((sc3,) if k < 2 else (p.mark("v"),)),
            incs=("fin_v",))
        p.op("v", lambda e: e.tensor_scalar(statA[:, 0:512], statA[:, 0:512],
                                            MAGIC, -32768.0, ALU.subtract, ALU.max),
             waits=(p.mark("v"),), incs=("fin_v",))
        p.op("v", lambda e: e.tensor_scalar(statA[:, 0:512], statA[:, 0:512],
                                            32767.0, 1.0, ALU.min, ALU.mult),
             waits=(p.mark("v"),), incs=("fin_v",))
        p.op("v", lambda e: e.tensor_scalar_mul(statA[:, 0:512], statA[:, 0:512],
                                                cpool[:, 10:11]),
             waits=(p.mark("v"),), incs=("fin_v",))
        p.op("v", lambda e, gld=gld, uld=uld: e.tensor_tensor(
            gld, statA[:, 0:512], uld, ALU.mult),
             waits=(p.mark("v"),), incs=("fin_v",))
        p.op("v", lambda e, gld=gld: e.tensor_tensor(statA[:, 0:512], gld, gld,
                                                     ALU.mult),
             waits=(p.mark("v"),), incs=("fin_v",))
        p.op("v", lambda e: e.tensor_reduce(col2d(smf[:, 7:8]), statA[:, 0:512],
                                            axis=AXX, op=ALU.max),
             waits=(p.mark("v"),), incs=("fin_v",))
        p.op("v", lambda e: e.tensor_max(smf[:, 6:7], smf[:, 6:7], smf[:, 7:8]),
             waits=(p.mark("v"),), incs=("fin_v",))
        o3 = p.mark("v")
        p.op("y", lambda e, ot=ot, ts=ts, gld=gld: e.dma_start(
            ctx["gtr"][ot, :, ts], gld), waits=(o3,), incs=(syg,), dma=True)
    _sqrt_col(ctx, smf[:, 6:7], 4)
    arm4 = _allreduce_max(ctx, [smf[:, 6:7]])
    p.op("v", lambda e: e.tensor_scalar_mul(cpool[:, 12:13], lcol[:, 64:65],
                                            1.0 / 32767.0), waits=(arm4,),
         incs=("fin_v",))
    p.op("v", lambda e: e.reciprocal(cpool[:, 13:14], cpool[:, 12:13]),
         waits=(p.mark("v"),), incs=("fin_v",))
    sc4 = p.mark("v")

    p.tag = f"f{idx}p4"
    spw = [(f"dmy{i}", p.semv[f"dmy{i}"]) for i in range(4)]
    cmark = {}
    wdv = wflat.rearrange("p (s x y) -> p s x y", s=2, x=32)
    for c in range(4):
        ts = slice(c * 512, (c + 1) * 512)
        buf = gav if c % 2 == 0 else ubv
        sld = f"dms{c % 2}"
        ldw = (spw if c == 0 else []) + [(sld, p.semv[sld]),
                                         cmark.get(c - 2, p.mark("t") if c < 2 else None)]
        ldw = [w for w in ldw if w is not None]
        for ot in range(NIT):
            p.op("g", lambda e, ot=ot, ts=ts, buf=buf: e.dma_start(
                buf[:, ot, :], ctx["gtr"][ot, :, ts]),
                waits=tuple(ldw) + ((sc4,) if c < 2 and ot == 0 else ())
                if ot == 0 else (), incs=(sld,), dma=True)
        ldall = [(sld, p.semv[sld])]
        for ot in range(NIT):
            p.op("v", lambda e, ot=ot, buf=buf: e.tensor_scalar(
                statA[:, 0:512], buf[:, ot, :], cpool[:, 13:14], MAGIC,
                ALU.mult, ALU.add),
                waits=tuple(ldall) + ((sc4,) if c == 0 and ot == 0 else (p.mark("v"),)),
                incs=("fin_v",))
            p.op("v", lambda e: e.tensor_scalar(statA[:, 0:512], statA[:, 0:512],
                                                MAGIC, -32768.0, ALU.subtract,
                                                ALU.max),
                 waits=(p.mark("v"),), incs=("fin_v",))
            p.op("v", lambda e, ot=ot, buf=buf: e.tensor_scalar_min(
                buf[:, ot, :], statA[:, 0:512], 32767.0),
                 waits=(p.mark("v"),), incs=("fin_v",))
        oqm = p.mark("v")
        wmms = {}
        for mt in range(NDT):
            sd = f"dms{4 + mt % 2}"
            p.op("g", lambda e, mt=mt, wdv=wdv: e.dma_start(
                wdv[:, mt % 2, 0:24, :],
                ctx["wq_ld"][:, mt, :, :].rearrange("a p o -> p a o")),
                waits=(wmms.get(mt - 2, p.mark("t")),), incs=(sd,), dma=True)
            wmk = [(sd, p.semv[sd])]
            bk = bank[mt % 2]
            bk.mm([lambda e, it=it, mt=mt, bk=bk, wdv=wdv, buf=buf:
                   e.matmul(bk.ap[:], wdv[:, mt % 2, it, :],
                            buf[:, it, :], start=(it == 0), stop=(it == NIT - 1))
                   for it in range(NIT)],
                  extra_waits=tuple(wmk) + ((oqm,) if mt < 2 else ()))
            wmms[mt] = p.mark("t")
            bk.read("v", lambda e, mt=mt, ts=ts, bk=bk: e.scalar_tensor_tensor(
                xres[:, mt, ts], bk.ap[:], cpool[:, 12:13], xres[:, mt, ts],
                ALU.mult, ALU.add), extra_waits=(p.mark("v"),))
        cmark[c] = p.mark("t")
        p.op("v", lambda e: e.drain(), incs=("fin_v",))
    return p.mark("v")


def _sqrt_col(ctx, ap, _tag):
    p = ctx["p"]
    p.op("v", lambda e, ap=ap: e.tensor_scalar_max(ap, ap, 1e-24),
         waits=(p.mark("v"),), incs=("fin_v",))
    vq = p.mark("v")
    p.op("s", lambda e, ap=ap: e.activation(ap, ap, AF.Ln, bias=0.0, scale=1.0),
         waits=(vq,), incs=("fin_s",))
    p.op("s", lambda e, ap=ap: e.activation(ap, ap, AF.Exp, bias=0.0, scale=0.5),
         waits=(p.mark("s"),), incs=("fin_s",))
    sm = p.mark("s")
    p.op("v", lambda e: e.drain(), waits=(sm,), incs=("fin_v",))


def _lpbq_out_ap(dst, ot, ic):
    nt = dst.shape[1]
    base = ((ic * 4) * nt + ot) * 128 * 128
    return bass.AP(dst.tensor, base, [[128, 128], [nt * 128 * 128, 4], [1, 128]])


def _allreduce_max(ctx, cols):
    nc, p = ctx["nc"], ctx["p"]
    lcol, aux = ctx["lcol"], ctx["aux"]
    tbank, bank = ctx["tbank"], ctx["bank"]
    ident, ones_row = ctx["ident"], ctx["ones_row"]
    n = len(cols)
    for j, colap in enumerate(cols):
        p.op("v", lambda e, j=j, colap=colap: e.tensor_copy(aux[:, j:j + 1], colap),
             waits=(p.mark("v"), p.mark("s")), incs=("fin_v",))
    p.op("v", lambda e: e.memset(aux[0:1, 4:8], 0.0),
         waits=(p.mark("v"),), incs=("fin_v",))
    cm = p.mark("v")
    tbank.mm([lambda e, j=j: e.transpose(
        tbank.ap[0:1, j * 128:(j + 1) * 128], aux[:, j:j + 1], ident[:])
        for j in range(n)], extra_waits=(cm,))
    for j in range(n):
        tbank.read("v", lambda e, j=j: e.tensor_reduce(
            col2d(aux[0:1, 4 + j:5 + j]), tbank.ap[0:1, j * 128:(j + 1) * 128],
            axis=AXX, op=ALU.max))
    p.op("v", lambda e: e.drain(), incs=("fin_v",))
    rm = p.mark("v")
    p.op("y", lambda e: e.dma_start(ctx["cc_in"][0:1, 0:4], aux[0:1, 4:8]),
         waits=(rm, ("dma", p.semv["dma"])), incs=("dma",), dma=True)
    dm = [("dma", p.semv["dma"])]
    p.op("g", lambda e: e.collective_compute(
        "AllReduce", ALU.max, replica_groups=[list(range(N_CORES))],
        ins=[ctx["cc_in"][0:1, 0:4]], outs=[ctx["cc_out"][0:1, 0:4]]),
        waits=dm, incs=("fin_g",))
    gm = p.mark("g")
    p.op("y", lambda e: e.dma_start(aux[0:1, 8:12], ctx["cc_out"][0:1, 0:4]),
         waits=(gm, ("dma", p.semv["dma"])), incs=("dma",), dma=True)
    dm2 = [("dma", p.semv["dma"])]
    bk = bank[2]
    bk.mm([lambda e, n=n, bk=bk: e.matmul(bk.ap[:, 0:n], ones_row[0:1, 0:128],
                                          aux[0:1, 8:8 + n], start=True, stop=True)],
          extra_waits=tuple(dm2))
    bk.read("v", lambda e, n=n, bk=bk: e.tensor_copy(lcol[:, 64:64 + n],
                                                     bk.ap[:, 0:n]))
    p.op("v", lambda e: e.drain(), incs=("fin_v",))
    return p.mark("v")


def emit_l1_attn_b(ctx, idx):
    nc, p, X = ctx["nc"], ctx["p"], ctx["X"]
    xloc = ctx["xn"]
    bigA, bigB = ctx["bigA"], ctx["bigB"]
    aux = ctx["aux"]
    wbuf, small, smf = ctx["wbuf"], ctx["small"], ctx["smf"]
    lcol = ctx["lcol"]
    xres = ctx["xres"]
    ident, ones_row = ctx["ident"], ctx["ones_row"]
    bank, dbank, tbank = ctx["bank"], ctx["dbank"], ctx["tbank"]
    DI = 2048
    NDI = DI // 128

    if "xa_scr" not in ctx:
        ctx["xa_scr"] = nc.dram_tensor("xa_scr", [NDI, 128, T], BF16).ap()
        ctx["y_scr"] = nc.dram_tensor("y_scr", [NDI, 128, T], BF16).ap()
    xa_scr, y_scr = ctx["xa_scr"], ctx["y_scr"]

    lnm = emit_layernorm(ctx)
    for i in range(NDT):
        p.op("v", lambda e, i=i: e.tensor_add(xres[:, i, :], xres[:, i, :],
                                              xloc[:, i, :]),
             waits=(lnm,) if i == 0 else (), incs=("fin_v",) if i == NDT - 1 else ())
    p.op("v", lambda e: e.drain(), incs=("fin_v",))

    wflat = wbuf[:].rearrange("p a t -> p (a t)")
    selnB = wflat[0:16, 4096:6144]
    selnC = wflat[32:48, 4096:6144]
    wdtT = wflat[64:128, 6144:8192]
    p.op("g", lambda e: e.dma_start(wflat[0:48, 4096:6144], X["c_seln"]),
         waits=(p.mark("t"),), incs=("dmag",), dma=True)
    p.op("v", lambda e: e.drain(), waits=[("dmag", p.semv["dmag"])],
         incs=("fin_v",))

    xdbl = small[:, 0:T]
    xiw = small[:, 2048:2568]
    xaw = small[:, 2568:3080]
    zsw = small[:, 3080:3592]
    carry = smf[:, 0:16]

    for L in range(5):
        lm = p.mark("v")
        p.op("y", lambda e, L=L: e.dma_start(lcol[:, 0:256],
                                             X["L1ab_A_pp"][L]),
             waits=(lm,), incs=("dma",), dma=True)
        p.op("y", lambda e, L=L: e.dma_start(lcol[:, 256:368],
                                             X["L1ab_cols"][L]),
             incs=("dma",), dma=True)
        dm = [("dma", p.semv["dma"])]
        p.op("s", lambda e: e.activation(lcol[:, 0:256], lcol[:, 0:256], AF.Exp,
                                         bias=0.0, scale=1.0),
             waits=dm, incs=("fin_s",))
        sm = p.mark("s")
        p.op("v", lambda e: e.tensor_scalar_mul(lcol[:, 0:256], lcol[:, 0:256], -1.0),
             waits=(sm,), incs=("fin_v",))
        acm = p.mark("v")
        p.op("g", lambda e, L=L: e.dma_start(wdtT, X["L1ab_wdtT"][L]),
             waits=(p.mark("t"),), incs=("dms7",), dma=True)
        wdtm = [("dms7", p.semv["dms7"])]

        p.tag = f"s6a{L}"
        xdbk = [dbank[0], dbank[0], dbank[1], dbank[1]]
        diagb = [aux[:, 0:512], aux[:, 768:1280]]
        wxb = [aux[:, 512:640], aux[:, 640:768]]
        lastwin = {}
        lastcnv = {}
        lastwx = {}
        t0a, v0a = p.mark("t"), p.mark("v")
        for dt in range(NDI):
            slot = dt % 2
            p.op("g", lambda e, L=L, dt=dt, slot=slot: e.dma_start(
                wflat[:, slot * 2048:slot * 2048 + 1024]
                .rearrange("p (a b) -> p a b", a=8),
                X["L1ab_win"][L, dt].rearrange("a p o -> p a o")),
                waits=(lastwin.get(dt - 2, t0a), v0a), incs=(f"dms{slot}",),
                dma=True)
            wm = [(f"dms{slot}", p.semv[f"dms{slot}"])]
            p.op("g", lambda e, L=L, dt=dt, slot=slot: e.dma_start(
                wxb[slot], X["L1ab_wx"][L, dt]),
                waits=(lastwx.get(dt - 2, t0a), v0a), incs=("dms3",), dma=True)
            wxm = [("dms3", p.semv["dms3"])]
            dgm0 = p.mark("v")
            for j in range(4):
                p.op("v", lambda e, j=j, dt=dt, slot=slot: e.tensor_tensor(
                    diagb[slot][:, j * 128:(j + 1) * 128], ident[:],
                    cbc(lcol[:, 304 + dt * 4 + j:304 + dt * 4 + j + 1], 128),
                    ALU.mult),
                    waits=(dgm0, acm, lastcnv.get(dt - 2, t0a)) if j == 0 else (),
                    incs=("fin_v",) if j == 3 else ())
            dgm = p.mark("v")
            if dt == 0:
                p.op("v", lambda e: e.memset(xiw[:, 0:4], 0.0), incs=("fin_v",))
            for c4 in range(4):
                ts = slice(c4 * 512, (c4 + 1) * 512)
                bk = bank[c4 % 2]
                bk.mm([lambda e, i=i, L=L, dt=dt, ts=ts, bk=bk, slot=slot:
                       e.matmul(bk.ap[:], wflat[:, slot * 2048 + i * 128:slot * 2048 + (i + 1) * 128],
                                xloc[:, i, ts], start=(i == 0), stop=(i == NDT - 1))
                       for i in range(NDT)],
                      extra_waits=tuple(wm) if c4 == 0 else ())
                if c4 == 3:
                    lastwin[dt] = p.mark("t")
                bk.read("v", lambda e, bk=bk: e.tensor_copy(xiw[:, 4:516], bk.ap[:]),
                        extra_waits=(p.mark("v"), p.mark("t")))
                xim = p.mark("v")
                cb2 = bank[2]
                cb2.mm([lambda e, j=j, cb2=cb2, slot=slot: e.matmul(
                    cb2.ap[:], diagb[slot][:, j * 128:(j + 1) * 128],
                    xiw[:, 1 + j:513 + j], start=(j == 0), stop=(j == 3))
                    for j in range(4)], extra_waits=(xim, dgm))
                if c4 == 3:
                    lastcnv[dt] = p.mark("t")
                cb2.read("s", lambda e, dt=dt, cb2=cb2: e.activation(
                    xaw[:], cb2.ap[:], AF.Silu,
                    bias=lcol[:, 272 + dt:273 + dt], scale=1.0),
                    extra_waits=(p.mark("v"), ("dms2", p.semv["dms2"])))
                ssm = p.mark("s")
                p.op("v", lambda e: e.tensor_copy(xiw[:, 0:4], xiw[:, 512:516]),
                     waits=(p.mark("s"),), incs=("fin_v",))
                p.op("g", lambda e, L=L, dt=dt, ts=ts: e.dma_start(
                    xa_scr[dt, :, ts], xaw[:]),
                    waits=(ssm,), incs=("dms2",), dma=True)
                xbk = xdbk[c4]
                half = (c4 % 2) * 512
                xbk.mm([lambda e, xbk=xbk, half=half, dt=dt, slot=slot: e.matmul(
                    xbk.ap[:, half:half + 512], wxb[slot], xaw[:],
                    start=(dt == 0), stop=(dt == NDI - 1))],
                    extra_waits=tuple(wxm) + (ssm,))
                if c4 == 3:
                    lastwx[dt] = p.mark("t")
        for c4 in range(4):
            xbk = xdbk[c4]
            half = (c4 % 2) * 512
            xbk.read("v", lambda e, xbk=xbk, c4=c4, half=half: e.tensor_copy(
                xdbl[:, c4 * 512:(c4 + 1) * 512], xbk.ap[:, half:half + 512]))
        p.op("v", lambda e: e.drain(), incs=("fin_v",))
        xdm = p.mark("v")

        p.tag = f"s6b{L}"
        p.op("v", lambda e: e.drain(), waits=[("dms2", p.semv["dms2"])],
             incs=("fin_v",))
        for ch in range(2):
            tsl = slice(ch * 1024, (ch + 1) * 1024)
            brep = bigB[:].rearrange("p a t -> p (a t)") \
                .rearrange("p (n t) -> p n t", n=16)
            crep = bigA[:].rearrange("p a t -> p (a t)") \
                .rearrange("p (n t) -> p n t", n=16)
            for which, rep, rbase in (("b", brep, 0), ("c", crep, 32)):
                seln = selnB if which == "b" else selnC
                for n in range(16):
                    bk = bank[2]
                    bk.mm([lambda e, n=n, rbase=rbase, tsl=tsl, bk=bk, seln=seln:
                           e.matmul(bk.ap[:], seln[:, n * 128:(n + 1) * 128],
                                    xdbl[rbase:rbase + 16, tsl.start:tsl.start + 512],
                                    start=True, stop=True)],
                        extra_waits=(xdm,) if n == 0 else ())
                    bk.read("v", lambda e, n=n, rep=rep, bk=bk: e.tensor_copy(
                        rep[:, n, 0:512], bk.ap[:]))
                    bk2 = bank[1]
                    bk2.mm([lambda e, n=n, rbase=rbase, tsl=tsl, bk2=bk2, seln=seln:
                            e.matmul(bk2.ap[:], seln[:, n * 128:(n + 1) * 128],
                                     xdbl[rbase:rbase + 16, tsl.start + 512:tsl.start + 1024],
                                     start=True, stop=True)],
                        extra_waits=(xdm,) if n == 0 else ())
                    bk2.read("v", lambda e, n=n, rep=rep, bk2=bk2: e.tensor_copy(
                        rep[:, n, 512:1024], bk2.ap[:]))
            p.op("v", lambda e: e.drain(), incs=("fin_v",))
            bcm = p.mark("v")

            hbuf = aux[:, 0:1024]
            hcbuf = aux[:, 1024:2048]
            dcol = aux[:, 2048:3072]
            ucol = aux[:, 3072:4096]
            xat = aux[:, 4096:5120]
            dan = aux[:, 5120:6144]
            for dt in range(NDI):
                bkd = dbank[0]
                bkd.mm([lambda e, dt=dt, tsl=tsl, s2=s2, bkd=bkd: e.matmul(
                    bkd.ap[:, s2 * 512:(s2 + 1) * 512],
                    wdtT[:, dt * 128:(dt + 1) * 128],
                    xdbl[64:128, tsl.start + s2 * 512:tsl.start + (s2 + 1) * 512],
                    start=True, stop=True)
                    for s2 in range(2)],
                    extra_waits=tuple(wdtm) + ((xdm,) if dt == 0 and ch == 0 else ()))
                bkd.read("v", lambda e, dt=dt, bkd=bkd: e.tensor_scalar_add(
                    ctx["statA"][:], bkd.ap[0:128, 0:1024], lcol[:, 256 + dt:257 + dt]),
                    extra_waits=(p.mark("v"),))
                p.op("v", lambda e: e.tensor_scalar_mul(ctx["statB"][:], ctx["statA"][:],
                                                        -1.0),
                     waits=(p.mark("v"),), incs=("fin_v",))
                p.op("v", lambda e: e.tensor_max(ctx["statB"][:], ctx["statB"][:],
                                                 ctx["statA"][:]),
                     waits=(p.mark("v"),), incs=("fin_v",))
                vsp = p.mark("v")
                p.op("s", lambda e: e.activation(ctx["statB"][:], ctx["statB"][:],
                                                 AF.Exp, bias=0.0, scale=-1.0),
                     waits=(vsp,), incs=("fin_s",))
                p.op("s", lambda e: e.activation(ctx["statB"][:], ctx["statB"][:],
                                                 AF.Ln, bias=1.0, scale=1.0),
                     waits=(p.mark("s"),), incs=("fin_s",))
                ssp = p.mark("s")
                p.op("v", lambda e: e.scalar_tensor_tensor(
                    dcol, ctx["statA"][:], 0.0, ctx["statB"][:], ALU.max, ALU.add),
                     waits=(ssp,), incs=("fin_v",))
                ddm = p.mark("v")
                p.op("g", lambda e, dt=dt, tsl=tsl: e.dma_start(xat, xa_scr[dt, :, tsl]),
                     waits=(p.mark("v"),), incs=("dms4",), dma=True)
                xam = [("dms4", p.semv["dms4"])]
                p.op("v", lambda e: e.tensor_tensor(ucol, dcol, xat, ALU.mult),
                     waits=tuple(xam) + (ddm,), incs=("fin_v",))
                um = p.mark("v")
                danb = [dan, small[:, 2048:3072]]
                hcb = [hcbuf, small[:, 3072:4096]]
                tm_dt = p.mark("t")
                ybk = dbank[1]
                scanmarks = {}
                ymmarks = {}
                for n in range(16):
                    danp, hcp = danb[n % 2], hcb[n % 2]
                    expw = [ddm, acm]
                    if n >= 2:
                        expw.append(scanmarks[n - 2])
                    else:
                        expw.append(("dms5", p.semv["dms5"]))
                    p.op("s", lambda e, n=n, dt=dt, danp=danp: e.activation(
                        danp, dcol, AF.Exp, bias=0.0,
                        scale=lcol[:, dt * 16 + n:dt * 16 + n + 1]),
                        waits=tuple(expw), incs=("fin_s",))
                    dam = p.mark("s")
                    p.op("v", lambda e, n=n, hcp=hcp: e.tensor_tensor(
                        hcp, ucol, brep[:, n, :], ALU.mult),
                        waits=(um, bcm, p.mark("v"),
                               ymmarks[n - 2] if n >= 2 else tm_dt),
                        incs=("fin_v",))
                    with nc.allow_low_precision(reason="scan state is fp32 internal; bf16 io"):
                        p.op("v", lambda e, n=n, ch=ch, danp=danp, hcp=hcp:
                             e.tensor_tensor_scan(
                                 hbuf, danp, hcp,
                                 0.0 if ch == 0 else carry[:, n:n + 1],
                                 ALU.mult, ALU.add),
                             waits=(dam, p.mark("v")),
                             incs=("fin_v",))
                    scanmarks[n] = p.mark("v")
                    if ch == 0:
                        p.op("v", lambda e, n=n: e.tensor_copy(carry[:, n:n + 1],
                                                               hbuf[:, 1023:1024]),
                             waits=(p.mark("v"),), incs=("fin_v",))
                    p.op("v", lambda e, n=n, hcp=hcp: e.tensor_tensor(
                        hcp, hbuf, crep[:, n, :], ALU.mult),
                         waits=(p.mark("v"),), incs=("fin_v",))
                    hcm = p.mark("v")
                    ybk.mm([lambda e, n=n, s2=s2, ybk=ybk, hcp=hcp: e.matmul(
                        ybk.ap[:, s2 * 512:(s2 + 1) * 512], ident[:],
                        hcp[:, s2 * 512:(s2 + 1) * 512],
                        start=(n == 0), stop=(n == 15))
                        for s2 in range(2)], extra_waits=(hcm,))
                    ymmarks[n] = p.mark("t")
                ybk.read("v", lambda e, dt=dt, ybk=ybk: e.scalar_tensor_tensor(
                    dan, xat, lcol[:, 288 + dt:289 + dt], ybk.ap[:],
                    ALU.mult, ALU.add),
                    extra_waits=(p.mark("v"), p.mark("s")))
                yem = p.mark("v")
                p.op("g", lambda e, dt=dt, tsl=tsl: e.dma_start(
                    y_scr[dt, :, tsl], dan),
                    waits=(yem,), incs=("dms5",), dma=True)
        p.op("v", lambda e: e.drain(), waits=[("dms5", p.semv["dms5"])],
             incs=("fin_v",))

        p.tag = f"s6c{L}"
        ygv = bigB[:].rearrange("p a t -> p (a t)")[:, 0:8192] \
            .rearrange("p (a t) -> p a t", a=16)
        zb = [aux[:, 0:512], aux[:, 512:1024]]
        yb = [aux[:, 1024:1536], aux[:, 1536:2048]]
        t0c, v0c = p.mark("t"), p.mark("v")
        lastz = {}
        gtm = {}
        lastw = {}
        prevwout = t0c
        for c4 in range(4):
            ts = slice(c4 * 512, (c4 + 1) * 512)
            for dt in range(NDI):
                k = c4 * NDI + dt
                slot = k % 2
                zw = max(lastz.get(k - 2, t0c), prevwout, key=lambda m: m[1])
                p.op("g", lambda e, L=L, dt=dt, slot=slot: e.dma_start(
                    wflat[:, slot * 2048:slot * 2048 + 1024]
                    .rearrange("p (a b) -> p a b", a=8),
                    X["L1ab_win"][L, 16 + dt].rearrange("a p o -> p a o")),
                    waits=(zw, v0c),
                    incs=(f"dms{slot}",), dma=True)
                wm = [(f"dms{slot}", p.semv[f"dms{slot}"])]
                bk = bank[c4 % 2]
                bk.mm([lambda e, i=i, ts=ts, bk=bk, slot=slot:
                       e.matmul(bk.ap[:], wflat[:, slot * 2048 + i * 128:slot * 2048 + (i + 1) * 128],
                                xloc[:, i, ts], start=(i == 0), stop=(i == NDT - 1))
                       for i in range(NDT)], extra_waits=tuple(wm))
                lastz[k] = p.mark("t")
                bk.read("s", lambda e, bk=bk, slot=slot: e.activation(
                    zb[slot], bk.ap[:], AF.Silu, bias=0.0, scale=1.0),
                        extra_waits=(gtm.get(k - 2, v0c),))
                zm = p.mark("s")
                sq = f"dms{6 + slot}"
                p.op("g", lambda e, dt=dt, ts=ts, slot=slot: e.dma_start(
                    yb[slot], y_scr[dt, :, ts]),
                    waits=(gtm.get(k - 2, v0c),), incs=(sq,), dma=True)
                ym = [(sq, p.semv[sq])]
                p.op("v", lambda e, dt=dt, slot=slot: e.tensor_tensor(
                    ygv[:, dt, :], yb[slot], zb[slot], ALU.mult),
                     waits=tuple(ym) + (zm,), incs=("fin_v",))
                gtm[k] = p.mark("v")
            ygm = p.mark("v")
            for mt in range(NDT):
                slot = mt % 2
                p.op("g", lambda e, L=L, mt=mt, slot=slot: e.dma_start(
                    wflat[:, slot * 2048:slot * 2048 + 2048]
                    .rearrange("p (a b) -> p a b", a=16),
                    X["L1ab_wout"][L, mt].rearrange("a p o -> p a o")),
                    waits=(lastw.get(mt - 2, lastz[4 * NDI - 1] if c4 == 3 else lastz[(c4 + 1) * NDI - 1]),),
                    incs=(f"dms{slot + 2}",), dma=True)
                wm = [(f"dms{slot + 2}", p.semv[f"dms{slot + 2}"])]
                bk = bank[2]
                bk.mm([lambda e, dt=dt2, mt=mt, slot=slot, bk=bk:
                       e.matmul(bk.ap[:], wflat[:, slot * 2048 + dt * 128:slot * 2048 + (dt + 1) * 128],
                                ygv[:, dt, :], start=(dt == 0), stop=(dt == NDI - 1))
                       for dt2 in range(NDI)],
                      extra_waits=tuple(wm) + (ygm,))
                lastw[mt] = p.mark("t")
                bk.read("v", lambda e, mt=mt, ts=ts, bk=bk: e.tensor_add(
                    xres[:, mt, ts], xres[:, mt, ts], bk.ap[:]),
                    extra_waits=(p.mark("v"),))
                p.op("v", lambda e, mt=mt, ts=ts: e.tensor_add(
                    xloc[:, mt, ts], xloc[:, mt, ts], bank[2].ap[:]),
                     waits=(p.mark("v"),), incs=("fin_v",))
                bank[2].rmarks.append(("fin_v", p.semv["fin_v"]))
            lastw = {}
            prevwout = p.mark("t")
        p.op("v", lambda e: e.drain(), incs=("fin_v",))
    return p.mark("v")



def prep_inputs(inputs, core):
    d = dict(host_constants())
    x = np.asarray(inputs["L0_attn_a_x"])[core]
    d["x_f"] = np.ascontiguousarray(x.T)
    d["L1_attn_a_WqT"] = np.ascontiguousarray(np.asarray(inputs["L1_attn_a_Wq"]).T)
    d["L1_attn_a_WkT"] = np.ascontiguousarray(np.asarray(inputs["L1_attn_a_Wk"]).T)
    d["L1_attn_a_WvT"] = np.ascontiguousarray(np.asarray(inputs["L1_attn_a_Wv"]).T)
    d["L0_ffn_w_gate_T"] = np.ascontiguousarray(np.asarray(inputs["L0_ffn_w_gate"]).T)
    d["L0_ffn_w_up_T"] = np.ascontiguousarray(np.asarray(inputs["L0_ffn_w_up"]).T)
    d["L0_ffn_w_down_T"] = np.ascontiguousarray(np.asarray(inputs["L0_ffn_w_down"]).T)
    perm = np.zeros(1024, dtype=np.int64)
    for par in range(2):
        for h in range(16):
            for i in range(32):
                op = (par * 4 + h // 4) * 128 + (h % 4) * 32 + i
                perm[op] = h * 64 + 2 * i + par
    d["L0_attn_b_WqT_perm"] = np.ascontiguousarray(
        np.asarray(inputs["L0_attn_b_Wq"]).T[:, perm])
    d["L0_attn_b_WkT_perm"] = np.ascontiguousarray(
        np.asarray(inputs["L0_attn_b_Wk"]).T[:, perm])
    d["L0_attn_b_WvT"] = np.ascontiguousarray(np.asarray(inputs["L0_attn_b_Wv"]).T)
    d["L0_attn_b_WoT"] = np.ascontiguousarray(np.asarray(inputs["L0_attn_b_Wo"]).T)
    WXP = np.asarray(inputs["L0_attn_a_x_proj_w"])
    pad = np.zeros((256, 1024), dtype=np.float32)
    pad[0:64] = WXP[64:128]
    pad[64:128] = WXP[0:64]
    pad[128:192] = WXP[128:192]
    d["L0aa_wxpT"] = np.ascontiguousarray(pad.T)
    d["L0aa_wdtT"] = np.ascontiguousarray(np.asarray(inputs["L0_attn_a_dt_proj_w"]).T)
    WOo = np.asarray(inputs["L0_attn_a_out_proj_w"])
    d["L0aa_wo"] = np.ascontiguousarray(WOo.T.reshape(8, 128, 8, 128)
                                        .transpose(2, 0, 1, 3))
    AL0 = np.asarray(inputs["L0_attn_a_A_log"])
    app = np.zeros((128, 512), dtype=np.float32)
    for kk in range(512):
        dtile, m = kk // 64, kk % 64
        for pp in range(128):
            dd = dtile * 128 + 2 * m + pp // 64
            nn2 = pp % 64
            app[pp, kk] = AL0[dd, nn2]
    d["L0aa_A_pp"] = app
    cols0 = np.zeros((128, 24), dtype=np.float32)
    cols0[:, 0:8] = np.asarray(inputs["L0_attn_a_norm_w"]).reshape(8, 128).T
    cols0[:, 8:16] = np.asarray(inputs["L0_attn_a_dt_proj_b"]).reshape(8, 128).T
    cols0[:, 16:24] = np.asarray(inputs["L0_attn_a_D"]).reshape(8, 128).T
    d["L0aa_cols"] = cols0
    seld = np.zeros((128, 64 * 128), dtype=np.float32)
    for m in range(64):
        for pp in range(128):
            seld[2 * m + pp // 64, m * 128 + pp] = 1.0
    d["c_seld"] = seld
    seln0 = np.zeros((128, 128), dtype=np.float32)
    for pp in range(128):
        seln0[pp % 64, pp] = 1.0
    d["c_seln0"] = seln0
    WI = np.asarray(inputs["L1_attn_b_W_in"])
    d["L1ab_win"] = np.ascontiguousarray(
        WI.transpose(0, 2, 1).reshape(5, 8, 128, 32, 128).transpose(0, 3, 1, 2, 4))
    WO = np.asarray(inputs["L1_attn_b_W_out"])
    d["L1ab_wout"] = np.ascontiguousarray(
        WO.transpose(0, 2, 1).reshape(5, 16, 128, 8, 128).transpose(0, 3, 1, 2, 4))
    WX = np.asarray(inputs["L1_attn_b_W_x"])
    wxp = np.zeros((5, 128, 2048), dtype=np.float32)
    wxp[:, 0:16] = WX[:, 64:80]
    wxp[:, 32:48] = WX[:, 80:96]
    wxp[:, 64:128] = WX[:, 0:64]
    d["L1ab_wx"] = np.ascontiguousarray(
        wxp.transpose(0, 2, 1).reshape(5, 16, 128, 128))
    d["L1ab_wdtT"] = np.ascontiguousarray(
        np.asarray(inputs["L1_attn_b_W_dt"]).transpose(0, 2, 1))
    AL = np.asarray(inputs["L1_attn_b_A_log"])
    d["L1ab_A_pp"] = np.ascontiguousarray(AL.reshape(5, 16, 128, 16)
                                          .transpose(0, 2, 1, 3).reshape(5, 128, 256))
    cols = np.zeros((5, 128, 112), dtype=np.float32)
    cols[:, :, 0:16] = np.asarray(inputs["L1_attn_b_b_dt"]).reshape(5, 16, 128).transpose(0, 2, 1)
    cols[:, :, 16:32] = np.asarray(inputs["L1_attn_b_conv_b"]).reshape(5, 16, 128).transpose(0, 2, 1)
    cols[:, :, 32:48] = np.asarray(inputs["L1_attn_b_D"]).reshape(5, 16, 128).transpose(0, 2, 1)
    CW = np.asarray(inputs["L1_attn_b_conv_w"]).reshape(5, 16, 128, 4)
    cols[:, :, 48:112] = CW.transpose(0, 2, 1, 3).reshape(5, 128, 64)
    d["L1ab_cols"] = np.ascontiguousarray(cols)
    d["L1_ffn_w_gate"] = np.asarray(inputs["L1_ffn_w_gate"])
    d["L1_ffn_w_up"] = np.asarray(inputs["L1_ffn_w_up"])
    d["L1_ffn_w_down"] = np.asarray(inputs["L1_ffn_w_down"])
    d["L1_attn_a_b_cat"] = np.concatenate([
        np.asarray(inputs["L1_attn_a_bq"]), np.asarray(inputs["L1_attn_a_bk"]),
        np.asarray(inputs["L1_attn_a_bv"])]).reshape(1, 3 * D).astype(np.float32)
    return d


SUBLAYERS = tuple(range(8))


def kernel(**inputs):
    per_core = [prep_inputs(inputs, c) for c in range(8)]
    nc = build_nc(per_core[0], SUBLAYERS)
    res = run_bass_kernel_spmd(nc, per_core, core_ids=list(range(8)))
    out = np.stack([r["out"].T for r in res.results], axis=0)
    return out.astype(np.float32)


# version 03
# speedup vs baseline: 1.2978x; 1.0314x over previous
import contextlib
import numpy as np
import ml_dtypes

import concourse.bass as bass
import concourse.mybir as mybir
from concourse.bass_utils import run_bass_kernel_spmd

F32 = mybir.dt.float32
BF16 = mybir.dt.bfloat16
AF = mybir.ActivationFunctionType
ALU = mybir.AluOpType
AXX = mybir.AxisListType.X

BF = ml_dtypes.bfloat16

B = 8
T = 2048
D = 1024
NDT = D // 128
NTT = T // 128
NTC = T // 512

MAGIC = float(3 << 22)
N_CORES = 8


def _ins(ap, axis, step, count):
    nap = [list(d) for d in ap.ap]
    nap.insert(axis, [step, count])
    return bass.AP(ap.tensor, ap.offset, nap)


def bcast(ap, axis, count):
    return _ins(ap, axis, 0, count)


def cbc(col_ap, n):
    return bass.AP(col_ap.tensor, col_ap.offset, [list(col_ap.ap[0]), [0, n]])


def col2d(ap):
    return _ins(ap, 2, 1, 1)



class Prog:

    ENGS = ("t", "v", "s", "g", "y")

    def __init__(self, nc, es):
        self.nc = nc
        self.es = es
        self.pend = {e: [] for e in self.ENGS}
        self.sems = {}
        self.semv = {}
        for e in self.ENGS:
            self.new_sem("fin_" + e)
        self.new_sem("dma")
        self.new_sem("dmag")
        self.new_sem("dmaw")
        for i in range(8):
            self.new_sem(f"dms{i}")
        for i in range(4):
            self.new_sem(f"dmy{i}")
        self.ninstr = 0
        self.tag = "init"

    def new_sem(self, name):
        self.sems[name] = self.es.enter_context(self.nc.semaphore(name))
        self.semv[name] = 0
        return name

    def mark(self, eng):
        return ("fin_" + eng, self.semv["fin_" + eng])

    def op(self, eng, emit, waits=(), incs=(), dma=False):
        wl = [(self.sems[w[0]], w[1]) for w in waits]
        il = []
        for sname in incs:
            n = 16 if dma else 1
            self.semv[sname] += n
            il.append((self.sems[sname], n))
        self.ninstr += 1

        tag = f"{self.tag}.{eng}" + ("d" if dma else "")

        def run(e, emit=emit, wl=wl, il=il, tag=tag):
            with self.nc.named_scope(tag):
                for h, v in wl:
                    if v > 0:
                        e.wait_ge(h, v)
                inst = emit(e)
                for h, n in il:
                    inst.then_inc(h, n)

        self.pend[eng].append(run)

    def barrier(self, producers=ENGS, consumers=ENGS):
        marks = []
        for e in producers:
            sname = "fin_" + e
            self.op(e, lambda en: en.drain(), incs=(sname,))
            marks.append((sname, self.semv[sname]))
        for e in consumers:
            self.op(e, lambda en: en.nop(), waits=marks)

    def flush(self, block):
        for e, attr in (("t", "tensor"), ("v", "vector"), ("s", "scalar"),
                        ("g", "gpsimd"), ("y", "sync")):
            ops = self.pend[e]
            if not ops:
                continue
            self.pend[e] = []

            def runall(eng, ops=ops):
                for f in ops:
                    f(eng)

            getattr(block, attr)(runall)


class Bank:

    def __init__(self, prog, ap):
        self.p = prog
        self.ap = ap
        self.wmark = None
        self.rmarks = []

    def mm(self, emits, extra_waits=()):
        p = self.p
        n = len(emits)
        for k, em in enumerate(emits):
            waits = tuple(self.rmarks) + tuple(extra_waits) if k == 0 else ()
            incs = ("fin_t",) if k == n - 1 else ()
            p.op("t", em, waits=waits, incs=incs)
        self.rmarks = []
        self.wmark = ("fin_t", p.semv["fin_t"])

    def read(self, eng, emit, extra_waits=(), incs_extra=()):
        p = self.p
        waits = ((self.wmark,) if self.wmark else ()) + tuple(extra_waits)
        sname = "fin_" + eng
        p.op(eng, emit, waits=waits, incs=(sname,) + tuple(incs_extra))
        self.rmarks.append((sname, p.semv[sname]))



def host_constants():
    c = {}
    c["c_ones_row"] = np.ones((1, 512), dtype=np.float32)
    c["c_ones_col"] = np.ones((128, 128), dtype=np.float32)
    c["c_ident"] = np.eye(128, dtype=np.float32)
    dk = 64
    inv_freq = 1.0 / (10000.0 ** (np.arange(0, dk, 2, dtype=np.float64) / dk))
    pos = np.arange(T, dtype=np.float64)
    fr = pos[:, None] * inv_freq[None, :]
    idx = np.arange(128) % 32
    c["c_rope_cos"] = np.ascontiguousarray(np.cos(fr).T[idx]).astype(np.float32)
    c["c_rope_sin"] = np.ascontiguousarray(np.sin(fr).T[idx]).astype(np.float32)
    tril = np.where(np.tril(np.ones((128, 128), dtype=bool)), 0.0, -1e9)
    c["c_trilm"] = tril.astype(np.float32)
    seln = np.zeros((48, 2048), dtype=np.float32)
    for n in range(16):
        seln[n, n * 128:(n + 1) * 128] = 1.0
        seln[32 + n, n * 128:(n + 1) * 128] = 1.0
    c["c_seln"] = seln
    return c



def build_nc(arrs, sublayers):
    nc = bass.Bass("TRN2", target_bir_lowering=False, debug=False)
    es = contextlib.ExitStack()
    p = Prog(nc, es)

    X = {}
    for name, arr in arrs.items():
        dt = F32 if arr.dtype == np.float32 else BF16
        X[name] = nc.dram_tensor(name, list(arr.shape), dt, kind="ExternalInput").ap()
    OUT = nc.dram_tensor("out", [D, T], F32, kind="ExternalOutput").ap()

    with es:
        build_body(nc, p, es, X, OUT, sublayers)
    return nc


def build_body(nc, p, es, X, OUT, sublayers):
    sb = lambda name, shape, dt=BF16: es.enter_context(nc.sbuf_tensor(name, shape, dt))

    ctx = dict(nc=nc, p=p, es=es, X=X, sb=sb)
    ctx["xres"] = sb("xres", [128, NDT, T], F32)
    ctx["bigA"] = sb("bigA", [128, NDT, T], BF16)
    ctx["bigB"] = sb("bigB", [128, NDT, T], BF16)
    ctx["bigC"] = sb("bigC", [128, NDT, T], BF16)
    ctx["xn"] = ctx["bigC"]
    ctx["statA"] = sb("statA", [128, 1024], F32)
    ctx["statB"] = sb("statB", [128, 1024], F32)
    ctx["wbuf"] = sb("wbuf", [128, NDT, 1024], BF16)
    ctx["small"] = sb("small", [128, 4096], BF16)
    ctx["aux"] = sb("aux", [128, 6144], BF16)
    ctx["lcol"] = sb("lcol", [128, 536], F32)
    ctx["smf"] = sb("smf", [128, 16], F32)
    ctx["ones_row"] = sb("ones_row", [1, 512])
    ctx["ones_col"] = sb("ones_col", [128, 128])
    ctx["ident"] = sb("ident", [128, 128])

    ctx["dbank"] = [Bank(p, es.enter_context(nc.psum_tensor(f"db{i}", [128, 1024], F32)))
                    for i in range(2)]
    ctx["bank"] = [Bank(p, es.enter_context(nc.psum_tensor(f"sb{i}", [128, 512], F32)))
                   for i in range(3)]
    ctx["tbank"] = Bank(p, es.enter_context(nc.psum_tensor("tb", [128, 1024], BF16)))

    cpool = sb("cpool", [128, 16], F32)
    ctx["cpool"] = cpool
    for i, vv in enumerate((1e-5, 1e-6, 1.0, 0.0)):
        p.op("v", lambda e, i=i, vv=vv: e.memset(cpool[:, i:i + 1], vv))
        nc.const_aps.register(F32, vv, cpool[:, i:i + 1])
    p.op("v", lambda e: e.drain(), incs=("fin_v",))

    for nm in ("ones_row", "ones_col", "ident"):
        dst = ctx[nm]
        p.op("g", lambda e, dst=dst, nm=nm: e.dma_start(dst[:], X["c_" + nm]),
             incs=("dmag",), dma=True)
    xf = X["x_f"].rearrange("(n q) t -> q n t", q=128)
    p.op("y", lambda e: e.dma_start(ctx["xres"][:], xf), incs=("dma",), dma=True)
    dmam = [("dma", p.semv["dma"]), ("dmag", p.semv["dmag"]), p.mark("v")]
    for e in ("t", "v", "s"):
        p.op(e, lambda en: en.nop(), waits=dmam)
    p.op("v", lambda en: en.nop(), waits=dmam)

    emitters = [emit_l0_attn_a, emit_l0_ffn, emit_l0_attn_b, emit_l0_ffn,
                emit_l1_attn_a, emit_l1_ffn, emit_l1_attn_b, emit_l1_ffn]
    for i in sublayers:
        p.tag = f"sub{i}"
        emitters[i](ctx, i)
        p.barrier(("t", "v", "s", "g"), ("t", "v", "s", "g"))
    p.tag = "fini"

    p.barrier(("v", "t", "s"), ("y",))
    ov = OUT.rearrange("(n q) t -> q n t", q=128)
    p.op("y", lambda e: e.dma_start(ov, ctx["xres"][:]), incs=("dma",), dma=True)
    p.op("y", lambda e: e.nop(), waits=[("dma", p.semv["dma"])])
    with nc.Block() as blk:
        p.flush(blk)



def emit_layernorm(ctx, eps=1e-5):
    p = ctx["p"]
    xres, xn = ctx["xres"], ctx["xn"]
    bigA = ctx["bigA"]
    statA, statB = ctx["statA"], ctx["statB"]
    ones_col = ctx["ones_col"]
    dbank = ctx["dbank"]

    for c in range(2):
        sl = slice(c * 1024, (c + 1) * 1024)
        for i in range(NDT):
            p.op("v", lambda e, i=i, sl=sl: e.tensor_copy(xn[:, i, sl], xres[:, i, sl]),
                 incs=("fin_v",) if i == NDT - 1 else ())
        cm = p.mark("v")
        for i in range(NDT):
            p.op("v", lambda e, i=i, sl=sl: e.tensor_tensor(
                bigA[:, i, sl], xn[:, i, sl], xn[:, i, sl], ALU.mult),
                waits=(cm,) if i == 0 else (),
                incs=("fin_v",) if i == NDT - 1 else ())
        vm = p.mark("v")
        for h, src in ((0, xn), (1, bigA)):
            bk = dbank[h]
            ems = []
            for s2 in range(2):
                for i in range(NDT):
                    ems.append(lambda e, i=i, s2=s2, src=src, c=c, bk=bk:
                               e.matmul(bk.ap[:, s2 * 512:(s2 + 1) * 512], ones_col[:],
                                        src[:, i, c * 1024 + s2 * 512:c * 1024 + (s2 + 1) * 512],
                                        start=(i == 0), stop=(i == NDT - 1)))
            bk.mm(ems, extra_waits=(vm,))
        dbank[0].read("v", lambda e: e.tensor_scalar_mul(statA[:], dbank[0].ap[:], 1.0 / D))
        p.op("v", lambda e: e.scalar_tensor_tensor(statB[:], statA[:], -1.0, statA[:],
                                                   ALU.mult, ALU.mult),
             waits=(p.mark("v"),), incs=("fin_v",))
        dbank[1].read("v", lambda e: e.scalar_tensor_tensor(
            statB[:], dbank[1].ap[:], 1.0 / D, statB[:], ALU.mult, ALU.add),
            extra_waits=(p.mark("v"),))
        vmk = p.mark("v")
        p.op("s", lambda e: e.activation(statB[:], statB[:], AF.Ln, bias=eps, scale=1.0),
             waits=(vmk,), incs=("fin_s",))
        p.op("s", lambda e: e.activation(statB[:], statB[:], AF.Exp, bias=0.0, scale=-0.5),
             waits=(p.mark("s"),), incs=("fin_s",))
        smk = p.mark("s")
        for i in range(NDT):
            p.op("v", lambda e, i=i, sl=sl: e.tensor_sub(xn[:, i, sl], xres[:, i, sl],
                                                         statA[:]),
                 waits=(smk,) if i == 0 else (),
                 incs=("fin_v",) if i == NDT - 1 else ())
        sm2 = p.mark("v")
        for i in range(NDT):
            p.op("v", lambda e, i=i, sl=sl: e.tensor_mul(xn[:, i, sl], xn[:, i, sl],
                                                         statB[:]),
                 waits=(sm2,) if i == 0 else (),
                 incs=("fin_v",) if i == NDT - 1 else ())
    return p.mark("v")


def emit_residual_add(ctx, src_bf):
    p = ctx["p"]
    xres = ctx["xres"]
    for i in range(NDT):
        p.op("v", lambda e, i=i: e.tensor_add(xres[:, i, :], xres[:, i, :], src_bf[:, i, :]),
             incs=("fin_v",) if i == NDT - 1 else ())
    return p.mark("v")



def emit_l1_attn_a(ctx, idx):
    nc, p, X = ctx["nc"], ctx["p"], ctx["X"]
    xn = ctx["xn"]
    k = ctx["bigA"]
    vtok = ctx["bigB"][:].rearrange("p a t -> p (a t)").rearrange("p (x y) -> p x y", x=NTT)
    qtile = ctx["aux"][:, 0:1024].rearrange("p (a b) -> p a b", a=8)
    brow = ctx["aux"][0:1, 1024:2048]
    wbuf = ctx["wbuf"]
    small, smf = ctx["small"], ctx["smf"]
    xres = ctx["xres"]
    dbank, bank, tbank = ctx["dbank"], ctx["bank"], ctx["tbank"]
    ones_row, ident = ctx["ones_row"], ctx["ident"]

    lnm = emit_layernorm(ctx)

    p.op("g", lambda e: e.dma_start(small[0:1, 0:3 * D], X["L1_attn_a_b_cat"]),
         incs=("dmaw",), dma=True)
    p.op("g", lambda e: e.dma_start(brow[0:1, 0:D],
                                    X["L1_attn_a_b_cat"][0:1, 0:D]),
         incs=("dmaw",), dma=True)
    bvrep = ctx["statA"]

    wv = {0: X["L1_attn_a_WqT"].rearrange("(n q) o -> q n o", q=128),
          1: X["L1_attn_a_WkT"].rearrange("(n q) o -> q n o", q=128),
          2: X["L1_attn_a_WvT"].rearrange("(n q) o -> q n o", q=128)}

    p.op("g", lambda e: e.dma_start(wbuf[:], wv[1]), incs=("dmaw",), dma=True)
    wm = [("dmaw", p.semv["dmaw"]), lnm]
    for ot in range(NDT):
        for c in range(2):
            bk = dbank[c]
            sl = slice(c * 1024, (c + 1) * 1024)
            ems = []
            for s2 in range(2):
                for i in range(NDT):
                    ems.append(lambda e, i=i, ot=ot, c=c, s2=s2, bk=bk:
                               e.matmul(bk.ap[:, s2 * 512:(s2 + 1) * 512],
                                        wbuf[:, i, ot * 128:(ot + 1) * 128],
                                        xn[:, i, c * 1024 + s2 * 512:c * 1024 + (s2 + 1) * 512],
                                        start=(i == 0), stop=False))
                ems.append(lambda e, ot=ot, s2=s2, bk=bk:
                           e.matmul(bk.ap[:, s2 * 512:(s2 + 1) * 512],
                                    small[0:1, D + ot * 128:D + (ot + 1) * 128],
                                    ones_row[0:1, 0:512],
                                    start=False, stop=True))
            bk.mm(ems, extra_waits=tuple(wm) if ot == 0 else ())
            bk.read("v", lambda e, ot=ot, sl=sl, bk=bk:
                    e.tensor_copy(k[:, ot, sl], bk.ap[:]))
    p.op("g", lambda e: e.dma_start(wbuf[:], wv[2]), waits=(p.mark("t"),),
         incs=("dmaw",), dma=True)
    wm = [("dmaw", p.semv["dmaw"])]
    dbank[0].mm([lambda e, s2=s2: e.matmul(dbank[0].ap[:, s2 * 512:(s2 + 1) * 512],
                                           ones_row[0:1, 0:128],
                                           small[0:1, 2 * D + s2 * 512:2 * D + (s2 + 1) * 512],
                                           start=True, stop=True)
                 for s2 in range(2)], extra_waits=tuple(wm))
    dbank[0].read("v", lambda e: e.tensor_copy(bvrep[:], dbank[0].ap[:]))
    bvm = p.mark("v")
    for tt in range(NTT):
        for c in range(2):
            bk = bank[c]
            sl = slice(c * 512, (c + 1) * 512)
            bk.mm([lambda e, i=i, tt=tt, sl=sl, bk=bk:
                   e.matmul(bk.ap[:], xn[:, i, tt * 128:(tt + 1) * 128],
                            wbuf[:, i, sl], start=(i == 0), stop=(i == NDT - 1))
                   for i in range(NDT)],
                  extra_waits=tuple(wm) if tt == 0 else ())
            bk.read("v", lambda e, tt=tt, sl=sl, bk=bk:
                    e.tensor_add(vtok[:, tt, sl], bk.ap[:], bvrep[:, sl]),
                    extra_waits=(bvm,))
    p.op("g", lambda e: e.dma_start(wbuf[:], wv[0]), waits=(p.mark("t"),),
         incs=("dmaw",), dma=True)
    wm = [("dmaw", p.semv["dmaw"])]
    p.op("t", lambda e: e.nop(), waits=wm)

    probs = small[:, 0:T]
    pt = small[:, T:2 * T]

    for tq in range(NTT):
        for ot in range(NDT):
            bk = bank[2]
            ems = [lambda e, i=i, ot=ot, tq=tq, bk=bk:
                   e.matmul(bk.ap[:, 0:128], wbuf[:, i, ot * 128:(ot + 1) * 128],
                            xn[:, i, tq * 128:(tq + 1) * 128], start=(i == 0), stop=False)
                   for i in range(NDT)]
            ems.append(lambda e, ot=ot, tq=tq, bk=bk:
                       e.matmul(bk.ap[:, 0:128], brow[0:1, ot * 128:(ot + 1) * 128],
                                ones_row[0:1, 0:128], start=False, stop=True))
            bk.mm(ems)
            bk.read("v", lambda e, ot=ot, bk=bk:
                    e.tensor_copy(qtile[:, ot, :], bk.ap[:, 0:128]))
        p.op("v", lambda e: e.drain(), incs=("fin_v",))
        qm = p.mark("v")
        for c in range(2):
            bk = dbank[c]
            ems = []
            for s2 in range(2):
                for i in range(NDT):
                    ems.append(lambda e, i=i, c=c, s2=s2, bk=bk:
                               e.matmul(bk.ap[:, s2 * 512:(s2 + 1) * 512], qtile[:, i, :],
                                        k[:, i, c * 1024 + s2 * 512:c * 1024 + (s2 + 1) * 512],
                                        start=(i == 0), stop=(i == NDT - 1)))
            bk.mm(ems, extra_waits=(qm,))
        dbank[0].read("v", lambda e: e.tensor_reduce(col2d(smf[:, 0:1]), dbank[0].ap[:],
                                                     axis=AXX, op=ALU.max))
        dbank[1].read("v", lambda e: e.tensor_reduce(col2d(smf[:, 1:2]), dbank[1].ap[:],
                                                     axis=AXX, op=ALU.max))
        p.op("v", lambda e: e.tensor_reduce(col2d(smf[:, 2:3]), smf[:, 0:2], axis=AXX,
                                            op=ALU.max), waits=(p.mark("v"),),
             incs=("fin_v",))
        p.op("v", lambda e: e.tensor_scalar_mul(smf[:, 3:4], smf[:, 2:3], -1.0),
             waits=(p.mark("v"),), incs=("fin_v",))
        vmk = p.mark("v")
        for c in range(2):
            dbank[c].read("s", lambda e, c=c: e.activation(
                probs[:, c * 1024:(c + 1) * 1024], dbank[c].ap[:], AF.Exp,
                bias=smf[:, 3:4], scale=1.0), extra_waits=(vmk,))
        smk = p.mark("s")
        p.op("v", lambda e: e.tensor_reduce(col2d(smf[:, 4:5]), probs[:], axis=AXX,
                                            op=ALU.add), waits=(smk,), incs=("fin_v",))
        p.op("v", lambda e: e.reciprocal(smf[:, 5:6], smf[:, 4:5]),
             waits=(p.mark("v"),), incs=("fin_v",))
        p.op("v", lambda e: e.tensor_scalar_mul(probs[:], probs[:], smf[:, 5:6]),
             waits=(p.mark("v"),), incs=("fin_v",))
        dmk = p.mark("v")
        for half in range(2):
            tbank.mm([lambda e, tk=tk: e.transpose(
                tbank.ap[:, (tk % 8) * 128:(tk % 8 + 1) * 128],
                probs[:, tk * 128:(tk + 1) * 128], ident[:])
                for tk in range(half * 8, (half + 1) * 8)], extra_waits=(dmk,))
            tbank.read("v", lambda e, half=half: e.tensor_copy(
                pt[:, half * 1024:(half + 1) * 1024], tbank.ap[:]))
        ptm = p.mark("v")
        for dv in range(NDT):
            bk = bank[2]
            bk.mm([lambda e, dv=dv, tk=tk, bk=bk:
                   e.matmul(bk.ap[:, 0:128], vtok[:, tk, dv * 128:(dv + 1) * 128],
                            pt[:, tk * 128:(tk + 1) * 128],
                            start=(tk == 0), stop=(tk == NTT - 1))
                   for tk in range(NTT)], extra_waits=(ptm,))
            bk.read("v", lambda e, dv=dv, tq=tq, bk=bk:
                    e.tensor_add(xres[:, dv, tq * 128:(tq + 1) * 128],
                                 xres[:, dv, tq * 128:(tq + 1) * 128], bk.ap[:, 0:128]))
    p.op("v", lambda e: e.drain(), incs=("fin_v",))
    return p.mark("v")



def emit_l0_attn_a(ctx, idx):
    nc, p, X = ctx["nc"], ctx["p"], ctx["X"]
    xn = ctx["xn"]
    xnn = ctx["bigA"]
    dlt = ctx["bigB"]
    wbuf, small, smf = ctx["wbuf"], ctx["small"], ctx["smf"]
    lcol = ctx["lcol"]
    xres = ctx["xres"]
    statA, statB = ctx["statA"], ctx["statB"]
    ones_col, ident = ctx["ones_col"], ctx["ident"]
    bank, dbank, tbank = ctx["bank"], ctx["dbank"], ctx["tbank"]
    wflat = wbuf[:].rearrange("p a t -> p (a t)")

    lnm = emit_layernorm(ctx)

    p.op("y", lambda e: e.dma_start(lcol[:, 0:512], X["L0aa_A_pp"]),
         waits=(lnm,), incs=("dma",), dma=True)
    p.op("y", lambda e: e.dma_start(lcol[:, 512:536], X["L0aa_cols"]),
         incs=("dma",), dma=True)
    dm = [("dma", p.semv["dma"])]
    p.op("s", lambda e: e.activation(lcol[:, 0:512], lcol[:, 0:512], AF.Exp,
                                     bias=0.0, scale=1.0), waits=dm, incs=("fin_s",))
    p.op("v", lambda e: e.tensor_scalar_mul(lcol[:, 0:512], lcol[:, 0:512], -1.0),
         waits=(p.mark("s"),), incs=("fin_v",))
    acm = p.mark("v")

    p.tag = "s0n"
    for c in range(2):
        sl = slice(c * 1024, (c + 1) * 1024)
        for i in range(NDT):
            p.op("v", lambda e, i=i, sl=sl: e.tensor_tensor(
                xnn[:, i, sl], xn[:, i, sl], xn[:, i, sl], ALU.mult),
                waits=(lnm, acm) if i == 0 else (),
                incs=("fin_v",) if i == NDT - 1 else ())
        vm = p.mark("v")
        bkr = dbank[0]
        ems = []
        for s2 in range(2):
            for i in range(NDT):
                ems.append(lambda e, i=i, s2=s2, c=c, bkr=bkr:
                           e.matmul(bkr.ap[:, s2 * 512:(s2 + 1) * 512], ones_col[:],
                                    xnn[:, i, c * 1024 + s2 * 512:c * 1024 + (s2 + 1) * 512],
                                    start=(i == 0), stop=(i == NDT - 1)))
        bkr.mm(ems, extra_waits=(vm,))
        bkr.read("v", lambda e, bkr=bkr: e.tensor_scalar_mul(statB[:], bkr.ap[:],
                                                             1.0 / D),
                 extra_waits=(p.mark("v"),))
        p.op("v", lambda e: e.drain(), incs=("fin_v",))
        vmk = p.mark("v")
        p.op("s", lambda e: e.activation(statB[:], statB[:], AF.Ln, bias=1e-6,
                                         scale=1.0), waits=(vmk,), incs=("fin_s",))
        p.op("s", lambda e: e.activation(statB[:], statB[:], AF.Exp, bias=0.0,
                                         scale=-0.5), waits=(p.mark("s"),),
             incs=("fin_s",))
        smk = p.mark("s")
        for i in range(NDT):
            p.op("v", lambda e, i=i, sl=sl: e.scalar_tensor_tensor(
                xnn[:, i, sl], xn[:, i, sl], lcol[:, 512 + i:513 + i], statB[:],
                ALU.mult, ALU.mult),
                waits=(smk,) if i == 0 else (), incs=("fin_v",))
            p.op("v", lambda e, i=i, sl=sl: e.tensor_scalar_mul(
                xn[:, i, sl], xn[:, i, sl], lcol[:, 528 + i:529 + i]),
                waits=(p.mark("v"),), incs=("fin_v",))
    p.op("v", lambda e: e.drain(), incs=("fin_v",))
    xdm = p.mark("v")

    p.tag = "s0o1"
    for mt in range(NDT):
        slot = mt % 2
        p.op("g", lambda e, mt=mt, slot=slot: e.dma_start(
            wflat[:, slot * 1024:slot * 1024 + 1024].rearrange("p (a b) -> p a b", a=8),
            X["L0aa_wo"][mt].rearrange("a p o -> p a o")),
            waits=(p.mark("t"),), incs=(f"dms{slot}",), dma=True)
        wm = [(f"dms{slot}", p.semv[f"dms{slot}"])]
        for c in range(NTC):
            ts = slice(c * 512, (c + 1) * 512)
            bk = bank[c % 2]
            bk.mm([lambda e, i=i, mt=mt, ts=ts, bk=bk, slot=slot:
                   e.matmul(bk.ap[:], wflat[:, slot * 1024 + i * 128:slot * 1024 + (i + 1) * 128],
                            xn[:, i, ts], start=(i == 0), stop=(i == NDT - 1))
                   for i in range(NDT)],
                  extra_waits=tuple(wm) + ((xdm,) if c == 0 else ()))
            bk.read("v", lambda e, mt=mt, ts=ts, bk=bk: e.tensor_add(
                xres[:, mt, ts], xres[:, mt, ts], bk.ap[:]))
    p.op("v", lambda e: e.drain(), incs=("fin_v",))

    p.tag = "s0pj"
    T1 = small[:, 0:T]
    T2 = small[:, 2048:2048 + T]
    p.op("g", lambda e: e.dma_start(
        wflat[:, 2048:4096].rearrange("p (a b) -> p a b", a=8),
        X["L0aa_wxpT"].rearrange("(n q) o -> q n o", q=128)),
        waits=(p.mark("t"),), incs=("dms2",), dma=True)
    wm = [("dms2", p.semv["dms2"])]
    for mt2, dst in ((0, T1), (1, T2)):
        for c in range(NTC):
            ts = slice(c * 512, (c + 1) * 512)
            bk = bank[2]
            bk.mm([lambda e, i=i, mt2=mt2, ts=ts, bk=bk:
                   e.matmul(bk.ap[:], wflat[:, 2048 + i * 256 + mt2 * 128:2048 + i * 256 + (mt2 + 1) * 128],
                            xnn[:, i, ts], start=(i == 0), stop=(i == NDT - 1))
                   for i in range(NDT)],
                  extra_waits=tuple(wm) if mt2 == 0 and c == 0 else ())
            bk.read("v", lambda e, dst=dst, ts=ts, bk=bk: e.tensor_copy(
                dst[:, ts], bk.ap[:]), extra_waits=(p.mark("v"),))
    p.op("v", lambda e: e.drain(), incs=("fin_v",))
    prm = p.mark("v")

    p.tag = "s0dl"
    p.op("g", lambda e: e.dma_start(wflat[64:128, 0:1024], X["L0aa_wdtT"]),
         waits=(p.mark("t"),), incs=("dms3",), dma=True)
    wm = [("dms3", p.semv["dms3"])]
    for mt in range(NDT):
        for c2 in range(2):
            ts = slice(c2 * 1024, (c2 + 1) * 1024)
            bkd = dbank[1]
            bkd.mm([lambda e, mt=mt, ts=ts, s2=s2, bkd=bkd: e.matmul(
                bkd.ap[:, s2 * 512:(s2 + 1) * 512],
                wflat[64:128, mt * 128:(mt + 1) * 128],
                T1[64:128, ts.start + s2 * 512:ts.start + (s2 + 1) * 512],
                start=True, stop=True)
                for s2 in range(2)],
                extra_waits=tuple(wm) + ((prm,) if mt == 0 and c2 == 0 else ()))
            bkd.read("v", lambda e, mt=mt, bkd=bkd: e.tensor_scalar_add(
                statA[:], bkd.ap[:], lcol[:, 520 + mt:521 + mt]),
                extra_waits=(p.mark("v"),))
            p.op("v", lambda e: e.tensor_scalar_mul(statB[:], statA[:], -1.0),
                 waits=(p.mark("v"),), incs=("fin_v",))
            p.op("v", lambda e: e.tensor_max(statB[:], statB[:], statA[:]),
                 waits=(p.mark("v"),), incs=("fin_v",))
            vsp = p.mark("v")
            p.op("s", lambda e: e.activation(statB[:], statB[:], AF.Exp, bias=0.0,
                                             scale=-1.0), waits=(vsp,), incs=("fin_s",))
            p.op("s", lambda e: e.activation(statB[:], statB[:], AF.Ln, bias=1.0,
                                             scale=1.0), waits=(p.mark("s"),),
                 incs=("fin_s",))
            ssp = p.mark("s")
            p.op("v", lambda e, mt=mt, ts=ts: e.scalar_tensor_tensor(
                dlt[:, mt, ts], statA[:], 0.0, statB[:], ALU.max, ALU.add),
                 waits=(ssp,), incs=("fin_v",))
    p.op("v", lambda e: e.drain(), incs=("fin_v",))
    dlm = p.mark("v")

    p.tag = "s0bc"
    seln0 = ctx["aux"][:, 0:128]
    p.op("g", lambda e: e.dma_start(seln0, X["c_seln0"]), incs=("dmag",), dma=True)
    sbm = [("dmag", p.semv["dmag"])]
    cflat = ctx["xn"][:].rearrange("p a t -> p (a t)")
    brep = cflat[:, 0:2048]
    crep = cflat[:, 2048:4096]
    ucol = cflat[:, 4096:6144]
    dac = cflat[:, 6144:8192]
    dbuc = cflat[:, 8192:10240]
    hbuf = cflat[:, 10240:12288]
    hcbuf = cflat[:, 12288:14336]
    selyb = cflat[:, 14336:14464]
    for half, (rep, src) in enumerate(((brep, T1), (crep, T2))):
        for c in range(NTC):
            ts = slice(c * 512, (c + 1) * 512)
            bk = bank[2]
            bk.mm([lambda e, src=src, ts=ts, bk=bk: e.matmul(
                bk.ap[:], seln0, src[:, ts], start=True, stop=True)],
                extra_waits=tuple(sbm) + (dlm,) if half == 0 and c == 0 else ())
            bk.read("v", lambda e, rep=rep, ts=ts, bk=bk: e.tensor_copy(
                rep[:, ts], bk.ap[:]), extra_waits=(p.mark("v"),))
    p.op("v", lambda e: e.drain(), incs=("fin_v",))
    bcm = p.mark("v")

    p.tag = "s0sd"
    p.op("g", lambda e: e.dma_start(wflat[:, 0:8192], X["c_seld"]),
         waits=(p.mark("t"),), incs=("dms4",), dma=True)
    sdm = [("dms4", p.semv["dms4"])]

    p.tag = "s0sc"
    dacb = [dac, small[:, 0:2048]]
    dbucb = [dbuc, small[:, 2048:4096]]
    selybb = [selyb, ctx["aux"][:, 1024:1152]]
    hcbufb = [hcbuf, ctx["aux"][:, 2048:4096]]
    scanmarks = {}
    ymarks = {}
    pend_ymm = None
    for dtile in range(NDT):
        p.op("v", lambda e, dtile=dtile: e.tensor_tensor(
            ucol, dlt[:, dtile, :], xnn[:, dtile, :], ALU.mult),
            waits=(bcm, dlm, p.mark("t"), p.mark("v")), incs=("fin_v",))
        um = p.mark("v")
        ybk0, ybk1 = dbank[0], dbank[1]
        for m in range(64):
            k = dtile * 64 + m
            par = k % 2
            dacp, dbucp = dacb[par], dbucb[par]
            selp, hcp = selybb[par], hcbufb[par]
            ymp = ymarks.get(k - 2)
            scp = scanmarks.get(k - 2)
            tbank.mm([lambda e, m=m, par=par: e.transpose(
                tbank.ap[:, par * 128:(par + 1) * 128],
                wflat[:, m * 128:(m + 1) * 128], ident[:])],
                extra_waits=tuple(sdm))
            tbank.read("v", lambda e, selp=selp, par=par: e.tensor_copy(
                selp, tbank.ap[:, par * 128:(par + 1) * 128]),
                       extra_waits=(ymp,) if ymp else (p.mark("v"),))
            selym = p.mark("v")
            for c in range(NTC):
                ts = slice(c * 512, (c + 1) * 512)
                bka, bku = bank[0], bank[1]
                bka.mm([lambda e, m=m, dtile=dtile, ts=ts, bka=bka: e.matmul(
                    bka.ap[:], wflat[:, m * 128:(m + 1) * 128],
                    dlt[:, dtile, ts], start=True, stop=True)],
                    extra_waits=(um,) if m == 0 and c == 0 else ())
                bka.read("s", lambda e, k=k, ts=ts, bka=bka, dacp=dacp:
                         e.activation(dacp[:, ts], bka.ap[:], AF.Exp, bias=0.0,
                                      scale=lcol[:, k:k + 1]),
                         extra_waits=(scp,) if scp and c == 0 else ())
                bku.mm([lambda e, m=m, ts=ts, bku=bku: e.matmul(
                    bku.ap[:], wflat[:, m * 128:(m + 1) * 128],
                    ucol[:, ts], start=True, stop=True)],
                    extra_waits=(um,) if m == 0 and c == 0 else ())
                bku.read("v", lambda e, ts=ts, bku=bku, dbucp=dbucp:
                         e.tensor_tensor(dbucp[:, ts], bku.ap[:], brep[:, ts],
                                         ALU.mult),
                    extra_waits=(p.mark("v"),))
            expm = p.mark("s")
            p.op("v", lambda e, dacp=dacp: e.memset(
                bass.AP(dacp.tensor, dacp.offset,
                        [list(dacp.ap[0]), [64, 32]]), 0.0),
                 waits=(expm, p.mark("v")), incs=("fin_v",))
            zm = p.mark("v")
            with nc.allow_low_precision(reason="scan state is fp32 internal"):
                p.op("v", lambda e, dacp=dacp, dbucp=dbucp: e.tensor_tensor_scan(
                    hbuf, dacp, dbucp, 0.0, ALU.mult, ALU.add),
                    waits=(zm,), incs=("fin_v",))
            scanmarks[k] = p.mark("v")
            p.op("v", lambda e, hcp=hcp: e.tensor_tensor(hcp, hbuf, crep,
                                                         ALU.mult),
                 waits=(p.mark("v"),) + ((ymp,) if ymp else ()), incs=("fin_v",))
            hcm = p.mark("v")

            def emit_ymm(m=m, selp=selp, hcp=hcp, hcm=hcm, selym=selym, k=k):
                for c in range(NTC):
                    ts = slice(c * 512, (c + 1) * 512)
                    ybk = ybk0 if c < 2 else ybk1
                    hf = (c % 2) * 512
                    ybk.mm([lambda e, ts=ts, hf=hf, m=m, ybk=ybk, selp=selp,
                            hcp=hcp: e.matmul(
                                ybk.ap[:, hf:hf + 512], selp, hcp[:, ts],
                                start=(m == 0), stop=(m == 63))],
                        extra_waits=(hcm, selym))
                ymarks[k] = p.mark("t")
            if pend_ymm is not None:
                this_ymm = emit_ymm
                pend_ymm()
                pend_ymm = this_ymm
            else:
                pend_ymm = emit_ymm
        pend_ymm()
        pend_ymm = None
        for c in range(NTC):
            ts = slice(c * 512, (c + 1) * 512)
            ybk = ybk0 if c < 2 else ybk1
            hf = (c % 2) * 512
            ybk.read("v", lambda e, dtile=dtile, ts=ts, hf=hf, ybk=ybk:
                     e.tensor_copy(dlt[:, dtile, ts], ybk.ap[:, hf:hf + 512]),
                     extra_waits=(p.mark("v"),))
    p.op("v", lambda e: e.drain(), incs=("fin_v",))
    yfm = p.mark("v")

    p.tag = "s0o2"
    for mt in range(NDT):
        slot = mt % 2
        p.op("g", lambda e, mt=mt, slot=slot: e.dma_start(
            wflat[:, slot * 1024:slot * 1024 + 1024].rearrange("p (a b) -> p a b", a=8),
            X["L0aa_wo"][mt].rearrange("a p o -> p a o")),
            waits=(p.mark("t"),), incs=(f"dms{slot}",), dma=True)
        wm = [(f"dms{slot}", p.semv[f"dms{slot}"])]
        for c in range(NTC):
            ts = slice(c * 512, (c + 1) * 512)
            bk = bank[c % 2]
            bk.mm([lambda e, i=i, mt=mt, ts=ts, bk=bk, slot=slot:
                   e.matmul(bk.ap[:], wflat[:, slot * 1024 + i * 128:slot * 1024 + (i + 1) * 128],
                            dlt[:, i, ts], start=(i == 0), stop=(i == NDT - 1))
                   for i in range(NDT)],
                  extra_waits=tuple(wm) + ((yfm,) if c == 0 else ())),
            bk.read("v", lambda e, mt=mt, ts=ts, bk=bk: e.tensor_add(
                xres[:, mt, ts], xres[:, mt, ts], bk.ap[:]))
    p.op("v", lambda e: e.drain(), incs=("fin_v",))
    return p.mark("v")


def emit_l0_ffn(ctx, idx):
    nc, p, X = ctx["nc"], ctx["p"], ctx["X"]
    xq = ctx["xn"]
    ga, ub = ctx["bigA"], ctx["bigB"]
    gav = ga[:].rearrange("p a t -> p (a t)").rearrange("p (x y) -> p x y", x=32)
    ubv = ub[:].rearrange("p a t -> p (a t)").rearrange("p (x y) -> p x y", x=32)
    small, smf = ctx["small"], ctx["smf"]
    statA, statB = ctx["statA"], ctx["statB"]
    cpool = ctx["cpool"]
    wbuf = ctx["wbuf"]
    wbv = wbuf[:].rearrange("p a t -> p (a t)").rearrange("p (x y z) -> p x y z", y=8, z=128)
    xres = ctx["xres"]
    ones_col = ctx["ones_col"]
    bank, dbank = ctx["bank"], ctx["dbank"]

    if "wq_g" not in ctx:
        ctx["wq_g"] = nc.dram_tensor("wq_g", [8, 32, 128, 128], BF16).ap()
        ctx["wq_u"] = nc.dram_tensor("wq_u", [8, 32, 128, 128], BF16).ap()
        ctx["wq_d"] = nc.dram_tensor("wq_d", [32, 8, 128, 128], BF16).ap()

    if not ctx.get("l0ffn_quantized"):
        ctx["l0ffn_quantized"] = True
        for wi, (nm, dst, I, O, mwslot) in enumerate((
                ("L0_ffn_w_gate_T", ctx["wq_g"], 1024, 4096, 4),
                ("L0_ffn_w_up_T", ctx["wq_u"], 1024, 4096, 5),
                ("L0_ffn_w_down_T", ctx["wq_d"], 4096, 1024, 6))):
            wv = X[nm].rearrange("(a q) o -> q a o", q=128)
            nit, noc = I // 128, O // 512
            p.op("v", lambda e: e.memset(smf[:, 0:1], 0.0), incs=("fin_v",))
            for it in range(nit):
                for oc in range(noc):
                    half = (it * noc + oc) % 2
                    hs = slice(half * 512, (half + 1) * 512)
                    p.op("y", lambda e, it=it, oc=oc, wv=wv, hs=hs:
                         e.dma_start(statA[:, hs], wv[:, it, oc * 512:(oc + 1) * 512]),
                         waits=(p.mark("v"),), incs=("dma",), dma=True)
                    p.op("v", lambda e, hs=hs: e.tensor_reduce(
                        col2d(smf[:, 1:2]), statA[:, hs], axis=AXX, op=ALU.add,
                        apply_absolute_value=True),
                        waits=[("dma", p.semv["dma"])], incs=("fin_v",))
                    p.op("v", lambda e: e.tensor_add(smf[:, 0:1], smf[:, 0:1],
                                                     smf[:, 1:2]),
                         waits=(p.mark("v"),), incs=("fin_v",))
            p.op("v", lambda e: e.tensor_copy(ctx["aux"][:, 144:145], smf[:, 0:1]),
                 waits=(p.mark("v"),), incs=("fin_v",))
            vm = p.mark("v")
            bk = bank[2]
            bk.mm([lambda e, bk=bk: e.matmul(bk.ap[:, 0:1], ones_col[:],
                                             ctx["aux"][:, 144:145], start=True, stop=True)],
                  extra_waits=(vm,))
            bk.read("v", lambda e, I=I, O=O, mwslot=mwslot: e.tensor_scalar(
                cpool[:, mwslot:mwslot + 1], bk.ap[:, 0:1], 1.0 / (I * O), 1e-5,
                ALU.mult, ALU.max))
            p.op("v", lambda e, mwslot=mwslot: e.reciprocal(smf[:, 2:3],
                                                            cpool[:, mwslot:mwslot + 1]),
                 waits=(p.mark("v"),), incs=("fin_v",))
            for it in range(nit):
                for oc in range(noc):
                    half = (it * noc + oc) % 2
                    hs = slice(half * 512, (half + 1) * 512)
                    p.op("y", lambda e, it=it, oc=oc, wv=wv, hs=hs:
                         e.dma_start(statA[:, hs], wv[:, it, oc * 512:(oc + 1) * 512]),
                         waits=(p.mark("v"),), incs=("dma",), dma=True)
                    p.op("v", lambda e, hs=hs: e.tensor_scalar(
                        statB[:, hs], statA[:, hs], smf[:, 2:3], MAGIC,
                        ALU.mult, ALU.add),
                        waits=[("dma", p.semv["dma"])], incs=("fin_v",))
                    p.op("v", lambda e, hs=hs: e.tensor_scalar(
                        statB[:, hs], statB[:, hs], MAGIC, -1.0,
                        ALU.subtract, ALU.max), waits=(p.mark("v"),), incs=("fin_v",))
                    p.op("v", lambda e, hs=hs, half=half: e.tensor_scalar_min(
                        small[:, half * 512:(half + 1) * 512],
                        statB[:, hs], 1.0),
                        waits=(p.mark("v"),
                               (f"dms{6 + half}", p.semv[f"dms{6 + half}"])),
                        incs=("fin_v",))
                    vmk = p.mark("v")
                    sq = f"dms{6 + half}"
                    p.op("g", lambda e, it=it, oc=oc, dst=dst, half=half:
                         e.dma_start(
                             _wq_out_ap(dst, it, oc),
                             small[:, half * 512:(half + 1) * 512]
                             .rearrange("p (a b) -> p a b", a=4)),
                         waits=(vmk,), incs=(sq,), dma=True)
            p.op("v", lambda e: e.drain(),
                 waits=[("dms6", p.semv["dms6"]), ("dms7", p.semv["dms7"])],
                 incs=("fin_v",))

    lnm = emit_layernorm(ctx)

    mrep = small[:, 0:T]
    mtmp = small[:, 2048:4096]
    p.op("v", lambda e: e.tensor_tensor(mtmp, xq[:, 0, :], xq[:, 0, :], ALU.mult),
         waits=(p.mark("v"),), incs=("fin_v",))
    for i in range(1, NDT):
        p.op("v", lambda e, i=i: e.tensor_tensor(mrep, xq[:, i, :], xq[:, i, :],
                                                 ALU.mult),
             waits=(p.mark("v"),), incs=("fin_v",))
        p.op("v", lambda e: e.tensor_max(mtmp, mtmp, mrep),
             waits=(p.mark("v"),), incs=("fin_v",))
    gm = emit_part_max_rep(ctx, mtmp, T, mrep, sqrt_of_sq=True)
    for c in range(2):
        sl = slice(c * 1024, (c + 1) * 1024)
        p.op("v", lambda e, sl=sl: e.tensor_scalar_max(statA[:], mrep[:, sl], 1e-5),
             waits=(gm,) if c == 0 else (p.mark("v"),), incs=("fin_v",))
        p.op("v", lambda e: e.reciprocal(statB[:], statA[:]),
             waits=(p.mark("v"),), incs=("fin_v",))
        bm = p.mark("v")
        for i in range(NDT):
            p.op("v", lambda e, i=i, sl=sl: e.tensor_tensor(statA[:], xq[:, i, sl],
                                                            statB[:], ALU.mult),
                 waits=(bm,) if i == 0 else (p.mark("v"),), incs=("fin_v",))
            p.op("v", lambda e: e.tensor_scalar(statA[:], statA[:], 127.0, MAGIC,
                                                ALU.mult, ALU.add),
                 waits=(p.mark("v"),), incs=("fin_v",))
            p.op("v", lambda e: e.tensor_scalar(statA[:], statA[:], MAGIC, -128.0,
                                                ALU.subtract, ALU.max),
                 waits=(p.mark("v"),), incs=("fin_v",))
            p.op("v", lambda e, i=i, sl=sl: e.tensor_scalar_min(xq[:, i, sl],
                                                                statA[:], 127.0),
                 waits=(p.mark("v"),), incs=("fin_v",))
    qm = p.mark("v")

    p.op("v", lambda e: e.tensor_tensor(cpool[:, 7:8], cpool[:, 5:6], cpool[:, 6:7],
                                        ALU.mult), waits=(p.mark("v"),), incs=("fin_v",))
    p.op("v", lambda e: e.tensor_scalar_mul(cpool[:, 7:8], cpool[:, 7:8],
                                            1.0 / (127.0 * 127.0)),
         waits=(p.mark("v"),), incs=("fin_v",))
    p.op("v", lambda e: e.tensor_scalar_mul(smf[:, 3:4], cpool[:, 4:5], 1.0 / 127.0),
         waits=(p.mark("v"),), incs=("fin_v",))
    scm = p.mark("v")

    mh = small[:, 2048:2560]
    for c in range(4):
        ts = slice(c * 512, (c + 1) * 512)
        for ot in range(32):
            slot_g = (ot % 2) * 2
            slot_u = slot_g + 1
            sg, su = f"dms{slot_g}", f"dms{slot_u}"
            p.op("g", lambda e, ot=ot, slot_g=slot_g:
                 e.dma_start(wbv[:, slot_g, :, :], ctx["wq_g"][:, ot, :, :]
                             .rearrange("a p o -> p a o")),
                 waits=(p.mark("t"),) if ot >= 2 else (qm,), incs=(sg,), dma=True)
            p.op("g", lambda e, ot=ot, slot_u=slot_u:
                 e.dma_start(wbv[:, slot_u, :, :], ctx["wq_u"][:, ot, :, :]
                             .rearrange("a p o -> p a o")),
                 incs=(su,), dma=True)
            wmk = [(sg, p.semv[sg]), (su, p.semv[su])]
            for h, slot, bidx in ((0, slot_g, 0), (1, slot_u, 1)):
                bk = bank[bidx]
                bk.mm([lambda e, i=i, slot=slot, ts=ts, bk=bk:
                       e.matmul(bk.ap[:], wbv[:, slot, i, :], xq[:, i, ts],
                                start=(i == 0), stop=(i == NDT - 1))
                       for i in range(NDT)],
                      extra_waits=tuple(wmk) + ((qm,) if ot < 2 else ()))
            bank[0].read("v", lambda e, ot=ot, ts=ts: e.tensor_tensor(
                gav[:, ot, :], bank[0].ap[:], mrep[:, ts], ALU.mult),
                extra_waits=(scm,))
            vmk = p.mark("v")
            p.op("s", lambda e, ot=ot, ts=ts: e.activation(
                gav[:, ot, :], gav[:, ot, :], AF.Silu, bias=0.0,
                scale=smf[:, 3:4]), waits=(vmk,), incs=("fin_s",))
            bank[1].read("v", lambda e, ot=ot, ts=ts: e.tensor_tensor(
                ubv[:, ot, :], bank[1].ap[:], mrep[:, ts], ALU.mult))
        smk = p.mark("s")
        hw0 = (smk, p.mark("v"))
        for ot in range(32):
            p.op("v", lambda e, ot=ot, ts=ts: e.tensor_tensor(
                gav[:, ot, :], gav[:, ot, :], ubv[:, ot, :], ALU.mult),
                waits=hw0 if ot == 0 else (),
                incs=("fin_v",) if ot == 31 else ())
        hm = p.mark("v")
        mhtmp = small[:, 2560:3072]
        mhsq = small[:, 3584:4096]
        p.op("v", lambda e: e.tensor_tensor(mhtmp, gav[:, 0, :], gav[:, 0, :],
                                            ALU.mult), waits=(hm,), incs=("fin_v",))
        for ot in range(1, 32):
            p.op("v", lambda e, ot=ot: e.tensor_tensor(mhsq, gav[:, ot, :],
                                                       gav[:, ot, :], ALU.mult),
                 waits=(p.mark("v"),), incs=("fin_v",))
            p.op("v", lambda e: e.tensor_max(mhtmp, mhtmp, mhsq),
                 waits=(p.mark("v"),), incs=("fin_v",))
        gm2 = emit_part_max_rep(ctx, mhtmp, 512, mh, sqrt_of_sq=True)
        p.op("v", lambda e: e.tensor_scalar_max(statA[:, 0:512], mh, 1e-5),
             waits=(gm2,), incs=("fin_v",))
        p.op("v", lambda e: e.reciprocal(statB[:, 0:512], statA[:, 0:512]),
             waits=(p.mark("v"),), incs=("fin_v",))
        bm2 = p.mark("v")
        for ot in range(32):
            p.op("v", lambda e, ot=ot, ts=ts: e.tensor_tensor(
                statA[:, 512:1024], gav[:, ot, :], statB[:, 0:512], ALU.mult),
                waits=(bm2,) if ot == 0 else (p.mark("v"),), incs=("fin_v",))
            p.op("v", lambda e: e.tensor_scalar(statA[:, 512:1024], statA[:, 512:1024],
                                                127.0, MAGIC, ALU.mult, ALU.add),
                 waits=(p.mark("v"),), incs=("fin_v",))
            p.op("v", lambda e: e.tensor_scalar(statA[:, 512:1024], statA[:, 512:1024],
                                                MAGIC, -128.0, ALU.subtract, ALU.max),
                 waits=(p.mark("v"),), incs=("fin_v",))
            p.op("v", lambda e, ot=ot, ts=ts: e.tensor_scalar_min(
                gav[:, ot, :], statA[:, 512:1024], 127.0),
                 waits=(p.mark("v"),), incs=("fin_v",))
        hqm = p.mark("v")
        for ot2 in range(NDT):
            sb4 = (ot2 % 2) * 4
            wdv = wbuf[:].rearrange("p a t -> p (a t)").rearrange(
                "p (s x y) -> p s x y", s=2, x=32)
            sd = f"dms{4 + ot2 % 2}"
            p.op("g", lambda e, ot2=ot2, sb4=sb4, wdv=wdv:
                 e.dma_start(wdv[:, sb4 // 4, :, :],
                             ctx["wq_d"][:, ot2, :, :].rearrange("a p o -> p a o")),
                 waits=(p.mark("t"),) if ot2 >= 2 else (hqm,), incs=(sd,), dma=True)
            wmk = [(sd, p.semv[sd])]
            bk = bank[2]
            bk.mm([lambda e, it=it, sb4=sb4, ts=ts, bk=bk, wdv=wdv:
                   e.matmul(bk.ap[:], wdv[:, sb4 // 4, it, :],
                            gav[:, it, :], start=(it == 0), stop=(it == 31))
                   for it in range(32)],
                  extra_waits=tuple(wmk) + ((hqm,) if ot2 < 2 else ()))
            bk.read("v", lambda e, ot2=ot2, ts=ts, bk=bk: e.tensor_tensor(
                statB[:, 0:512], bk.ap[:], mh, ALU.mult),
                extra_waits=(p.mark("v"),))
            p.op("v", lambda e, ot2=ot2, ts=ts: e.scalar_tensor_tensor(
                xres[:, ot2, ts], statB[:, 0:512], cpool[:, 7:8],
                xres[:, ot2, ts], ALU.mult, ALU.add),
                waits=(p.mark("v"),), incs=("fin_v",))
    return p.mark("v")



def emit_part_max_rep(ctx, m_ap, W, out_rep, clipv=1e-5, sqrt_of_sq=False):
    p = ctx["p"]
    tbank = ctx["tbank"]
    bank = ctx["bank"]
    ident, ones_row = ctx["ident"], ctx["ones_row"]
    small = ctx["small"]
    nt = W // 128
    mtok = ctx["aux"][:, 128:128 + nt]
    for j0 in range(0, nt, 8):
        jn = min(8, nt - j0)
        tbank.mm([lambda e, j=j: e.transpose(
            tbank.ap[:, (j % 8) * 128:(j % 8 + 1) * 128],
            m_ap[:, j * 128:(j + 1) * 128], ident[:])
            for j in range(j0, j0 + jn)], extra_waits=(p.mark("v"),))
        for j in range(j0, j0 + jn):
            tbank.read("v", lambda e, j=j: e.tensor_reduce(
                col2d(mtok[:, j:j + 1]), tbank.ap[:, (j % 8) * 128:(j % 8 + 1) * 128],
                axis=AXX, op=ALU.max))
    p.op("v", lambda e: e.drain(), incs=("fin_v",))
    if sqrt_of_sq:
        p.op("v", lambda e, nt=nt: e.tensor_scalar_max(mtok[:, 0:nt], mtok[:, 0:nt],
                                                       float(clipv) ** 2),
             waits=(p.mark("v"),), incs=("fin_v",))
        vq = p.mark("v")
        p.op("s", lambda e, nt=nt: e.activation(mtok[:, 0:nt], mtok[:, 0:nt], AF.Ln,
                                                bias=0.0, scale=1.0), waits=(vq,),
             incs=("fin_s",))
        p.op("s", lambda e, nt=nt: e.activation(mtok[:, 0:nt], mtok[:, 0:nt], AF.Exp,
                                                bias=0.0, scale=0.5),
             waits=(p.mark("s"),), incs=("fin_s",))
        sq = p.mark("s")
        p.op("v", lambda e: e.drain(), waits=(sq,), incs=("fin_v",))
    mm2 = p.mark("v")
    for j0 in range(0, nt, 8):
        jn = min(8, nt - j0)
        tbank.mm([lambda e, j=j: e.transpose(
            tbank.ap[0:1, (j % 8) * 128:(j % 8 + 1) * 128],
            mtok[:, j:j + 1], ident[:])
            for j in range(j0, j0 + jn)], extra_waits=(mm2,))
        tbank.read("v", lambda e, j0=j0, jn=jn: e.tensor_copy(
            out_rep[0:1, j0 * 128:(j0 + jn) * 128], tbank.ap[0:1, 0:jn * 128]))
    p.op("v", lambda e: e.drain(), incs=("fin_v",))
    rm = p.mark("v")
    for j in range(nt):
        bk = bank[2]
        bk.mm([lambda e, j=j, bk=bk: e.matmul(
            bk.ap[:, 0:128], ones_row[0:1, 0:128],
            out_rep[0:1, j * 128:(j + 1) * 128], start=True, stop=True)],
              extra_waits=(rm,))
        bk.read("v", lambda e, j=j, bk=bk: e.tensor_scalar_max(
            out_rep[:, j * 128:(j + 1) * 128], bk.ap[:, 0:128], clipv))
    p.op("v", lambda e: e.drain(), incs=("fin_v",))
    return p.mark("v")


def _wq_out_ap(dst, it, oc):
    nt = dst.shape[1]
    base = (it * nt + oc * 4) * 128 * 128
    return bass.AP(dst.tensor, base, [[128, 128], [128 * 128, 4], [1, 128]])


def emit_l0_attn_b(ctx, idx):
    nc, p, X = ctx["nc"], ctx["p"], ctx["X"]
    xn = ctx["xn"]
    k = ctx["bigA"]
    vtok = ctx["bigB"][:].rearrange("p a t -> p (a t)").rearrange("p (x y) -> p x y", x=NTT)
    aux = ctx["aux"]
    qtile = aux[:, 4480:5504].rearrange("p (a b) -> p a b", a=8)
    cosb = aux[:, 0:T]
    sinb = aux[:, T:2 * T]
    qstg = aux[:, 2 * T:2 * T + 256].rearrange("p (a b) -> p a b", a=2)
    trilm = aux[:, 2 * T + 256:2 * T + 384]
    wbuf, small, smf = ctx["wbuf"], ctx["small"], ctx["smf"]
    xres = ctx["xres"]
    dbank, bank, tbank = ctx["dbank"], ctx["bank"], ctx["tbank"]
    ones_row, ident = ctx["ones_row"], ctx["ident"]

    p.op("v", lambda e: e.memset(qstg[:], 0.0), incs=("fin_v",))
    lnm = emit_layernorm(ctx)

    wq = X["L0_attn_b_WqT_perm"].rearrange("(n q) o -> q n o", q=128)
    wk = X["L0_attn_b_WkT_perm"].rearrange("(n q) o -> q n o", q=128)
    wv = X["L0_attn_b_WvT"].rearrange("(n q) o -> q n o", q=128)
    wo = X["L0_attn_b_WoT"].rearrange("(n q) o -> q n o", q=128)

    p.op("g", lambda e: e.dma_start(wbuf[:], wk), incs=("dmaw",), dma=True)
    wm = [("dmaw", p.semv["dmaw"]), lnm]
    for ot in range(NDT):
        for c in range(2):
            bk = dbank[c]
            ems = []
            for s2 in range(2):
                for i in range(NDT):
                    ems.append(lambda e, i=i, ot=ot, c=c, s2=s2, bk=bk:
                               e.matmul(bk.ap[:, s2 * 512:(s2 + 1) * 512],
                                        wbuf[:, i, ot * 128:(ot + 1) * 128],
                                        xn[:, i, c * 1024 + s2 * 512:c * 1024 + (s2 + 1) * 512],
                                        start=(i == 0), stop=(i == NDT - 1)))
            bk.mm(ems, extra_waits=tuple(wm) if ot == 0 else ())
            bk.read("v", lambda e, ot=ot, c=c, bk=bk:
                    e.tensor_copy(k[:, ot, c * 1024:(c + 1) * 1024], bk.ap[:]))
    p.op("g", lambda e: e.dma_start(wbuf[:], wv), waits=(p.mark("t"),),
         incs=("dmaw",), dma=True)
    wm = [("dmaw", p.semv["dmaw"])]
    for tt in range(NTT):
        for c in range(2):
            bk = bank[c]
            sl = slice(c * 512, (c + 1) * 512)
            bk.mm([lambda e, i=i, tt=tt, sl=sl, bk=bk:
                   e.matmul(bk.ap[:], xn[:, i, tt * 128:(tt + 1) * 128],
                            wbuf[:, i, sl], start=(i == 0), stop=(i == NDT - 1))
                   for i in range(NDT)],
                  extra_waits=tuple(wm) if tt == 0 else ())
            bk.read("v", lambda e, tt=tt, sl=sl, bk=bk:
                    e.tensor_copy(vtok[:, tt, sl], bk.ap[:]))
    p.op("g", lambda e: e.dma_start(wbuf[:], wq), waits=(p.mark("t"),),
         incs=("dmaw",), dma=True)
    p.op("g", lambda e: e.dma_start(cosb, X["c_rope_cos"]),
         incs=("dmaw",), dma=True)
    p.op("g", lambda e: e.dma_start(sinb, X["c_rope_sin"]),
         incs=("dmaw",), dma=True)
    p.op("g", lambda e: e.dma_start(trilm, X["c_trilm"]),
         incs=("dmaw",), dma=True)
    wm = [("dmaw", p.semv["dmaw"])]
    p.op("t", lambda e: e.nop(), waits=wm)
    p.op("v", lambda e: e.nop(), waits=wm)

    ta = small[:, 0:1024]
    tb = small[:, 1024:2048]
    tc2 = small[:, 2048:3072]
    for g in range(4):
        for c in range(2):
            sl = slice(c * 1024, (c + 1) * 1024)
            vw = (p.mark("v"),)
            p.op("v", lambda e, g=g, sl=sl: e.tensor_tensor(ta, k[:, g, sl],
                                                            cosb[:, sl], ALU.mult),
                 waits=vw, incs=("fin_v",))
            p.op("v", lambda e, g=g, sl=sl: e.tensor_tensor(tb, k[:, 4 + g, sl],
                                                            sinb[:, sl], ALU.mult),
                 incs=("fin_v",))
            p.op("v", lambda e, g=g, sl=sl: e.tensor_tensor(tc2, k[:, g, sl],
                                                            sinb[:, sl], ALU.mult),
                 incs=("fin_v",))
            vm = p.mark("v")
            p.op("v", lambda e, g=g, sl=sl: e.tensor_tensor(k[:, g, sl], ta, tb,
                                                            ALU.subtract),
                 waits=(vm,), incs=("fin_v",))
            p.op("v", lambda e, g=g, sl=sl: e.tensor_tensor(ta, k[:, 4 + g, sl],
                                                            cosb[:, sl], ALU.mult),
                 waits=(p.mark("v"),), incs=("fin_v",))
            vm = p.mark("v")
            p.op("v", lambda e, g=g, sl=sl: e.tensor_tensor(k[:, 4 + g, sl], tc2, ta,
                                                            ALU.add),
                 waits=(vm,), incs=("fin_v",))
    p.op("v", lambda e: e.drain(), incs=("fin_v",))
    krm = p.mark("v")

    probs = small[:, 0:T]
    pt = small[:, T:2 * T]

    for tq in range(NTT):
        tqs = slice(tq * 128, (tq + 1) * 128)
        w = (tq + 1) * 128
        for ot in range(NDT):
            bk = bank[2]
            bk.mm([lambda e, i=i, ot=ot, tq=tq, bk=bk:
                   e.matmul(bk.ap[:, 0:128], wbuf[:, i, ot * 128:(ot + 1) * 128],
                            xn[:, i, tq * 128:(tq + 1) * 128], start=(i == 0),
                            stop=(i == NDT - 1))
                   for i in range(NDT)],
                  extra_waits=(krm,) if (tq == 0 and ot == 0) else ())
            bk.read("v", lambda e, ot=ot, bk=bk:
                    e.tensor_copy(qtile[:, ot, :], bk.ap[:, 0:128]))
        vw = (p.mark("v"),)
        p.op("v", lambda e, tqs=tqs: e.tensor_tensor(
            ta[:, 0:512], qtile[:, 0:4, :].rearrange("p a t -> p (a t)"),
            _rope4(cosb, tqs), ALU.mult), waits=vw, incs=("fin_v",))
        p.op("v", lambda e, tqs=tqs: e.tensor_tensor(
            tb[:, 0:512], qtile[:, 4:8, :].rearrange("p a t -> p (a t)"),
            _rope4(sinb, tqs), ALU.mult), incs=("fin_v",))
        p.op("v", lambda e, tqs=tqs: e.tensor_tensor(
            tc2[:, 0:512], qtile[:, 0:4, :].rearrange("p a t -> p (a t)"),
            _rope4(sinb, tqs), ALU.mult), incs=("fin_v",))
        vm = p.mark("v")
        p.op("v", lambda e: e.tensor_tensor(
            qtile[:, 0:4, :].rearrange("p a t -> p (a t)"), ta[:, 0:512],
            tb[:, 0:512], ALU.subtract), waits=(vm,), incs=("fin_v",))
        p.op("v", lambda e, tqs=tqs: e.tensor_tensor(
            ta[:, 0:512], qtile[:, 4:8, :].rearrange("p a t -> p (a t)"),
            _rope4(cosb, tqs), ALU.mult), waits=(p.mark("v"),), incs=("fin_v",))
        vm = p.mark("v")
        p.op("v", lambda e: e.tensor_tensor(
            qtile[:, 4:8, :].rearrange("p a t -> p (a t)"), tc2[:, 0:512],
            ta[:, 0:512], ALU.add), waits=(vm,), incs=("fin_v",))
        qrm = p.mark("v")

        for hi in range(16):
            m, g = hi // 4, hi % 4
            h = g * 4 + m
            if g == 0:
                pm = (m - 1) % 4
                p.op("v", lambda e, pm=pm: e.memset(
                    qstg[32 * pm:32 * pm + 32, :, :], 0.0),
                    waits=(qrm, p.mark("t")), incs=("fin_v",))
            p.op("v", lambda e, g=g, m=m: e.tensor_copy(
                qstg[32 * m:32 * m + 32, 0, :], qtile[32 * m:32 * m + 32, g, :]),
                waits=(qrm, p.mark("t")), incs=("fin_v",))
            p.op("v", lambda e, g=g, m=m: e.tensor_copy(
                qstg[32 * m:32 * m + 32, 1, :], qtile[32 * m:32 * m + 32, 4 + g, :]),
                incs=("fin_v",))
            stm = p.mark("v")
            nch = tq // 4 + 1
            for ch in range(nch):
                bk = dbank[ch // 2]
                base = ch * 512
                nn = min(512, w - base)
                bk.mm([lambda e, par=par, ch=ch, nn=nn, bk=bk, g=g, base=base:
                       e.matmul(bk.ap[:, (ch % 2) * 512:(ch % 2) * 512 + nn],
                                qstg[:, par, :], k[:, par * 4 + g, base:base + nn],
                                start=(par == 0), stop=(par == 1))
                       for par in range(2)],
                      extra_waits=(stm,) if ch == 0 else ())
            dbk = dbank[(tq // 4) // 2]
            doff = ((tq // 4) % 2) * 512 + (tq % 4) * 128
            dbk.read("v", lambda e, dbk=dbk, doff=doff: e.tensor_add(
                dbk.ap[:, doff:doff + 128], dbk.ap[:, doff:doff + 128], trilm))
            mskm = p.mark("v")
            nred = (w + 1023) // 1024
            for r in range(nred):
                ww = min(1024, w - r * 1024)
                dbank[r].read("v", lambda e, r=r, ww=ww: e.tensor_reduce(
                    col2d(smf[:, r:r + 1]), dbank[r].ap[:, 0:ww], axis=AXX,
                    op=ALU.max), extra_waits=(mskm,))
            p.op("v", lambda e, nred=nred: e.tensor_reduce(
                col2d(smf[:, 2:3]), smf[:, 0:nred], axis=AXX, op=ALU.max),
                waits=(p.mark("v"),), incs=("fin_v",))
            p.op("v", lambda e: e.tensor_scalar_mul(smf[:, 3:4], smf[:, 2:3],
                                                    -0.125),
                 waits=(p.mark("v"),), incs=("fin_v",))
            vmk = p.mark("v")
            for r in range(nred):
                ww = min(1024, w - r * 1024)
                dbank[r].read("s", lambda e, r=r, ww=ww: e.activation(
                    probs[:, r * 1024:r * 1024 + ww], dbank[r].ap[:, 0:ww], AF.Exp,
                    bias=smf[:, 3:4], scale=0.125), extra_waits=(vmk,))
            smk = p.mark("s")
            p.op("v", lambda e, w=w: e.tensor_reduce(col2d(smf[:, 4:5]),
                                                     probs[:, 0:w], axis=AXX,
                                                     op=ALU.add),
                 waits=(smk,), incs=("fin_v",))
            p.op("v", lambda e: e.reciprocal(smf[:, 5:6], smf[:, 4:5]),
                 waits=(p.mark("v"),), incs=("fin_v",))
            p.op("v", lambda e, w=w: e.tensor_scalar_mul(probs[:, 0:w], probs[:, 0:w],
                                                         smf[:, 5:6]),
                 waits=(p.mark("v"),), incs=("fin_v",))
            dmk = p.mark("v")
            for tk0 in range(0, tq + 1, 8):
                tkn = min(8, tq + 1 - tk0)
                tbank.mm([lambda e, tkt=tkt: e.transpose(
                    tbank.ap[:, (tkt % 8) * 128:(tkt % 8 + 1) * 128],
                    probs[:, tkt * 128:(tkt + 1) * 128], ident[:])
                    for tkt in range(tk0, tk0 + tkn)], extra_waits=(dmk,))
                tbank.read("v", lambda e, tk0=tk0, tkn=tkn: e.tensor_copy(
                    pt[:, tk0 * 128:(tk0 + tkn) * 128],
                    tbank.ap[:, 0:tkn * 128]))
            ptm = p.mark("v")
            hb = (h % 2) * 64
            bk = bank[0] if h % 2 == 0 else bank[1]
            bk.mm([lambda e, tkt=tkt, h=h, hb=hb, bk=bk, tq=tq:
                   e.matmul(bk.ap[hb:hb + 64, 0:128],
                            vtok[:, tkt, h * 64:(h + 1) * 64],
                            pt[:, tkt * 128:(tkt + 1) * 128],
                            start=(tkt == 0), stop=(tkt == tq))
                   for tkt in range(tq + 1)], extra_waits=(ptm,))
            bk.read("v", lambda e, h=h, hb=hb, tqs=tqs, bk=bk: e.tensor_copy(
                bass.AP(xn[:].tensor, xn[:].offset + (h // 2) * T + tqs.start,
                        [[NDT * T, 64], [1, 128]]) if False else
                xn[hb:hb + 64, h // 2, tqs], bk.ap[hb:hb + 64, 0:128]))
    p.op("v", lambda e: e.drain(), incs=("fin_v",))
    aom = p.mark("v")

    p.op("g", lambda e: e.dma_start(wbuf[:], wo), waits=(p.mark("t"),),
         incs=("dmaw",), dma=True)
    wm = [("dmaw", p.semv["dmaw"]), aom]
    for ot in range(NDT):
        for c in range(NTC):
            bk = bank[c % 2]
            sl = slice(c * 512, (c + 1) * 512)
            bk.mm([lambda e, i=i, ot=ot, sl=sl, bk=bk:
                   e.matmul(bk.ap[:], wbuf[:, i, ot * 128:(ot + 1) * 128],
                            xn[:, i, sl], start=(i == 0), stop=(i == NDT - 1))
                   for i in range(NDT)],
                  extra_waits=tuple(wm) if ot == 0 and c == 0 else ())
            bk.read("v", lambda e, ot=ot, sl=sl, bk=bk: e.tensor_add(
                xres[:, ot, sl], xres[:, ot, sl], bk.ap[:]))
    p.op("v", lambda e: e.drain(), incs=("fin_v",))
    return p.mark("v")


def _rope4(tab, tqs):
    sl = tab[:, tqs]
    return bcast(sl, 1, 4)


def emit_l1_ffn(ctx, idx):
    nc, p, X = ctx["nc"], ctx["p"], ctx["X"]
    xq = ctx["xn"]
    ga, ub = ctx["bigA"], ctx["bigB"]
    gav = ga[:].rearrange("p a t -> p (a t)")[:, 0:24 * 512].rearrange(
        "p (x y) -> p x y", x=24)
    ubv = ub[:].rearrange("p a t -> p (a t)")[:, 0:24 * 512].rearrange(
        "p (x y) -> p x y", x=24)
    small, smf = ctx["small"], ctx["smf"]
    statA, statB = ctx["statA"], ctx["statB"]
    cpool, lcol = ctx["cpool"], ctx["lcol"]
    wbuf = ctx["wbuf"]
    wflat = wbuf[:].rearrange("p a t -> p (a t)")
    xres = ctx["xres"]
    ident = ctx["ident"]
    bank, dbank, tbank = ctx["bank"], ctx["dbank"], ctx["tbank"]
    II = 3072
    NIT = II // 128

    if "wq_lg" not in ctx:
        ctx["wq_lg"] = nc.dram_tensor("wq_lg", [8, NIT, 128, 128], BF16).ap()
        ctx["wq_lu"] = nc.dram_tensor("wq_lu", [8, NIT, 128, 128], BF16).ap()
        ctx["wq_ld"] = nc.dram_tensor("wq_ld", [NIT, 8, 128, 128], BF16).ap()
        ctx["upr"] = nc.dram_tensor("upr", [NIT, 128, T], BF16).ap()
        ctx["gtr"] = nc.dram_tensor("gtr", [NIT, 128, T], BF16).ap()
        ctx["cc_in"] = nc.dram_tensor("cc_in", [1, 8], BF16).ap()
        ctx["cc_out"] = nc.dram_tensor("cc_out", [1, 8], BF16).ap()

    def dbar(sems):
        p.op("v", lambda e: e.drain(),
             waits=tuple((sq, p.semv[sq]) for sq in sems), incs=("fin_v",))

    p.tag = f"f{idx}wq"
    if not ctx.get("l1ffn_quantized"):
        ctx["l1ffn_quantized"] = True
        sb_par = [smf[:, 0:16], lcol[:, 16:32]]
        iv_par = [lcol[:, 0:16], lcol[:, 32:48]]
        v0w = p.mark("v")
        consumed = {}
        tile = 0
        for nm, dst, O, I in (("L1_ffn_w_gate", ctx["wq_lg"], II, 1024),
                              ("L1_ffn_w_up", ctx["wq_lu"], II, 1024),
                              ("L1_ffn_w_down", ctx["wq_ld"], 1024, II)):
            wv = X[nm].rearrange("(a q) o -> q a o", q=128)
            for ot in range(O // 128):
                for ic in range(I // 512):
                    half = tile % 2
                    hs = slice(half * 512, (half + 1) * 512)
                    sbuf, ivb = sb_par[half], iv_par[half]
                    p.op("y", lambda e, ot=ot, ic=ic, wv=wv, hs=hs:
                         e.dma_start(statA[:, hs], wv[:, ot, ic * 512:(ic + 1) * 512]),
                         waits=(consumed.get(tile - 2, v0w),
                                ("dma", p.semv["dma"])),
                         incs=("dma",), dma=True)
                    dmw = [("dma", p.semv["dma"])]
                    p.op("v", lambda e, hs=hs, sbuf=sbuf: e.tensor_reduce(
                        _ins(sbuf, 2, 1, 1),
                        statA[:, hs].rearrange("p (a b) -> p a b", b=32),
                        axis=AXX, op=ALU.max, apply_absolute_value=True),
                        waits=tuple(dmw) + (p.mark("v"),), incs=("fin_v",))
                    p.op("v", lambda e, sbuf=sbuf: e.tensor_scalar(
                        sbuf, sbuf, 1.0 / 7.0, 1e-12,
                        ALU.mult, ALU.max), waits=(p.mark("v"),), incs=("fin_v",))
                    p.op("v", lambda e, sbuf=sbuf, ivb=ivb: e.reciprocal(ivb, sbuf),
                         waits=(p.mark("v"),), incs=("fin_v",))
                    p.op("v", lambda e, hs=hs, ivb=ivb: e.tensor_tensor(
                        statB[:, hs].rearrange("p (a b) -> p a b", b=32),
                        statA[:, hs].rearrange("p (a b) -> p a b", b=32),
                        bcast(ivb, 2, 32), ALU.mult),
                        waits=(p.mark("v"),), incs=("fin_v",))
                    p.op("v", lambda e, hs=hs: e.tensor_scalar(
                        statB[:, hs], statB[:, hs], MAGIC, MAGIC,
                        ALU.add, ALU.subtract), waits=(p.mark("v"),), incs=("fin_v",))
                    p.op("v", lambda e, hs=hs: e.tensor_scalar(
                        statB[:, hs], statB[:, hs], -8.0, 7.0,
                        ALU.max, ALU.min), waits=(p.mark("v"),), incs=("fin_v",))
                    p.op("v", lambda e, hs=hs, sbuf=sbuf: e.tensor_tensor(
                        small[:, hs].rearrange("p (a b) -> p a b", b=32),
                        statB[:, hs].rearrange("p (a b) -> p a b", b=32),
                        bcast(sbuf, 2, 32), ALU.mult),
                        waits=(p.mark("v"),), incs=("fin_v",))
                    wqm = p.mark("v")
                    consumed[tile] = wqm
                    tbank.mm([lambda e, j=j, half=half: e.transpose(
                        tbank.ap[:, (half * 4 + j) * 128:(half * 4 + j + 1) * 128],
                        small[:, half * 512 + j * 128:half * 512 + (j + 1) * 128],
                        ident[:]) for j in range(4)], extra_waits=(wqm,))
                    sq = f"dms{6 + half}"
                    for j in range(4):
                        tbank.read("v", lambda e, j=j, half=half: e.tensor_copy(
                            small[:, 2048 + half * 512 + j * 128:
                                  2048 + half * 512 + (j + 1) * 128],
                            tbank.ap[:, (half * 4 + j) * 128:(half * 4 + j + 1) * 128]),
                            extra_waits=((sq, p.semv[sq]),) if j == 0 else ())
                    trm = p.mark("v")
                    p.op("g", lambda e, ot=ot, ic=ic, half=half, dst=dst, I=I:
                         e.dma_start(_lpbq_out_ap(dst, ot, ic),
                                     small[:, 2048 + half * 512:2048 + (half + 1) * 512]
                                     .rearrange("p (a b) -> p a b", a=4)),
                         waits=(trm,), incs=(sq,), dma=True)
                    tile += 1
        dbar(("dms6", "dms7"))

    p.tag = f"f{idx}ln"
    lnm = emit_layernorm(ctx)

    p.tag = f"f{idx}xq"
    mcol = smf[:, 4:5]
    p.op("v", lambda e: e.tensor_tensor(statA[:], xq[:, 0, 0:1024], xq[:, 0, 0:1024],
                                        ALU.mult), waits=(lnm,), incs=("fin_v",))
    for i in range(NDT):
        for c in range(2):
            if i == 0 and c == 0:
                continue
            p.op("v", lambda e, i=i, c=c: e.tensor_tensor(
                statB[:], xq[:, i, c * 1024:(c + 1) * 1024],
                xq[:, i, c * 1024:(c + 1) * 1024], ALU.mult),
                waits=(p.mark("v"),), incs=("fin_v",))
            p.op("v", lambda e: e.tensor_max(statA[:], statA[:], statB[:]),
                 waits=(p.mark("v"),), incs=("fin_v",))
    p.op("v", lambda e: e.tensor_reduce(col2d(mcol), statA[:], axis=AXX, op=ALU.max),
         waits=(p.mark("v"),), incs=("fin_v",))
    _sqrt_col(ctx, mcol, 1)
    arm = _allreduce_max(ctx, [mcol])
    p.op("v", lambda e: e.tensor_scalar_mul(cpool[:, 8:9], lcol[:, 64:65], 1.0 / 32767.0),
         waits=(arm,), incs=("fin_v",))
    p.op("v", lambda e: e.reciprocal(cpool[:, 9:10], cpool[:, 8:9]),
         waits=(p.mark("v"),), incs=("fin_v",))
    scm = p.mark("v")
    for i in range(NDT):
        for c in range(2):
            sl = slice(c * 1024, (c + 1) * 1024)
            p.op("v", lambda e, i=i, sl=sl: e.tensor_scalar(
                statA[:], xq[:, i, sl], cpool[:, 9:10], MAGIC, ALU.mult, ALU.add),
                waits=(scm,) if i == 0 and c == 0 else (p.mark("v"),), incs=("fin_v",))
            p.op("v", lambda e: e.tensor_scalar(statA[:], statA[:], MAGIC, -32768.0,
                                                ALU.subtract, ALU.max),
                 waits=(p.mark("v"),), incs=("fin_v",))
            p.op("v", lambda e, i=i, sl=sl: e.tensor_scalar_min(
                xq[:, i, sl], statA[:], 32767.0),
                 waits=(p.mark("v"),), incs=("fin_v",))
    p.op("v", lambda e: e.drain(), incs=("fin_v",))
    qm = p.mark("v")

    p.tag = f"f{idx}p1"
    p.op("v", lambda e: e.memset(smf[:, 5:7], 0.0), incs=("fin_v",))
    t0p = p.mark("t")
    mmarks = {}
    for ot in range(NIT):
        slot_g, slot_u = (ot % 2) * 2, (ot % 2) * 2 + 1
        sg, su = f"dms{slot_g}", f"dms{slot_u}"
        p.op("g", lambda e, ot=ot, slot_g=slot_g: e.dma_start(
            wflat[:, slot_g * 2048:slot_g * 2048 + 1024]
            .rearrange("p (a b) -> p a b", a=8),
            ctx["wq_lg"][:, ot, :, :].rearrange("a p o -> p a o")),
            waits=(mmarks.get(ot - 2, t0p),), incs=(sg,), dma=True)
        p.op("g", lambda e, ot=ot, slot_u=slot_u: e.dma_start(
            wflat[:, slot_u * 2048:slot_u * 2048 + 1024]
            .rearrange("p (a b) -> p a b", a=8),
            ctx["wq_lu"][:, ot, :, :].rearrange("a p o -> p a o")),
            incs=(su,), dma=True)
        wmk = [(sg, p.semv[sg]), (su, p.semv[su])]
        for c in range(4):
            ts = slice(c * 512, (c + 1) * 512)
            for slot, bidx, stv, scr, sy, mc in (
                    (slot_g, 0, gav, ctx["gtr"], f"dmy{c % 2}", 6),
                    (slot_u, 1, ubv, ctx["upr"], f"dmy{2 + c % 2}", 5)):
                bk = bank[bidx]
                bk.mm([lambda e, i=i, slot=slot, ts=ts, bk=bk:
                       e.matmul(bk.ap[:], wflat[:, slot * 2048 + i * 128:slot * 2048 + (i + 1) * 128],
                                xq[:, i, ts], start=(i == 0), stop=(i == NDT - 1))
                       for i in range(NDT)],
                      extra_waits=tuple(wmk) + ((qm,) if ot == 0 and c == 0 else ()))
                st = stv[:, c % 2, :]
                bk.read("v", lambda e, bk=bk, st=st: e.tensor_scalar_mul(
                    st, bk.ap[:], cpool[:, 8:9]),
                    extra_waits=(p.mark("v"), (sy, p.semv[sy])))
                em = p.mark("v")
                p.op("y", lambda e, ot=ot, ts=ts, st=st, scr=scr: e.dma_start(
                    scr[ot, :, ts], st), waits=(em,), incs=(sy,), dma=True)
                p.op("v", lambda e, st=st: e.tensor_tensor(
                    statA[:, 0:512], st, st, ALU.mult),
                    waits=(p.mark("v"),), incs=("fin_v",))
                p.op("v", lambda e: e.tensor_reduce(col2d(smf[:, 7:8]),
                                                    statA[:, 0:512],
                                                    axis=AXX, op=ALU.max),
                     waits=(p.mark("v"),), incs=("fin_v",))
                p.op("v", lambda e, mc=mc: e.tensor_max(smf[:, mc:mc + 1],
                                                        smf[:, mc:mc + 1],
                                                        smf[:, 7:8]),
                     waits=(p.mark("v"),), incs=("fin_v",))
        mmarks[ot] = p.mark("t")
    _sqrt_col(ctx, smf[:, 5:7], 2)
    arm2 = _allreduce_max(ctx, [smf[:, 5:6], smf[:, 6:7]])
    p.op("v", lambda e: e.tensor_scalar_mul(cpool[:, 10:11], lcol[:, 64:65],
                                            1.0 / 32767.0), waits=(arm2,),
         incs=("fin_v",))
    p.op("v", lambda e: e.tensor_scalar_mul(cpool[:, 12:13], lcol[:, 65:66],
                                            1.0 / 32767.0),
         waits=(p.mark("v"),), incs=("fin_v",))
    p.op("v", lambda e: e.reciprocal(cpool[:, 11:12], cpool[:, 10:11]),
         waits=(p.mark("v"),), incs=("fin_v",))
    p.op("v", lambda e: e.reciprocal(cpool[:, 13:14], cpool[:, 12:13]),
         waits=(p.mark("v"),), incs=("fin_v",))
    p.op("v", lambda e: e.memset(smf[:, 5:6], 0.0), waits=(p.mark("v"),),
         incs=("fin_v",))
    sc2 = p.mark("v")

    p.tag = f"f{idx}p2"
    spw = [(f"dmy{i}", p.semv[f"dmy{i}"]) for i in range(4)]
    for k in range(96):
        c, ot = divmod(k, NIT)
        ts = slice(c * 512, (c + 1) * 512)
        gld = gav[:, 2 + k % 2, :]
        uld = ubv[:, 2 + k % 2, :]
        sgl, sul = f"dms{k % 2}", f"dms{2 + k % 2}"
        syg, syu = f"dmy{k % 2}", f"dmy{2 + k % 2}"
        ldw = (spw if k < 2 else []) + [(syg, p.semv[syg]), (syu, p.semv[syu])]
        p.op("g", lambda e, ot=ot, ts=ts, gld=gld: e.dma_start(
            gld, ctx["gtr"][ot, :, ts]),
            waits=tuple(ldw), incs=(sgl,), dma=True)
        p.op("g", lambda e, ot=ot, ts=ts, uld=uld: e.dma_start(
            uld, ctx["upr"][ot, :, ts]), incs=(sul,), dma=True)
        ldm = [(sgl, p.semv[sgl]), (sul, p.semv[sul])]
        p.op("v", lambda e, gld=gld: e.tensor_scalar(
            statA[:, 0:512], gld, cpool[:, 13:14], MAGIC, ALU.mult, ALU.add),
            waits=tuple(ldm) + ((sc2,) if k < 2 else (p.mark("v"),)),
            incs=("fin_v",))
        p.op("v", lambda e: e.tensor_scalar(statA[:, 0:512], statA[:, 0:512],
                                            MAGIC, -32768.0, ALU.subtract, ALU.max),
             waits=(p.mark("v"),), incs=("fin_v",))
        p.op("v", lambda e: e.tensor_scalar(statA[:, 0:512], statA[:, 0:512],
                                            32767.0, 1.0, ALU.min, ALU.mult),
             waits=(p.mark("v"),), incs=("fin_v",))
        p.op("v", lambda e, gld=gld: e.tensor_scalar_mul(gld, statA[:, 0:512],
                                                         cpool[:, 12:13]),
             waits=(p.mark("v"),), incs=("fin_v",))
        gdm = p.mark("v")
        p.op("s", lambda e, gld=gld: e.activation(statB[:, 0:512], gld,
                                                  AF.Sigmoid, bias=0.0, scale=1.0),
             waits=(gdm, p.mark("v")), incs=("fin_s",))
        sgm = p.mark("s")
        p.op("v", lambda e, uld=uld: e.tensor_scalar(
            statA[:, 512:1024], uld, cpool[:, 11:12], MAGIC, ALU.mult, ALU.add),
            waits=(p.mark("v"),), incs=("fin_v",))
        p.op("v", lambda e: e.tensor_scalar(statA[:, 512:1024], statA[:, 512:1024],
                                            MAGIC, -32768.0, ALU.subtract, ALU.max),
             waits=(p.mark("v"),), incs=("fin_v",))
        p.op("v", lambda e: e.tensor_scalar_min(statA[:, 512:1024],
                                                statA[:, 512:1024], 32767.0),
             waits=(p.mark("v"),), incs=("fin_v",))
        p.op("v", lambda e, uld=uld: e.tensor_scalar_mul(uld, statA[:, 512:1024],
                                                         cpool[:, 10:11]),
             waits=(p.mark("v"),), incs=("fin_v",))
        p.op("v", lambda e: e.tensor_scalar(statB[:, 0:512], statB[:, 0:512],
                                            65536.0, MAGIC, ALU.mult, ALU.add),
             waits=(sgm, p.mark("v")), incs=("fin_v",))
        p.op("v", lambda e: e.tensor_scalar(statB[:, 0:512], statB[:, 0:512],
                                            MAGIC, 0.0, ALU.subtract, ALU.max),
             waits=(p.mark("v"),), incs=("fin_v",))
        p.op("v", lambda e: e.tensor_scalar(statB[:, 0:512], statB[:, 0:512],
                                            65535.0, 1.0 / 65536.0, ALU.min,
                                            ALU.mult),
             waits=(p.mark("v"),), incs=("fin_v",))
        p.op("v", lambda e, gld=gld: e.tensor_tensor(gld, gld, statB[:, 0:512],
                                                     ALU.mult),
             waits=(p.mark("v"),), incs=("fin_v",))
        p.op("v", lambda e, gld=gld: e.tensor_tensor(statA[:, 0:512], gld, gld,
                                                     ALU.mult),
             waits=(p.mark("v"),), incs=("fin_v",))
        p.op("v", lambda e: e.tensor_reduce(col2d(smf[:, 7:8]), statA[:, 0:512],
                                            axis=AXX, op=ALU.max),
             waits=(p.mark("v"),), incs=("fin_v",))
        p.op("v", lambda e: e.tensor_max(smf[:, 5:6], smf[:, 5:6], smf[:, 7:8]),
             waits=(p.mark("v"),), incs=("fin_v",))
        g2m = p.mark("v")
        p.op("y", lambda e, ot=ot, ts=ts, gld=gld: e.dma_start(
            ctx["gtr"][ot, :, ts], gld), waits=(g2m,), incs=(syg,), dma=True)
        p.op("y", lambda e, ot=ot, ts=ts, uld=uld: e.dma_start(
            ctx["upr"][ot, :, ts], uld), incs=(syu,), dma=True)
    _sqrt_col(ctx, smf[:, 5:6], 3)
    arm3 = _allreduce_max(ctx, [smf[:, 5:6]])
    p.op("v", lambda e: e.tensor_scalar_mul(cpool[:, 10:11], lcol[:, 64:65],
                                            1.0 / 32767.0), waits=(arm3,),
         incs=("fin_v",))
    p.op("v", lambda e: e.reciprocal(cpool[:, 11:12], cpool[:, 10:11]),
         waits=(p.mark("v"),), incs=("fin_v",))
    p.op("v", lambda e: e.memset(smf[:, 6:7], 0.0), waits=(p.mark("v"),),
         incs=("fin_v",))
    sc3 = p.mark("v")

    p.tag = f"f{idx}p3"
    spw = [(f"dmy{i}", p.semv[f"dmy{i}"]) for i in range(4)]
    for k in range(96):
        c, ot = divmod(k, NIT)
        ts = slice(c * 512, (c + 1) * 512)
        gld = gav[:, 2 + k % 2, :]
        uld = ubv[:, 2 + k % 2, :]
        sgl, sul = f"dms{k % 2}", f"dms{2 + k % 2}"
        syg = f"dmy{k % 2}"
        ldw = (spw if k < 2 else []) + [(syg, p.semv[syg])]
        p.op("g", lambda e, ot=ot, ts=ts, gld=gld: e.dma_start(
            gld, ctx["gtr"][ot, :, ts]),
            waits=tuple(ldw), incs=(sgl,), dma=True)
        p.op("g", lambda e, ot=ot, ts=ts, uld=uld: e.dma_start(
            uld, ctx["upr"][ot, :, ts]), incs=(sul,), dma=True)
        ldm = [(sgl, p.semv[sgl]), (sul, p.semv[sul])]
        p.op("v", lambda e, gld=gld: e.tensor_scalar(
            statA[:, 0:512], gld, cpool[:, 11:12], MAGIC, ALU.mult, ALU.add),
            waits=tuple(ldm) + ((sc3,) if k < 2 else (p.mark("v"),)),
            incs=("fin_v",))
        p.op("v", lambda e: e.tensor_scalar(statA[:, 0:512], statA[:, 0:512],
                                            MAGIC, -32768.0, ALU.subtract, ALU.max),
             waits=(p.mark("v"),), incs=("fin_v",))
        p.op("v", lambda e: e.tensor_scalar(statA[:, 0:512], statA[:, 0:512],
                                            32767.0, 1.0, ALU.min, ALU.mult),
             waits=(p.mark("v"),), incs=("fin_v",))
        p.op("v", lambda e: e.tensor_scalar_mul(statA[:, 0:512], statA[:, 0:512],
                                                cpool[:, 10:11]),
             waits=(p.mark("v"),), incs=("fin_v",))
        p.op("v", lambda e, gld=gld, uld=uld: e.tensor_tensor(
            gld, statA[:, 0:512], uld, ALU.mult),
             waits=(p.mark("v"),), incs=("fin_v",))
        p.op("v", lambda e, gld=gld: e.tensor_tensor(statA[:, 0:512], gld, gld,
                                                     ALU.mult),
             waits=(p.mark("v"),), incs=("fin_v",))
        p.op("v", lambda e: e.tensor_reduce(col2d(smf[:, 7:8]), statA[:, 0:512],
                                            axis=AXX, op=ALU.max),
             waits=(p.mark("v"),), incs=("fin_v",))
        p.op("v", lambda e: e.tensor_max(smf[:, 6:7], smf[:, 6:7], smf[:, 7:8]),
             waits=(p.mark("v"),), incs=("fin_v",))
        o3 = p.mark("v")
        p.op("y", lambda e, ot=ot, ts=ts, gld=gld: e.dma_start(
            ctx["gtr"][ot, :, ts], gld), waits=(o3,), incs=(syg,), dma=True)
    _sqrt_col(ctx, smf[:, 6:7], 4)
    arm4 = _allreduce_max(ctx, [smf[:, 6:7]])
    p.op("v", lambda e: e.tensor_scalar_mul(cpool[:, 12:13], lcol[:, 64:65],
                                            1.0 / 32767.0), waits=(arm4,),
         incs=("fin_v",))
    p.op("v", lambda e: e.reciprocal(cpool[:, 13:14], cpool[:, 12:13]),
         waits=(p.mark("v"),), incs=("fin_v",))
    sc4 = p.mark("v")

    p.tag = f"f{idx}p4"
    spw = [(f"dmy{i}", p.semv[f"dmy{i}"]) for i in range(4)]
    cmark = {}
    wdv = wflat.rearrange("p (s x y) -> p s x y", s=2, x=32)
    for c in range(4):
        ts = slice(c * 512, (c + 1) * 512)
        buf = gav if c % 2 == 0 else ubv
        sld = f"dms{c % 2}"
        ldw = (spw if c == 0 else []) + [(sld, p.semv[sld]),
                                         cmark.get(c - 2, p.mark("t") if c < 2 else None)]
        ldw = [w for w in ldw if w is not None]
        for ot in range(NIT):
            p.op("g", lambda e, ot=ot, ts=ts, buf=buf: e.dma_start(
                buf[:, ot, :], ctx["gtr"][ot, :, ts]),
                waits=tuple(ldw) + ((sc4,) if c < 2 and ot == 0 else ())
                if ot == 0 else (), incs=(sld,), dma=True)
        ldall = [(sld, p.semv[sld])]
        for ot in range(NIT):
            p.op("v", lambda e, ot=ot, buf=buf: e.tensor_scalar(
                statA[:, 0:512], buf[:, ot, :], cpool[:, 13:14], MAGIC,
                ALU.mult, ALU.add),
                waits=tuple(ldall) + ((sc4,) if c == 0 and ot == 0 else (p.mark("v"),)),
                incs=("fin_v",))
            p.op("v", lambda e: e.tensor_scalar(statA[:, 0:512], statA[:, 0:512],
                                                MAGIC, -32768.0, ALU.subtract,
                                                ALU.max),
                 waits=(p.mark("v"),), incs=("fin_v",))
            p.op("v", lambda e, ot=ot, buf=buf: e.tensor_scalar_min(
                buf[:, ot, :], statA[:, 0:512], 32767.0),
                 waits=(p.mark("v"),), incs=("fin_v",))
        oqm = p.mark("v")
        wmms = {}
        for mt in range(NDT):
            sd = f"dms{4 + mt % 2}"
            p.op("g", lambda e, mt=mt, wdv=wdv: e.dma_start(
                wdv[:, mt % 2, 0:24, :],
                ctx["wq_ld"][:, mt, :, :].rearrange("a p o -> p a o")),
                waits=(wmms.get(mt - 2, p.mark("t")),), incs=(sd,), dma=True)
            wmk = [(sd, p.semv[sd])]
            bk = bank[mt % 2]
            bk.mm([lambda e, it=it, mt=mt, bk=bk, wdv=wdv, buf=buf:
                   e.matmul(bk.ap[:], wdv[:, mt % 2, it, :],
                            buf[:, it, :], start=(it == 0), stop=(it == NIT - 1))
                   for it in range(NIT)],
                  extra_waits=tuple(wmk) + ((oqm,) if mt < 2 else ()))
            wmms[mt] = p.mark("t")
            bk.read("v", lambda e, mt=mt, ts=ts, bk=bk: e.scalar_tensor_tensor(
                xres[:, mt, ts], bk.ap[:], cpool[:, 12:13], xres[:, mt, ts],
                ALU.mult, ALU.add), extra_waits=(p.mark("v"),))
        cmark[c] = p.mark("t")
        p.op("v", lambda e: e.drain(), incs=("fin_v",))
    return p.mark("v")


def _sqrt_col(ctx, ap, _tag):
    p = ctx["p"]
    p.op("v", lambda e, ap=ap: e.tensor_scalar_max(ap, ap, 1e-24),
         waits=(p.mark("v"),), incs=("fin_v",))
    vq = p.mark("v")
    p.op("s", lambda e, ap=ap: e.activation(ap, ap, AF.Ln, bias=0.0, scale=1.0),
         waits=(vq,), incs=("fin_s",))
    p.op("s", lambda e, ap=ap: e.activation(ap, ap, AF.Exp, bias=0.0, scale=0.5),
         waits=(p.mark("s"),), incs=("fin_s",))
    sm = p.mark("s")
    p.op("v", lambda e: e.drain(), waits=(sm,), incs=("fin_v",))


def _lpbq_out_ap(dst, ot, ic):
    nt = dst.shape[1]
    base = ((ic * 4) * nt + ot) * 128 * 128
    return bass.AP(dst.tensor, base, [[128, 128], [nt * 128 * 128, 4], [1, 128]])


def _allreduce_max(ctx, cols):
    nc, p = ctx["nc"], ctx["p"]
    lcol, aux = ctx["lcol"], ctx["aux"]
    tbank, bank = ctx["tbank"], ctx["bank"]
    ident, ones_row = ctx["ident"], ctx["ones_row"]
    n = len(cols)
    for j, colap in enumerate(cols):
        p.op("v", lambda e, j=j, colap=colap: e.tensor_copy(aux[:, j:j + 1], colap),
             waits=(p.mark("v"), p.mark("s")), incs=("fin_v",))
    p.op("v", lambda e: e.memset(aux[0:1, 4:8], 0.0),
         waits=(p.mark("v"),), incs=("fin_v",))
    cm = p.mark("v")
    tbank.mm([lambda e, j=j: e.transpose(
        tbank.ap[0:1, j * 128:(j + 1) * 128], aux[:, j:j + 1], ident[:])
        for j in range(n)], extra_waits=(cm,))
    for j in range(n):
        tbank.read("v", lambda e, j=j: e.tensor_reduce(
            col2d(aux[0:1, 4 + j:5 + j]), tbank.ap[0:1, j * 128:(j + 1) * 128],
            axis=AXX, op=ALU.max))
    p.op("v", lambda e: e.drain(), incs=("fin_v",))
    rm = p.mark("v")
    p.op("y", lambda e: e.dma_start(ctx["cc_in"][0:1, 0:4], aux[0:1, 4:8]),
         waits=(rm, ("dma", p.semv["dma"])), incs=("dma",), dma=True)
    dm = [("dma", p.semv["dma"])]
    p.op("g", lambda e: e.collective_compute(
        "AllReduce", ALU.max, replica_groups=[list(range(N_CORES))],
        ins=[ctx["cc_in"][0:1, 0:4]], outs=[ctx["cc_out"][0:1, 0:4]]),
        waits=dm, incs=("fin_g",))
    gm = p.mark("g")
    p.op("y", lambda e: e.dma_start(aux[0:1, 8:12], ctx["cc_out"][0:1, 0:4]),
         waits=(gm, ("dma", p.semv["dma"])), incs=("dma",), dma=True)
    dm2 = [("dma", p.semv["dma"])]
    bk = bank[2]
    bk.mm([lambda e, n=n, bk=bk: e.matmul(bk.ap[:, 0:n], ones_row[0:1, 0:128],
                                          aux[0:1, 8:8 + n], start=True, stop=True)],
          extra_waits=tuple(dm2))
    bk.read("v", lambda e, n=n, bk=bk: e.tensor_copy(lcol[:, 64:64 + n],
                                                     bk.ap[:, 0:n]))
    p.op("v", lambda e: e.drain(), incs=("fin_v",))
    return p.mark("v")


def emit_l1_attn_b(ctx, idx):
    nc, p, X = ctx["nc"], ctx["p"], ctx["X"]
    xloc = ctx["xn"]
    bigA, bigB = ctx["bigA"], ctx["bigB"]
    aux = ctx["aux"]
    wbuf, small, smf = ctx["wbuf"], ctx["small"], ctx["smf"]
    lcol = ctx["lcol"]
    xres = ctx["xres"]
    ident, ones_row = ctx["ident"], ctx["ones_row"]
    bank, dbank, tbank = ctx["bank"], ctx["dbank"], ctx["tbank"]
    DI = 2048
    NDI = DI // 128

    if "xa_scr" not in ctx:
        ctx["xa_scr"] = nc.dram_tensor("xa_scr", [NDI, 128, T], BF16).ap()
        ctx["y_scr"] = nc.dram_tensor("y_scr", [NDI, 128, T], BF16).ap()
    xa_scr, y_scr = ctx["xa_scr"], ctx["y_scr"]

    lnm = emit_layernorm(ctx)
    for i in range(NDT):
        p.op("v", lambda e, i=i: e.tensor_add(xres[:, i, :], xres[:, i, :],
                                              xloc[:, i, :]),
             waits=(lnm,) if i == 0 else (), incs=("fin_v",) if i == NDT - 1 else ())
    p.op("v", lambda e: e.drain(), incs=("fin_v",))

    wflat = wbuf[:].rearrange("p a t -> p (a t)")
    selnB = wflat[0:16, 4096:6144]
    selnC = wflat[32:48, 4096:6144]
    wdtT = wflat[64:128, 6144:8192]
    p.op("g", lambda e: e.dma_start(wflat[0:48, 4096:6144], X["c_seln"]),
         waits=(p.mark("t"),), incs=("dmag",), dma=True)
    p.op("v", lambda e: e.drain(), waits=[("dmag", p.semv["dmag"])],
         incs=("fin_v",))

    xdbl = small[:, 0:T]
    xiw = small[:, 2048:2568]
    xaw = small[:, 2568:3080]
    zsw = small[:, 3080:3592]
    carry = smf[:, 0:16]

    for L in range(5):
        lm = p.mark("v")
        p.op("y", lambda e, L=L: e.dma_start(lcol[:, 0:256],
                                             X["L1ab_A_pp"][L]),
             waits=(lm,), incs=("dma",), dma=True)
        p.op("y", lambda e, L=L: e.dma_start(lcol[:, 256:368],
                                             X["L1ab_cols"][L]),
             incs=("dma",), dma=True)
        dm = [("dma", p.semv["dma"])]
        p.op("s", lambda e: e.activation(lcol[:, 0:256], lcol[:, 0:256], AF.Exp,
                                         bias=0.0, scale=1.0),
             waits=dm, incs=("fin_s",))
        sm = p.mark("s")
        p.op("v", lambda e: e.tensor_scalar_mul(lcol[:, 0:256], lcol[:, 0:256], -1.0),
             waits=(sm,), incs=("fin_v",))
        acm = p.mark("v")
        p.op("g", lambda e, L=L: e.dma_start(wdtT, X["L1ab_wdtT"][L]),
             waits=(p.mark("t"),), incs=("dms7",), dma=True)
        wdtm = [("dms7", p.semv["dms7"])]

        p.tag = f"s6a{L}"
        xdbk = [dbank[0], dbank[0], dbank[1], dbank[1]]
        diagb = [aux[:, 0:512], aux[:, 768:1280]]
        wxb = [aux[:, 512:640], aux[:, 640:768]]
        lastwin = {}
        lastcnv = {}
        lastwx = {}
        t0a, v0a = p.mark("t"), p.mark("v")
        for dt in range(NDI):
            slot = dt % 2
            p.op("g", lambda e, L=L, dt=dt, slot=slot: e.dma_start(
                wflat[:, slot * 2048:slot * 2048 + 1024]
                .rearrange("p (a b) -> p a b", a=8),
                X["L1ab_win"][L, dt].rearrange("a p o -> p a o")),
                waits=(lastwin.get(dt - 2, t0a), v0a), incs=(f"dms{slot}",),
                dma=True)
            wm = [(f"dms{slot}", p.semv[f"dms{slot}"])]
            p.op("g", lambda e, L=L, dt=dt, slot=slot: e.dma_start(
                wxb[slot], X["L1ab_wx"][L, dt]),
                waits=(lastwx.get(dt - 2, t0a), v0a), incs=("dms3",), dma=True)
            wxm = [("dms3", p.semv["dms3"])]
            dgm0 = p.mark("v")
            for j in range(4):
                p.op("v", lambda e, j=j, dt=dt, slot=slot: e.tensor_tensor(
                    diagb[slot][:, j * 128:(j + 1) * 128], ident[:],
                    cbc(lcol[:, 304 + dt * 4 + j:304 + dt * 4 + j + 1], 128),
                    ALU.mult),
                    waits=(dgm0, acm, lastcnv.get(dt - 2, t0a)) if j == 0 else (),
                    incs=("fin_v",) if j == 3 else ())
            dgm = p.mark("v")
            if dt == 0:
                p.op("v", lambda e: e.memset(xiw[:, 0:4], 0.0), incs=("fin_v",))
            for c4 in range(4):
                ts = slice(c4 * 512, (c4 + 1) * 512)
                bk = bank[c4 % 2]
                bk.mm([lambda e, i=i, L=L, dt=dt, ts=ts, bk=bk, slot=slot:
                       e.matmul(bk.ap[:], wflat[:, slot * 2048 + i * 128:slot * 2048 + (i + 1) * 128],
                                xloc[:, i, ts], start=(i == 0), stop=(i == NDT - 1))
                       for i in range(NDT)],
                      extra_waits=tuple(wm) if c4 == 0 else ())
                if c4 == 3:
                    lastwin[dt] = p.mark("t")
                bk.read("v", lambda e, bk=bk: e.tensor_copy(xiw[:, 4:516], bk.ap[:]),
                        extra_waits=(p.mark("v"), p.mark("t")))
                xim = p.mark("v")
                cb2 = bank[2]
                cb2.mm([lambda e, j=j, cb2=cb2, slot=slot: e.matmul(
                    cb2.ap[:], diagb[slot][:, j * 128:(j + 1) * 128],
                    xiw[:, 1 + j:513 + j], start=(j == 0), stop=(j == 3))
                    for j in range(4)], extra_waits=(xim, dgm))
                if c4 == 3:
                    lastcnv[dt] = p.mark("t")
                cb2.read("s", lambda e, dt=dt, cb2=cb2: e.activation(
                    xaw[:], cb2.ap[:], AF.Silu,
                    bias=lcol[:, 272 + dt:273 + dt], scale=1.0),
                    extra_waits=(p.mark("v"), ("dms2", p.semv["dms2"])))
                ssm = p.mark("s")
                p.op("v", lambda e: e.tensor_copy(xiw[:, 0:4], xiw[:, 512:516]),
                     waits=(p.mark("s"),), incs=("fin_v",))
                p.op("g", lambda e, L=L, dt=dt, ts=ts: e.dma_start(
                    xa_scr[dt, :, ts], xaw[:]),
                    waits=(ssm,), incs=("dms2",), dma=True)
                xbk = xdbk[c4]
                half = (c4 % 2) * 512
                xbk.mm([lambda e, xbk=xbk, half=half, dt=dt, slot=slot: e.matmul(
                    xbk.ap[:, half:half + 512], wxb[slot], xaw[:],
                    start=(dt == 0), stop=(dt == NDI - 1))],
                    extra_waits=tuple(wxm) + (ssm,))
                if c4 == 3:
                    lastwx[dt] = p.mark("t")
        for c4 in range(4):
            xbk = xdbk[c4]
            half = (c4 % 2) * 512
            xbk.read("v", lambda e, xbk=xbk, c4=c4, half=half: e.tensor_copy(
                xdbl[:, c4 * 512:(c4 + 1) * 512], xbk.ap[:, half:half + 512]))
        p.op("v", lambda e: e.drain(), incs=("fin_v",))
        xdm = p.mark("v")

        p.tag = f"s6b{L}"
        p.op("v", lambda e: e.drain(), waits=[("dms2", p.semv["dms2"])],
             incs=("fin_v",))
        for ch in range(2):
            tsl = slice(ch * 1024, (ch + 1) * 1024)
            brep = bigB[:].rearrange("p a t -> p (a t)") \
                .rearrange("p (n t) -> p n t", n=16)
            crep = bigA[:].rearrange("p a t -> p (a t)") \
                .rearrange("p (n t) -> p n t", n=16)
            for which, rep, rbase in (("b", brep, 0), ("c", crep, 32)):
                seln = selnB if which == "b" else selnC
                for n in range(16):
                    bk = bank[2]
                    bk.mm([lambda e, n=n, rbase=rbase, tsl=tsl, bk=bk, seln=seln:
                           e.matmul(bk.ap[:], seln[:, n * 128:(n + 1) * 128],
                                    xdbl[rbase:rbase + 16, tsl.start:tsl.start + 512],
                                    start=True, stop=True)],
                        extra_waits=(xdm,) if n == 0 else ())
                    bk.read("v", lambda e, n=n, rep=rep, bk=bk: e.tensor_copy(
                        rep[:, n, 0:512], bk.ap[:]))
                    bk2 = bank[1]
                    bk2.mm([lambda e, n=n, rbase=rbase, tsl=tsl, bk2=bk2, seln=seln:
                            e.matmul(bk2.ap[:], seln[:, n * 128:(n + 1) * 128],
                                     xdbl[rbase:rbase + 16, tsl.start + 512:tsl.start + 1024],
                                     start=True, stop=True)],
                        extra_waits=(xdm,) if n == 0 else ())
                    bk2.read("v", lambda e, n=n, rep=rep, bk2=bk2: e.tensor_copy(
                        rep[:, n, 512:1024], bk2.ap[:]))
            p.op("v", lambda e: e.drain(), incs=("fin_v",))
            bcm = p.mark("v")

            hbuf = aux[:, 0:1024]
            hcbuf = aux[:, 1024:2048]
            dcol = aux[:, 2048:3072]
            ucol = aux[:, 3072:4096]
            xat = aux[:, 4096:5120]
            dan = aux[:, 5120:6144]
            for dt in range(NDI):
                bkd = dbank[0]
                bkd.mm([lambda e, dt=dt, tsl=tsl, s2=s2, bkd=bkd: e.matmul(
                    bkd.ap[:, s2 * 512:(s2 + 1) * 512],
                    wdtT[:, dt * 128:(dt + 1) * 128],
                    xdbl[64:128, tsl.start + s2 * 512:tsl.start + (s2 + 1) * 512],
                    start=True, stop=True)
                    for s2 in range(2)],
                    extra_waits=tuple(wdtm) + ((xdm,) if dt == 0 and ch == 0 else ()))
                bkd.read("v", lambda e, dt=dt, bkd=bkd: e.tensor_scalar_add(
                    ctx["statA"][:], bkd.ap[0:128, 0:1024], lcol[:, 256 + dt:257 + dt]),
                    extra_waits=(p.mark("v"),))
                p.op("v", lambda e: e.tensor_scalar_mul(ctx["statB"][:], ctx["statA"][:],
                                                        -1.0),
                     waits=(p.mark("v"),), incs=("fin_v",))
                p.op("v", lambda e: e.tensor_max(ctx["statB"][:], ctx["statB"][:],
                                                 ctx["statA"][:]),
                     waits=(p.mark("v"),), incs=("fin_v",))
                vsp = p.mark("v")
                p.op("s", lambda e: e.activation(ctx["statB"][:], ctx["statB"][:],
                                                 AF.Exp, bias=0.0, scale=-1.0),
                     waits=(vsp,), incs=("fin_s",))
                p.op("s", lambda e: e.activation(ctx["statB"][:], ctx["statB"][:],
                                                 AF.Ln, bias=1.0, scale=1.0),
                     waits=(p.mark("s"),), incs=("fin_s",))
                ssp = p.mark("s")
                p.op("v", lambda e: e.scalar_tensor_tensor(
                    dcol, ctx["statA"][:], 0.0, ctx["statB"][:], ALU.max, ALU.add),
                     waits=(ssp,), incs=("fin_v",))
                ddm = p.mark("v")
                p.op("g", lambda e, dt=dt, tsl=tsl: e.dma_start(xat, xa_scr[dt, :, tsl]),
                     waits=(p.mark("v"),), incs=("dms4",), dma=True)
                xam = [("dms4", p.semv["dms4"])]
                p.op("v", lambda e: e.tensor_tensor(ucol, dcol, xat, ALU.mult),
                     waits=tuple(xam) + (ddm,), incs=("fin_v",))
                um = p.mark("v")
                danb = [dan, small[:, 2048:3072]]
                hcb = [hcbuf, small[:, 3072:4096]]
                tm_dt = p.mark("t")
                ybk = dbank[1]
                scanmarks = {}
                ymmarks = {}
                pend_ymm = [None]
                for n in range(16):
                    danp, hcp = danb[n % 2], hcb[n % 2]
                    expw = [ddm, acm]
                    if n >= 2:
                        expw.append(scanmarks[n - 2])
                    else:
                        expw.append(("dms5", p.semv["dms5"]))
                    p.op("s", lambda e, n=n, dt=dt, danp=danp: e.activation(
                        danp, dcol, AF.Exp, bias=0.0,
                        scale=lcol[:, dt * 16 + n:dt * 16 + n + 1]),
                        waits=tuple(expw), incs=("fin_s",))
                    dam = p.mark("s")
                    p.op("v", lambda e, n=n, hcp=hcp: e.tensor_tensor(
                        hcp, ucol, brep[:, n, :], ALU.mult),
                        waits=(um, bcm, p.mark("v"),
                               ymmarks[n - 2] if n >= 2 else tm_dt),
                        incs=("fin_v",))
                    with nc.allow_low_precision(reason="scan state is fp32 internal; bf16 io"):
                        p.op("v", lambda e, n=n, ch=ch, danp=danp, hcp=hcp:
                             e.tensor_tensor_scan(
                                 hbuf, danp, hcp,
                                 0.0 if ch == 0 else carry[:, n:n + 1],
                                 ALU.mult, ALU.add),
                             waits=(dam, p.mark("v")),
                             incs=("fin_v",))
                    scanmarks[n] = p.mark("v")
                    if ch == 0:
                        p.op("v", lambda e, n=n: e.tensor_copy(carry[:, n:n + 1],
                                                               hbuf[:, 1023:1024]),
                             waits=(p.mark("v"),), incs=("fin_v",))
                    p.op("v", lambda e, n=n, hcp=hcp: e.tensor_tensor(
                        hcp, hbuf, crep[:, n, :], ALU.mult),
                         waits=(p.mark("v"),), incs=("fin_v",))
                    hcm = p.mark("v")

                    def emit_ymm(n=n, hcp=hcp, hcm=hcm):
                        ybk.mm([lambda e, n=n, s2=s2, ybk=ybk, hcp=hcp: e.matmul(
                            ybk.ap[:, s2 * 512:(s2 + 1) * 512], ident[:],
                            hcp[:, s2 * 512:(s2 + 1) * 512],
                            start=(n == 0), stop=(n == 15))
                            for s2 in range(2)], extra_waits=(hcm,))
                        ymmarks[n] = p.mark("t")
                    if pend_ymm[0] is not None:
                        this_ymm = emit_ymm
                        pend_ymm[0]()
                        pend_ymm[0] = this_ymm
                    else:
                        pend_ymm[0] = emit_ymm
                pend_ymm[0]()
                ybk.read("v", lambda e, dt=dt, ybk=ybk: e.scalar_tensor_tensor(
                    dan, xat, lcol[:, 288 + dt:289 + dt], ybk.ap[:],
                    ALU.mult, ALU.add),
                    extra_waits=(p.mark("v"), p.mark("s")))
                yem = p.mark("v")
                p.op("g", lambda e, dt=dt, tsl=tsl: e.dma_start(
                    y_scr[dt, :, tsl], dan),
                    waits=(yem,), incs=("dms5",), dma=True)
        p.op("v", lambda e: e.drain(), waits=[("dms5", p.semv["dms5"])],
             incs=("fin_v",))

        p.tag = f"s6c{L}"
        ygv = bigB[:].rearrange("p a t -> p (a t)")[:, 0:8192] \
            .rearrange("p (a t) -> p a t", a=16)
        zb = [aux[:, 0:512], aux[:, 512:1024]]
        yb = [aux[:, 1024:1536], aux[:, 1536:2048]]
        t0c, v0c = p.mark("t"), p.mark("v")
        lastz = {}
        gtm = {}
        lastw = {}
        prevwout = t0c
        for c4 in range(4):
            ts = slice(c4 * 512, (c4 + 1) * 512)
            for dt in range(NDI):
                k = c4 * NDI + dt
                slot = k % 2
                zw = max(lastz.get(k - 2, t0c), prevwout, key=lambda m: m[1])
                p.op("g", lambda e, L=L, dt=dt, slot=slot: e.dma_start(
                    wflat[:, slot * 2048:slot * 2048 + 1024]
                    .rearrange("p (a b) -> p a b", a=8),
                    X["L1ab_win"][L, 16 + dt].rearrange("a p o -> p a o")),
                    waits=(zw, v0c),
                    incs=(f"dms{slot}",), dma=True)
                wm = [(f"dms{slot}", p.semv[f"dms{slot}"])]
                bk = bank[c4 % 2]
                bk.mm([lambda e, i=i, ts=ts, bk=bk, slot=slot:
                       e.matmul(bk.ap[:], wflat[:, slot * 2048 + i * 128:slot * 2048 + (i + 1) * 128],
                                xloc[:, i, ts], start=(i == 0), stop=(i == NDT - 1))
                       for i in range(NDT)], extra_waits=tuple(wm))
                lastz[k] = p.mark("t")
                bk.read("s", lambda e, bk=bk, slot=slot: e.activation(
                    zb[slot], bk.ap[:], AF.Silu, bias=0.0, scale=1.0),
                        extra_waits=(gtm.get(k - 2, v0c),))
                zm = p.mark("s")
                sq = f"dms{6 + slot}"
                p.op("g", lambda e, dt=dt, ts=ts, slot=slot: e.dma_start(
                    yb[slot], y_scr[dt, :, ts]),
                    waits=(gtm.get(k - 2, v0c),), incs=(sq,), dma=True)
                ym = [(sq, p.semv[sq])]
                p.op("v", lambda e, dt=dt, slot=slot: e.tensor_tensor(
                    ygv[:, dt, :], yb[slot], zb[slot], ALU.mult),
                     waits=tuple(ym) + (zm,), incs=("fin_v",))
                gtm[k] = p.mark("v")
            ygm = p.mark("v")
            for mt in range(NDT):
                slot = mt % 2
                p.op("g", lambda e, L=L, mt=mt, slot=slot: e.dma_start(
                    wflat[:, slot * 2048:slot * 2048 + 2048]
                    .rearrange("p (a b) -> p a b", a=16),
                    X["L1ab_wout"][L, mt].rearrange("a p o -> p a o")),
                    waits=(lastw.get(mt - 2, lastz[4 * NDI - 1] if c4 == 3 else lastz[(c4 + 1) * NDI - 1]),),
                    incs=(f"dms{slot + 2}",), dma=True)
                wm = [(f"dms{slot + 2}", p.semv[f"dms{slot + 2}"])]
                bk = bank[2]
                bk.mm([lambda e, dt=dt2, mt=mt, slot=slot, bk=bk:
                       e.matmul(bk.ap[:], wflat[:, slot * 2048 + dt * 128:slot * 2048 + (dt + 1) * 128],
                                ygv[:, dt, :], start=(dt == 0), stop=(dt == NDI - 1))
                       for dt2 in range(NDI)],
                      extra_waits=tuple(wm) + (ygm,))
                lastw[mt] = p.mark("t")
                bk.read("v", lambda e, mt=mt, ts=ts, bk=bk: e.tensor_add(
                    xres[:, mt, ts], xres[:, mt, ts], bk.ap[:]),
                    extra_waits=(p.mark("v"),))
                p.op("v", lambda e, mt=mt, ts=ts: e.tensor_add(
                    xloc[:, mt, ts], xloc[:, mt, ts], bank[2].ap[:]),
                     waits=(p.mark("v"),), incs=("fin_v",))
                bank[2].rmarks.append(("fin_v", p.semv["fin_v"]))
            lastw = {}
            prevwout = p.mark("t")
        p.op("v", lambda e: e.drain(), incs=("fin_v",))
    return p.mark("v")



def prep_inputs(inputs, core):
    d = dict(host_constants())
    x = np.asarray(inputs["L0_attn_a_x"])[core]
    d["x_f"] = np.ascontiguousarray(x.T)
    d["L1_attn_a_WqT"] = np.ascontiguousarray(np.asarray(inputs["L1_attn_a_Wq"]).T)
    d["L1_attn_a_WkT"] = np.ascontiguousarray(np.asarray(inputs["L1_attn_a_Wk"]).T)
    d["L1_attn_a_WvT"] = np.ascontiguousarray(np.asarray(inputs["L1_attn_a_Wv"]).T)
    d["L0_ffn_w_gate_T"] = np.ascontiguousarray(np.asarray(inputs["L0_ffn_w_gate"]).T)
    d["L0_ffn_w_up_T"] = np.ascontiguousarray(np.asarray(inputs["L0_ffn_w_up"]).T)
    d["L0_ffn_w_down_T"] = np.ascontiguousarray(np.asarray(inputs["L0_ffn_w_down"]).T)
    perm = np.zeros(1024, dtype=np.int64)
    for par in range(2):
        for h in range(16):
            for i in range(32):
                op = (par * 4 + h // 4) * 128 + (h % 4) * 32 + i
                perm[op] = h * 64 + 2 * i + par
    d["L0_attn_b_WqT_perm"] = np.ascontiguousarray(
        np.asarray(inputs["L0_attn_b_Wq"]).T[:, perm])
    d["L0_attn_b_WkT_perm"] = np.ascontiguousarray(
        np.asarray(inputs["L0_attn_b_Wk"]).T[:, perm])
    d["L0_attn_b_WvT"] = np.ascontiguousarray(np.asarray(inputs["L0_attn_b_Wv"]).T)
    d["L0_attn_b_WoT"] = np.ascontiguousarray(np.asarray(inputs["L0_attn_b_Wo"]).T)
    WXP = np.asarray(inputs["L0_attn_a_x_proj_w"])
    pad = np.zeros((256, 1024), dtype=np.float32)
    pad[0:64] = WXP[64:128]
    pad[64:128] = WXP[0:64]
    pad[128:192] = WXP[128:192]
    d["L0aa_wxpT"] = np.ascontiguousarray(pad.T)
    d["L0aa_wdtT"] = np.ascontiguousarray(np.asarray(inputs["L0_attn_a_dt_proj_w"]).T)
    WOo = np.asarray(inputs["L0_attn_a_out_proj_w"])
    d["L0aa_wo"] = np.ascontiguousarray(WOo.T.reshape(8, 128, 8, 128)
                                        .transpose(2, 0, 1, 3))
    AL0 = np.asarray(inputs["L0_attn_a_A_log"])
    app = np.zeros((128, 512), dtype=np.float32)
    for kk in range(512):
        dtile, m = kk // 64, kk % 64
        for pp in range(128):
            dd = dtile * 128 + 2 * m + pp // 64
            nn2 = pp % 64
            app[pp, kk] = AL0[dd, nn2]
    d["L0aa_A_pp"] = app
    cols0 = np.zeros((128, 24), dtype=np.float32)
    cols0[:, 0:8] = np.asarray(inputs["L0_attn_a_norm_w"]).reshape(8, 128).T
    cols0[:, 8:16] = np.asarray(inputs["L0_attn_a_dt_proj_b"]).reshape(8, 128).T
    cols0[:, 16:24] = np.asarray(inputs["L0_attn_a_D"]).reshape(8, 128).T
    d["L0aa_cols"] = cols0
    seld = np.zeros((128, 64 * 128), dtype=np.float32)
    for m in range(64):
        for pp in range(128):
            seld[2 * m + pp // 64, m * 128 + pp] = 1.0
    d["c_seld"] = seld
    seln0 = np.zeros((128, 128), dtype=np.float32)
    for pp in range(128):
        seln0[pp % 64, pp] = 1.0
    d["c_seln0"] = seln0
    WI = np.asarray(inputs["L1_attn_b_W_in"])
    d["L1ab_win"] = np.ascontiguousarray(
        WI.transpose(0, 2, 1).reshape(5, 8, 128, 32, 128).transpose(0, 3, 1, 2, 4))
    WO = np.asarray(inputs["L1_attn_b_W_out"])
    d["L1ab_wout"] = np.ascontiguousarray(
        WO.transpose(0, 2, 1).reshape(5, 16, 128, 8, 128).transpose(0, 3, 1, 2, 4))
    WX = np.asarray(inputs["L1_attn_b_W_x"])
    wxp = np.zeros((5, 128, 2048), dtype=np.float32)
    wxp[:, 0:16] = WX[:, 64:80]
    wxp[:, 32:48] = WX[:, 80:96]
    wxp[:, 64:128] = WX[:, 0:64]
    d["L1ab_wx"] = np.ascontiguousarray(
        wxp.transpose(0, 2, 1).reshape(5, 16, 128, 128))
    d["L1ab_wdtT"] = np.ascontiguousarray(
        np.asarray(inputs["L1_attn_b_W_dt"]).transpose(0, 2, 1))
    AL = np.asarray(inputs["L1_attn_b_A_log"])
    d["L1ab_A_pp"] = np.ascontiguousarray(AL.reshape(5, 16, 128, 16)
                                          .transpose(0, 2, 1, 3).reshape(5, 128, 256))
    cols = np.zeros((5, 128, 112), dtype=np.float32)
    cols[:, :, 0:16] = np.asarray(inputs["L1_attn_b_b_dt"]).reshape(5, 16, 128).transpose(0, 2, 1)
    cols[:, :, 16:32] = np.asarray(inputs["L1_attn_b_conv_b"]).reshape(5, 16, 128).transpose(0, 2, 1)
    cols[:, :, 32:48] = np.asarray(inputs["L1_attn_b_D"]).reshape(5, 16, 128).transpose(0, 2, 1)
    CW = np.asarray(inputs["L1_attn_b_conv_w"]).reshape(5, 16, 128, 4)
    cols[:, :, 48:112] = CW.transpose(0, 2, 1, 3).reshape(5, 128, 64)
    d["L1ab_cols"] = np.ascontiguousarray(cols)
    d["L1_ffn_w_gate"] = np.asarray(inputs["L1_ffn_w_gate"])
    d["L1_ffn_w_up"] = np.asarray(inputs["L1_ffn_w_up"])
    d["L1_ffn_w_down"] = np.asarray(inputs["L1_ffn_w_down"])
    d["L1_attn_a_b_cat"] = np.concatenate([
        np.asarray(inputs["L1_attn_a_bq"]), np.asarray(inputs["L1_attn_a_bk"]),
        np.asarray(inputs["L1_attn_a_bv"])]).reshape(1, 3 * D).astype(np.float32)
    return d


SUBLAYERS = tuple(range(8))


def kernel(**inputs):
    per_core = [prep_inputs(inputs, c) for c in range(8)]
    nc = build_nc(per_core[0], SUBLAYERS)
    res = run_bass_kernel_spmd(nc, per_core, core_ids=list(range(8)))
    out = np.stack([r["out"].T for r in res.results], axis=0)
    return out.astype(np.float32)
